# version 02
# baseline (speedup 1.0000x reference)
import numpy as np
import concourse.bass as bass
import concourse.mybir as mybir
from concourse.ap import AP

F32 = mybir.dt.float32
BF16 = mybir.dt.bfloat16
I32 = mybir.dt.int32
AF = mybir.ActivationFunctionType
ALU = mybir.AluOpType
AX = mybir.AxisListType.X

NFULL, NOWN, D = 4096, 2048, 1024
EPS = 1e-5
WPAD = 4224


class KB:
    def __init__(self, nc):
        self.nc = nc
        self.st = {k: [] for k in ("t", "s", "v", "y", "g")}
        self.sem = {}
        self.cnt = {}
        for name in ("sq", "sg", "sT", "sS", "sV"):
            self.sem[name] = nc.semaphore(name).__enter__()
            self.cnt[name] = 0

    def _emit(self, stream, fn, sem=None, k=1):
        if sem is not None:
            self.cnt[sem] += k
            h = self.sem[sem]
            def f(e, fn=fn, h=h, k=k):
                fn(e).then_inc(h, k)
        else:
            def f(e, fn=fn):
                fn(e)
        self.st[stream].append(f)
        return self.cnt[sem] if sem else None

    def t(self, fn, inc=False):  return self._emit("t", fn, "sT" if inc else None)
    def vW(self, fn):
        tk = self._emit("v", fn, "sV")
        self.wait("v", "sV", tk)
        return tk
    def sW(self, fn):
        tk = self._emit("s", fn, "sS")
        self.wait("s", "sS", tk)
        return tk
    def s(self, fn, inc=False):  return self._emit("s", fn, "sS" if inc else None)
    def v(self, fn, inc=False):  return self._emit("v", fn, "sV" if inc else None)
    def ydma(self, out, in_):
        return self._emit("y", lambda e, o=out, i=in_: e.dma_start(out=o, in_=i), "sq", 16)
    def gdma(self, out, in_):
        return self._emit("g", lambda e, o=out, i=in_: e.dma_start(out=o, in_=i), "sg", 16)
    def wait(self, stream, sem, val):
        if not val:
            return
        h = self.sem[sem]
        self.st[stream].append(lambda e, h=h, v=val: e.wait_ge(h, v))
    def barrier(self):
        snap = dict(self.cnt)
        for s in ("t", "s", "v", "y", "g"):
            for sem in ("sq", "sg", "sT", "sS", "sV"):
                self.wait(s, sem, snap[sem])
    def run(self, block):
        for name, key in [("tensor", "t"), ("scalar", "s"), ("vector", "v"),
                          ("sync", "y"), ("gpsimd", "g")]:
            lst = self.st[key]
            def mk(lst):
                def body(eng):
                    for f in lst:
                        f(eng)
                return body
            getattr(block, name)(mk(lst))


def build_l0(debug=()):
    nc = bass.Bass("TRN2", target_bir_lowering=False)
    I = {}
    for nm, shp in [("x", [NFULL, D]), ("xown", [NOWN, D]), ("rbig", [128, 8192]),
                    ("wqT", [D, D]), ("wkT", [D, D]), ("wvT", [D, D]), ("woT", [D, D]),
                    ("bq", [128, 8]), ("bk", [128, 8]), ("bv", [1, D]), ("bo", [1, D]),
                    ("wgT", [D, 4096]), ("wuT", [D, 4096]), ("wdT", [4096, D]),
                    ("pwT", [D, 8]), ("pb", [8, 1]), ("ident", [128, 128])]:
        I[nm] = nc.dram_tensor(nm, shp, F32, kind="ExternalInput").ap()
    x1_out = nc.dram_tensor("x1", [NOWN, D], F32, kind="ExternalOutput").ap()
    zT_out = nc.dram_tensor("zT", [8, NOWN], F32, kind="ExternalOutput").ap()
    dbg = {}
    for nm, shp in [("d_xnfm", [128, 8 * NFULL]), ("d_xnown", [128, 8 * NOWN]),
                    ("d_gtm", [128, 16 * D]), ("d_kfm", [128, 8 * NFULL]),
                    ("d_attn", [128, 16 * D]), ("d_xint", [128, 8 * NOWN]),
                    ("d_vtm", [128, 32 * D]), ("d_qfm", [128, 8 * NOWN]),
                    ("d_dnm", [128, 32]), ("d_sx", [128, 48]),
                    ("d_epad", [128, 2 * WPAD]), ("d_ab", [128, 2 * NFULL]),
                    ("d_gs", [128, 16 * 4096]), ("d_gi", [128, 16 * 4096])]:
        if nm in debug:
            dbg[nm] = nc.dram_tensor(nm, shp, F32, kind="ExternalOutput").ap()

    k_sp = nc.dram_tensor("k_sp", [128, 8 * NFULL], BF16, kind="Internal").ap()
    v_sp = nc.dram_tensor("v_sp", [128, 32 * D], BF16, kind="Internal").ap()
    q_sp = nc.dram_tensor("q_sp", [128, 8 * NOWN], BF16, kind="Internal").ap()
    gs_sp = nc.dram_tensor("gs_sp", [128, 16 * 4096], BF16, kind="Internal").ap()
    gi_sp = nc.dram_tensor("gi_sp", [128, 16 * 4096], BF16, kind="Internal").ap()

    Al = lambda *a: nc.alloc_sbuf_tensor(*a)
    XB = Al("XB", [128, 49152], BF16)
    S16 = Al("S16", [128, 8192], BF16)
    S32 = Al("S32", [128, 16384], BF16)
    g_tm = Al("g_tm", [128, 16 * D], BF16)
    MISC = Al("MISC", [128, 5120], F32)
    idn = Al("idn", [128, 128], BF16)
    bq8 = Al("bq8", [128, 8], F32)
    bk_sb = Al("bk_sb", [128, 8], F32)

    pw_sb = Al("pw_sb", [128, 64], BF16)
    pb_sb = Al("pb_sb", [8, 1], F32)
    onesb = Al("onesb", [1, 128], BF16)
    onef = Al("onef", [128, 1], F32)
    sxraw = Al("sxraw", [128, 16], F32)
    sxg = Al("sxg", [128, 16], F32)
    sxu = Al("sxu", [128, 16], F32)
    sgd = Al("sgd", [128, 16], F32)
    swm = Al("swm", [128, 8], F32)
    stat = Al("stat", [128, 8], F32)
    sml = Al("sml", [1, 8], F32)
    dnm = Al("dnm", [128, 32], F32)
    dnr = Al("dnr", [128, 32], F32)
    zst = Al("zst", [8, 128], F32)
    fmb = Al("fmb", [128, 1024], BF16)

    xn_fm = XB[:, 0:32768]
    xn_own = XB[:, 32768:49152]
    wint = XB[:, 0:32768]
    gsil_t = XB[:, 32768:36864]
    up_t = XB[:, 36864:40960]
    guint_tm = XB[:, 40960:45056]
    guint_fm = XB[:, 45056:49152]
    epad = XB[:, 0:2 * WPAD]
    esh = XB[:, 2 * WPAD:2 * WPAD + 2 * NFULL]
    a0 = XB[:, 16640:16640 + 2 * NFULL]
    ab = XB[:, 24832:24832 + 2 * NFULL]
    atb = XB[:, 33024:33024 + 32 * 128]
    bv_bc = XB[:, 37120:38144]
    bo_bc = XB[:, 37120:38144]
    rb = XB[:, 38144:46336]
    wsb = S16[:, 0:8192]
    kpair = S16[:, 0:4096]
    vpair = S16[:, 4096:8192]
    xint_fm = S32[:, 0:16384]
    attn_tm = S32[:, 0:16384]
    xtile = MISC[:, 0:2048]
    tmpf = MISC[:, 2048:4096]
    tmpi = MISC[:, 4096:5120].bitcast(I32)
    ksg = MISC[:, 0:1536].bitcast(BF16)
    wraw = MISC[:, 0:4096]
    tmpi_d = MISC[:, 4096:5120].bitcast(I32)
    qpair = MISC[:, 0:1024].bitcast(BF16)
    srow = MISC[:, 2048:3072]
    xcF = MISC[:, 3072:4096]
    sqF = MISC[:, 4096:5120]

    with (
        nc.psum_tensor("PA", [128, 1536], F32) as PA,
        nc.psum_tensor("PB", [128, 1536], F32) as PB,
        nc.psum_tensor("PT0", [128, 1024], BF16) as PT0,
        nc.psum_tensor("PT1", [128, 1024], BF16) as PT1,
        nc.Block() as block,
    ):
        kb = KB(nc)
        W = kb.wait

        kb.v(lambda e: e.memset(onesb[:], 1.0))
        kb.v(lambda e: e.memset(onef[:], 1.0))
        kb.ydma(bq8[:], I["bq"])
        kb.ydma(bk_sb[:], I["bk"])
        kb.ydma(pb_sb[:], I["pb"])
        kb.gdma(rb[:], I["rbig"])
        kb.gdma(idn[:], I["ident"])
        kb.gdma(pw_sb[:].rearrange("p (c e) -> p c e", c=8),
                I["pwT"].rearrange("(c p) e -> p c e", p=128))
        kb.barrier()
        kb.v(lambda e: e.tensor_scalar_mul(bq8[:], bq8[:], 0.125), inc=True)
        kb.barrier()

        xfull_t = I["x"].rearrange("(a p) d -> a p d", p=128)
        xown_t = I["xown"].rearrange("(a p) d -> a p d", p=128)
        pt_guard = [kb.cnt["sV"], kb.cnt["sV"]]
        xload_guard = [0, 0]
        s_end = [kb.cnt["sS"]]

        def transp8(src_tm_bf16, dst_cview, pti):
            pt = PT0 if pti % 2 == 0 else PT1
            W("t", "sV", pt_guard[pti % 2])
            ttr = None
            for cb in range(8):
                ttr = kb.t(lambda e, cb=cb, pt=pt, s=src_tm_bf16: e.transpose(
                    pt[:, cb * 128:cb * 128 + 128], s[:, cb * 128:cb * 128 + 128], idn[:]), inc=True)
            W("v", "sT", ttr)
            tev = kb.v(lambda e, pt=pt, d=dst_cview: e.tensor_copy(
                d, pt[:].rearrange("p (c n) -> p c n", c=8)), inc=True)
            pt_guard[pti % 2] = tev
            return tev

        pti_c = [0]
        def ln_tile(ti, src, fm_dst_col, own_t):
            buf = ti % 2
            xt = xtile[:, buf * D:(buf + 1) * D]
            W("y", "sV", xload_guard[buf])
            tld = kb.ydma(xt, src)
            W("v", "sq", tld)
            W("v", "sS", s_end[0])
            kb.vW(lambda e, xt=xt: e.reduce_sum(stat[:, 0:1], xt, AX))
            kb.vW(lambda e: e.tensor_scalar_mul(stat[:, 0:1], stat[:, 0:1], 1.0 / D))
            tsub = kb.v(lambda e, xt=xt: e.tensor_scalar(tmpf[:, 0:D], xt, stat[:, 0:1], None, ALU.subtract), inc=True)
            xload_guard[buf] = tsub
            W("s", "sV", tsub)
            tsq = kb.s(lambda e: e.activation(tmpf[:, D:2 * D], tmpf[:, 0:D], AF.Square), inc=True)
            W("v", "sS", tsq)
            kb.vW(lambda e: e.reduce_sum(stat[:, 1:2], tmpf[:, D:2 * D], AX))
            tvar = kb.v(lambda e: e.tensor_scalar(stat[:, 1:2], stat[:, 1:2], 1.0 / D, EPS, ALU.mult, ALU.add), inc=True)
            W("s", "sV", tvar)
            kb.sW(lambda e: e.activation(stat[:, 2:3], stat[:, 1:2], AF.Ln))
            trs = kb.s(lambda e: e.activation(stat[:, 3:4], stat[:, 2:3], AF.Exp, scale=-0.5), inc=True)
            s_end[0] = trs
            W("v", "sS", trs)
            kb.v(lambda e: e.tensor_scalar(tmpf[:, 0:D], tmpf[:, 0:D], stat[:, 3:4], None, ALU.mult))
            tbf = kb.v(lambda e: e.tensor_copy(fmb[:], tmpf[:, 0:D]), inc=True)
            W("t", "sV", tbf)
            if fm_dst_col is not None:
                dst = xn_fm.rearrange("p (c n) -> p c n", c=8)[:, :, fm_dst_col:fm_dst_col + 128]
            else:
                dst = xn_own.rearrange("p (c n) -> p c n", c=8)[:, :, own_t * 128:own_t * 128 + 128]
            transp8(fmb, dst, pti_c[0]); pti_c[0] += 1
            if own_t is not None:
                t = own_t
                kb.vW(lambda e: e.tensor_reduce(stat[:, 4:5], tmpf[:, 0:D], AX, ALU.max, apply_absolute_value=True))
                kb.vW(lambda e: e.tensor_scalar_max(stat[:, 4:5], stat[:, 4:5], 1e-5))
                kb.vW(lambda e, t=t: e.tensor_scalar_mul(sxraw[:, t:t + 1], stat[:, 4:5], 1.0 / 127.0))
                kb.vW(lambda e: e.reciprocal(stat[:, 5:6], stat[:, 4:5]))
                kb.vW(lambda e: e.tensor_scalar_mul(stat[:, 5:6], stat[:, 5:6], 127.0))
                kb.v(lambda e: e.tensor_scalar(tmpf[:, D:2 * D], tmpf[:, 0:D], stat[:, 5:6], None, ALU.mult))
                kb.v(lambda e: e.tensor_copy(tmpi[:], tmpf[:, D:2 * D]))
                tbi = kb.v(lambda e: e.tensor_copy(fmb[:], tmpi[:]), inc=True)
                W("t", "sV", tbi)
                dsti = xint_fm.rearrange("p (c n) -> p c n", c=8)[:, :, t * 128:t * 128 + 128]
                transp8(fmb, dsti, pti_c[0]); pti_c[0] += 1

        for ti in range(32):
            ln_tile(ti, xfull_t[ti], ti * 128, None)
        for t in range(16):
            ln_tile(32 + t, xown_t[t], None, t)
        kb.barrier()
        for nm, src in [("d_xnfm", xn_fm), ("d_xnown", xn_own), ("d_xint", xint_fm)]:
            if nm in dbg:
                kb.gdma(dbg[nm], src)
        kb.barrier()

        ev_guard = {"A": None, "B": None}
        stg_guard = [0, 0]
        gidx = [0]
        wsb_guard = [0]

        def mm_group(ps, emit_mms, evac_fn, spill_dst, eng="v"):
            pkey = "A" if ps is PA else "B"
            sbf = gidx[0] % 2
            if ev_guard[pkey]:
                W("t", ev_guard[pkey][0], ev_guard[pkey][1])
            tmm = emit_mms()
            W(eng, "sT", tmm)
            W(eng, "sq", stg_guard[sbf])
            emit = kb.v if eng == "v" else kb.s
            sem = "sV" if eng == "v" else "sS"
            tse = emit(lambda e, sbf=sbf: evac_fn(e, ksg[:, sbf * 1536:sbf * 1536 + 1536]), inc=True)
            ev_guard[pkey] = (sem, tse)
            W("y", sem, tse)
            tst = kb.ydma(spill_dst, ksg[:, sbf * 1536:sbf * 1536 + spill_dst.shape[-1]])
            stg_guard[sbf] = tst
            gidx[0] += 1

        def proj_fm(wname, src_fm, ncols, spill, bias_sb, scale):
            W("g", "sT", wsb_guard[0])
            tw = kb.gdma(wsb[:].rearrange("p (c f) -> p c f", c=8),
                         I[wname].rearrange("(c p) f -> p c f", p=128))
            W("t", "sg", tw)
            groups = [(0, 1536), (1536, 1536), (3072, 1024)] if ncols == 4096 else [(0, 1536), (1536, 512)]
            for fc in range(8):
                for (nb, nw) in groups:
                    ps = PA if gidx[0] % 2 == 0 else PB
                    def mms(fc=fc, nb=nb, nw=nw, ps=ps, src=src_fm, ncols=ncols):
                        tmm = None
                        for ec in range(8):
                            for sb in range(nw // 512):
                                tmm = kb.t(lambda e, fc=fc, ec=ec, nb=nb, sb=sb, ps=ps, src=src, ncols=ncols: e.matmul(
                                    ps[:, sb * 512:sb * 512 + 512],
                                    wsb[:, ec * D + fc * 128:ec * D + fc * 128 + 128],
                                    src[:, ec * ncols + nb + sb * 512:ec * ncols + nb + sb * 512 + 512],
                                    start=(ec == 0), stop=(ec == 7)), inc=(ec == 7 and sb == nw // 512 - 1))
                        return tmm
                    def ev(e, dst, fc=fc, nw=nw, ps=ps, bias_sb=bias_sb, scale=scale):
                        return e.tensor_scalar(dst[:, 0:nw], ps[:, 0:nw], scale,
                                               bias_sb[:, fc:fc + 1], ALU.mult, ALU.add)
                    mm_group(ps, mms, ev, spill.rearrange("p (c n) -> p c n", c=8)[:, fc, nb:nb + nw])
            wsb_guard[0] = kb.cnt["sT"]

        proj_fm("wkT", xn_fm, NFULL, k_sp, bk_sb, 1.0)
        kb.barrier()
        proj_fm("wqT", xn_own, NOWN, q_sp, bq8, 0.125)
        kb.barrier()
        W("g", "sT", wsb_guard[0])
        tw = kb.gdma(wsb[:].rearrange("p (c f) -> p c f", c=8),
                     I["wvT"].rearrange("(c p) f -> p c f", p=128))
        W("t", "sg", tw)
        for nt in list(range(16, 32)) + list(range(16)):
            ps = PA if gidx[0] % 2 == 0 else PB
            def mms(nt=nt, ps=ps):
                tmm = None
                for ec in range(8):
                    for sb in range(2):
                        tmm = kb.t(lambda e, nt=nt, ec=ec, sb=sb, ps=ps: e.matmul(
                            ps[:, sb * 512:sb * 512 + 512],
                            xn_fm[:, ec * NFULL + nt * 128:ec * NFULL + nt * 128 + 128],
                            wsb[:, ec * D + sb * 512:ec * D + sb * 512 + 512],
                            start=(ec == 0), stop=(ec == 7)), inc=(ec == 7 and sb == 1))
                return tmm
            def ev(e, dst, ps=ps):
                return e.activation(dst[:, 0:1024], ps[:, 0:1024], AF.Copy)
            mm_group(ps, mms, ev, v_sp[:, nt * D:nt * D + D], eng="s")
        kb.barrier()
        for nm, src in [("d_kfm", k_sp), ("d_vtm", v_sp), ("d_qfm", q_sp)]:
            if nm in dbg:
                kb.gdma(dbg[nm], src)
        kb.barrier()

        onefr = Al("onefr", [1, 128], F32)
        idf = Al("idf", [4, 4], F32)
        m3 = Al("m3", [4, 4], F32)
        smr = Al("smr", [1, 8], F32)
        kb.v(lambda e: e.memset(onefr[:], 1.0))
        kb.ydma(idf[:], I["ident"][0:4, 0:4])
        wsrc = {"g": I["wgT"], "u": I["wuT"], "d": I["wdT"]}
        nch = {"g": 8, "u": 8, "d": 32}
        ccol = {"g": 4096, "u": 4096, "d": 1024}
        vprev = [kb.cnt["sV"]]
        for mi, m in enumerate(("g", "u", "d")):
            for ch in range(nch[m]):
                W("y", "sV", vprev[0])
                tld = kb.ydma(wraw[:, 0:ccol[m]],
                              wsrc[m].rearrange("(c p) f -> p c f", p=128)[:, ch, :])
                W("v", "sq", tld)
                if ch == 0:
                    tv = kb.vW(lambda e, mi=mi, cc=ccol[m]: e.tensor_reduce(
                        dnm[:, mi:mi + 1], wraw[:, 0:cc], AX, ALU.add, apply_absolute_value=True))
                else:
                    kb.vW(lambda e, cc=ccol[m]: e.tensor_reduce(
                        stat[:, 7:8], wraw[:, 0:cc], AX, ALU.add, apply_absolute_value=True))
                    tv = kb.vW(lambda e, mi=mi: e.tensor_add(dnm[:, mi:mi + 1], dnm[:, mi:mi + 1], stat[:, 7:8]))
                vprev[0] = tv
        kb.barrier()
        kb.t(lambda e: e.matmul(PA[0:3, 0:1], dnm[:, 0:3], onef[:], start=True, stop=True), inc=True)
        kb.barrier()
        kb.vW(lambda e: e.tensor_copy(m3[0:3, 0:1], PA[0:3, 0:1]))
        kb.vW(lambda e: e.tensor_scalar_mul(m3[0:3, 0:1], m3[0:3, 0:1], 1.0 / (1024.0 * 4096.0)))
        kb.vW(lambda e: e.tensor_scalar_max(m3[0:3, 0:1], m3[0:3, 0:1], 1e-5))
        kb.v(lambda e: e.reciprocal(m3[0:3, 1:2], m3[0:3, 0:1]), inc=True)
        kb.barrier()
        kb.t(lambda e: e.matmul(PB[0:1, 0:3], m3[0:3, 0:1], idf[0:3, 0:3], start=True, stop=True))
        kb.t(lambda e: e.matmul(PB[0:1, 4:7], m3[0:3, 1:2], idf[0:3, 0:3], start=True, stop=True), inc=True)
        kb.barrier()
        kb.v(lambda e: e.tensor_copy(smr[0:1, 0:3], PB[0:1, 0:3]))
        kb.v(lambda e: e.tensor_copy(smr[0:1, 3:6], PB[0:1, 4:7]), inc=True)
        kb.barrier()
        kb.t(lambda e: e.matmul(PA[:, 0:3], onefr[:], smr[0:1, 0:3], start=True, stop=True))
        kb.t(lambda e: e.matmul(PA[:, 4:7], onefr[:], smr[0:1, 3:6], start=True, stop=True), inc=True)
        kb.barrier()
        kb.v(lambda e: e.tensor_copy(swm[:, 0:3], PA[:, 0:3]))
        kb.v(lambda e: e.tensor_copy(swm[:, 3:6], PA[:, 4:7]))
        kb.v(lambda e: e.tensor_scalar(sxg[:], sxraw[:], swm[:, 0:1], None, ALU.mult))
        kb.v(lambda e: e.tensor_scalar(sxu[:], sxraw[:], swm[:, 1:2], None, ALU.mult), inc=True)
        kb.barrier()

        def quantize_resident(m, mi):
            vq = [kb.cnt["sV"]]
            for ch in range(nch[m]):
                cc = ccol[m]
                W("y", "sV", vq[0])
                tld = kb.ydma(wraw[:, 0:cc],
                              wsrc[m].rearrange("(c p) f -> p c f", p=128)[:, ch, :])
                W("v", "sq", tld)
                for qq in range(0, cc, 1024):
                    kb.v(lambda e, qq=qq, mi=mi: e.tensor_scalar(
                        tmpi_d[:, 0:1024], wraw[:, qq:qq + 1024], swm[:, 3 + mi:4 + mi], None, ALU.mult))
                    vq[0] = kb.v(lambda e, qq=qq, cc=cc, ch=ch: e.tensor_scalar(
                        wint[:, ch * cc + qq:ch * cc + qq + 1024], tmpi_d[:, 0:1024], 1, -1, ALU.min, ALU.max), inc=True)
            kb.barrier()

        quantize_resident("g", 0)
        evg = {"A": 0, "B": 0}
        sg_store = [0]
        for t in range(16):
            for gi, (ob, ow) in enumerate([(0, 1536), (1536, 1536), (3072, 1024)]):
                ps, pk = (PA, "A") if (t * 3 + gi) % 2 == 0 else (PB, "B")
                W("t", "sS", evg[pk])
                tmm = None
                for ec in range(8):
                    for sb in range(ow // 512):
                        tmm = kb.t(lambda e, t=t, ec=ec, ob=ob, sb=sb, ps=ps: e.matmul(
                            ps[:, sb * 512:sb * 512 + 512],
                            xint_fm[:, ec * NOWN + t * 128:ec * NOWN + t * 128 + 128],
                            wint[:, ec * 4096 + ob + sb * 512:ec * 4096 + ob + sb * 512 + 512],
                            start=(ec == 0), stop=(ec == 7)), inc=(ec == 7 and sb == ow // 512 - 1))
                W("s", "sT", tmm)
                W("s", "sg", sg_store[0])
                tse = kb.s(lambda e, t=t, ob=ob, ow=ow, ps=ps: e.activation(
                    gsil_t[:, ob:ob + ow], ps[:, 0:ow], AF.Silu, scale=sxg[:, t:t + 1]), inc=True)
                evg[pk] = tse
            W("g", "sS", evg["A"])
            W("g", "sS", evg["B"])
            tst = kb.gdma(gs_sp[:, t * 4096:t * 4096 + 4096], gsil_t[:])
            sg_store[0] = tst
        kb.barrier()
        kb.barrier()
        quantize_resident("u", 1)
        evg = {"A": 0, "B": 0}
        gu_guard = [0]
        ptd_guard = [kb.cnt["sV"], kb.cnt["sV"]]
        for t in range(16):
            W("g", "sV", gu_guard[0])
            tldg = kb.gdma(gsil_t[:], gs_sp[:, t * 4096:t * 4096 + 4096])
            for gi, (ob, ow) in enumerate([(0, 1536), (1536, 1536), (3072, 1024)]):
                ps, pk = (PA, "A") if (t * 3 + gi) % 2 == 0 else (PB, "B")
                W("t", "sV", evg[pk])
                tmm = None
                for ec in range(8):
                    for sb in range(ow // 512):
                        tmm = kb.t(lambda e, t=t, ec=ec, ob=ob, sb=sb, ps=ps: e.matmul(
                            ps[:, sb * 512:sb * 512 + 512],
                            xint_fm[:, ec * NOWN + t * 128:ec * NOWN + t * 128 + 128],
                            wint[:, ec * 4096 + ob + sb * 512:ec * 4096 + ob + sb * 512 + 512],
                            start=(ec == 0), stop=(ec == 7)), inc=(ec == 7 and sb == ow // 512 - 1))
                W("v", "sT", tmm)
                tse = kb.v(lambda e, t=t, ob=ob, ow=ow, ps=ps: e.tensor_scalar(
                    up_t[:, ob:ob + ow], ps[:, 0:ow], sxu[:, t:t + 1], None, ALU.mult), inc=True)
                evg[pk] = tse
            W("v", "sV", max(evg["A"], evg["B"])) if False else None
            W("v", "sg", tldg)
            kb.v(lambda e: e.tensor_mul(up_t[:], gsil_t[:], up_t[:]))
            kb.vW(lambda e: e.tensor_reduce(stat[:, 0:1], up_t[:], AX, ALU.max, apply_absolute_value=True))
            kb.vW(lambda e: e.tensor_scalar_max(stat[:, 0:1], stat[:, 0:1], 1e-5))
            kb.vW(lambda e, t=t: e.tensor_scalar_mul(stat[:, 1:2], stat[:, 0:1], 1.0 / 127.0))
            kb.vW(lambda e, t=t: e.tensor_scalar(sgd[:, t:t + 1], stat[:, 1:2], swm[:, 2:3], None, ALU.mult))
            kb.vW(lambda e: e.reciprocal(stat[:, 2:3], stat[:, 1:2]))
            tgu = None
            for qq in range(0, 4096, 1024):
                kb.v(lambda e, qq=qq: e.tensor_scalar(tmpi_d[:, 0:1024], up_t[:, qq:qq + 1024], stat[:, 2:3], None, ALU.mult))
                tgu = kb.v(lambda e, qq=qq: e.tensor_copy(guint_tm[:, qq:qq + 1024], tmpi_d[:, 0:1024]), inc=True)
            gu_guard[0] = tgu
            tprev = None
            for grp in range(4):
                pt = PT0 if grp % 2 == 0 else PT1
                W("t", "sV", ptd_guard[grp % 2])
                if grp == 0:
                    W("t", "sV", tgu)
                for cb in range(8):
                    tprev = kb.t(lambda e, grp=grp, cb=cb, pt=pt: e.transpose(
                        pt[:, cb * 128:cb * 128 + 128],
                        guint_tm[:, grp * 1024 + cb * 128:grp * 1024 + cb * 128 + 128], idn[:]), inc=True)
                W("v", "sT", tprev)
                tev = kb.v(lambda e, grp=grp, pt=pt: e.tensor_copy(
                    guint_fm[:, grp * 1024:grp * 1024 + 1024], pt[:]), inc=True)
                ptd_guard[grp % 2] = tev
            W("g", "sV", ptd_guard[0])
            W("g", "sV", ptd_guard[1])
            kb.gdma(gi_sp[:, t * 4096:t * 4096 + 4096], guint_fm[:])
        kb.barrier()
        kb.barrier()
        quantize_resident("d", 2)
        evg = {"A": 0, "B": 0}
        dmm_guard = [0]
        for t in range(16):
            W("g", "sT", dmm_guard[0])
            tldg = kb.gdma(guint_fm[:], gi_sp[:, t * 4096:t * 4096 + 4096])
            W("t", "sg", tldg)
            for gi, (ob, ow) in enumerate([(0, 512), (512, 512)]):
                ps, pk = (PA, "A") if (t * 2 + gi) % 2 == 0 else (PB, "B")
                W("t", "sS", evg[pk])
                tmm = None
                for oc in range(32):
                    tmm = kb.t(lambda e, t=t, oc=oc, ob=ob, ps=ps: e.matmul(
                        ps[:, 0:512],
                        guint_fm[:, oc * 128:oc * 128 + 128],
                        wint[:, oc * 1024 + ob:oc * 1024 + ob + 512],
                        start=(oc == 0), stop=(oc == 31)), inc=(oc == 31))
                dmm_guard[0] = tmm
                W("s", "sT", tmm)
                tse = kb.s(lambda e, t=t, ob=ob, ps=ps: e.activation(
                    g_tm[:, t * D + ob:t * D + ob + 512], ps[:, 0:512], AF.Silu,
                    scale=sgd[:, t:t + 1]), inc=True)
                evg[pk] = tse
        kb.barrier()
        if "d_gtm" in dbg:
            kb.gdma(dbg["d_gtm"], g_tm[:])
        if "d_sx" in dbg:
            kb.gdma(dbg["d_sx"][:, 0:16], sxraw[:])
            kb.gdma(dbg["d_sx"][:, 16:32], sxg[:])
            kb.gdma(dbg["d_sx"][:, 32:40], swm[:])
            kb.gdma(dbg["d_sx"][:, 40:48], sgd[:, 0:8])
        if "d_gs" in dbg:
            kb.gdma(dbg["d_gs"], gs_sp)
        if "d_gi" in dbg:
            kb.gdma(dbg["d_gi"], gi_sp)
        kb.barrier()

        kb.gdma(rb[:], I["rbig"])
        kb.gdma(fmb[0:1, :], I["bv"])
        kb.barrier()
        kb.t(lambda e: e.matmul(PA[:, 0:512], onesb[:], fmb[0:1, 0:512], start=True, stop=True))
        _tbc = kb.t(lambda e: e.matmul(PA[:, 512:1024], onesb[:], fmb[0:1, 512:1024], start=True, stop=True), inc=True)
        W("v", "sT", _tbc)
        kb.v(lambda e: e.tensor_copy(bv_bc[:], PA[:, 0:1024]), inc=True)
        kb.barrier()
        pair_guard = [0]
        ptE = [kb.cnt["sV"], kb.cnt["sV"]]
        psE = {"A": 0, "B": 0}
        epad_guard = [0]
        a0_guard = [0]
        ab_guard = [0]
        atb_guard = [0]
        av_guard = [0]
        SEGS_P = [(0, 1536), (1536, 1536), (3072, 1152)]
        SEGS_C = [(0, 1536), (1536, 1536), (3072, 1024)]
        XBROW = 49152
        for p in range(8):
            W("y", "sT", pair_guard[0])
            tk = kb.ydma(kpair[:], k_sp.rearrange("p (c n) -> p c n", c=8)[:, p, :])
            tv_ = kb.ydma(vpair[:].rearrange("p (k c) -> p k c", k=32),
                          v_sp.rearrange("p (k f) -> p k f", k=32)[:, :, p * 128:p * 128 + 128])
            tq_ = kb.ydma(qpair[:], q_sp.rearrange("p (c n) -> p c n", c=8)[:, p, :])
            for qt in range(16):
                q0 = qt * 128
                g0col = 3968 - q0
                if qt == 0:
                    W("t", "sq", tq_)
                W("t", "sV", av_guard[0])
                for (sb_, sw) in SEGS_P:
                    W("t", "sS", psE["A"])
                    W("t", "sS", psE["B"])
                    nmm = (sw + 511) // 512
                    tmm = None
                    for s_ in range(nmm):
                        w_ = min(512, sw - s_ * 512)
                        for he, ps in ((0, PA), (1, PB)):
                            tmm = kb.t(lambda e, he=he, q0=q0, g0col=g0col, sb_=sb_, s_=s_, w_=w_, ps=ps: e.matmul(
                                ps[:, s_ * 512:s_ * 512 + w_],
                                qpair[64 * he:64 * he + 64, q0:q0 + 128],
                                rb[64 * he:64 * he + 64, g0col + sb_ + s_ * 512:g0col + sb_ + s_ * 512 + w_],
                                start=True, stop=True), inc=(s_ == nmm - 1 and he == 1))
                    for he, ps, pk in ((0, PA, "A"), (1, PB, "B")):
                        W("s", "sT", tmm)
                        W("s", "sg", epad_guard[0])
                        tse = kb.s(lambda e, he=he, sb_=sb_, sw=sw, ps=ps: e.activation(
                            epad[:, he * WPAD + sb_:he * WPAD + sb_ + sw], ps[:, 0:sw], AF.Exp), inc=True)
                        psE[pk] = tse
                W("g", "sS", kb.cnt["sS"])
                W("g", "sV", a0_guard[0])
                tsh = None
                for he in range(2):
                    src = AP(XB, he * WPAD + 127, [[XBROW - 1, 128], [1, NFULL]])
                    tsh = kb.gdma(esh[:, he * NFULL:he * NFULL + NFULL], src)
                epad_guard[0] = tsh
                for (sb_, sw) in SEGS_C:
                    W("t", "sS", psE["A"])
                    W("t", "sS", psE["B"])
                    tmm = None
                    for s_ in range(sw // 512):
                        for he, ps in ((0, PA), (1, PB)):
                            tmm = kb.t(lambda e, he=he, q0=q0, sb_=sb_, s_=s_, ps=ps: e.matmul(
                                ps[:, s_ * 512:s_ * 512 + 512],
                                qpair[64 * he:64 * he + 64, q0:q0 + 128],
                                kpair[64 * he:64 * he + 64, sb_ + s_ * 512:sb_ + s_ * 512 + 512],
                                start=True, stop=True), inc=(s_ == sw // 512 - 1 and he == 1))
                    for he, ps, pk in ((0, PA, "A"), (1, PB, "B")):
                        W("s", "sT", tmm)
                        W("s", "sV", a0_guard[0])
                        tse = kb.s(lambda e, he=he, sb_=sb_, sw=sw, ps=ps: e.activation(
                            a0[:, he * NFULL + sb_:he * NFULL + sb_ + sw], ps[:, 0:sw], AF.Exp), inc=True)
                        psE[pk] = tse
                tCe = kb.cnt["sS"]
                W("v", "sS", tCe)
                W("v", "sg", tsh)
                W("v", "sT", ab_guard[0])
                col = qt
                tvh = None
                for he in range(2):
                    kb.vW(lambda e, he=he, col=col: e.scalar_tensor_tensor(
                        ab[:, he * NFULL:he * NFULL + NFULL], a0[:, he * NFULL:he * NFULL + NFULL],
                        1.0, esh[:, he * NFULL:he * NFULL + NFULL], ALU.mult, ALU.mult,
                        accum_out=dnm[:, he * 16 + col:he * 16 + col + 1]))
                    tvh = kb.vW(lambda e, he=he, col=col: e.reciprocal(
                        dnr[:, he * 16 + col:he * 16 + col + 1], dnm[:, he * 16 + col:he * 16 + col + 1]))
                a0_guard[0] = tvh
                for he in range(2):
                    tprev = None
                    for grp in range(4):
                        pt = PT0 if grp % 2 == 0 else PT1
                        W("t", "sV", ptE[grp % 2])
                        if grp == 0 and he == 0:
                            W("t", "sV", tvh)
                        for cb in range(8):
                            kt = grp * 8 + cb
                            tprev = kb.t(lambda e, he=he, kt=kt, cb=cb, pt=pt: e.transpose(
                                pt[:, cb * 128:cb * 128 + 128],
                                ab[:, he * NFULL + kt * 128:he * NFULL + kt * 128 + 128], idn[:]), inc=True)
                        W("v", "sT", tprev)
                        W("v", "sT", atb_guard[0])
                        tev = kb.v(lambda e, grp=grp, pt=pt: e.tensor_copy(
                            atb[:, grp * 1024:grp * 1024 + 1024], pt[:]), inc=True)
                        ptE[grp % 2] = tev
                    W("t", "sV", ptE[0])
                    W("t", "sV", ptE[1])
                    tav = None
                    for kt in range(32):
                        tav = kb.t(lambda e, he=he, kt=kt: e.matmul(
                            PA[:, 1024 + he * 64:1024 + he * 64 + 64],
                            atb[:, kt * 128:kt * 128 + 128],
                            vpair[:, kt * 128 + he * 64:kt * 128 + he * 64 + 64],
                            start=(kt == 0), stop=(kt == 31)), inc=(kt == 31))
                    atb_guard[0] = tav
                    W("v", "sT", tav)
                    kb.vW(lambda e, he=he, col=col, qt=qt, p=p: e.tensor_scalar(
                        attn_tm[:].rearrange("p (a n) -> p a n", a=16)[:, qt, p * 128 + he * 64:p * 128 + he * 64 + 64],
                        PA[:, 1024 + he * 64:1024 + he * 64 + 64],
                        dnr[:, he * 16 + col:he * 16 + col + 1], None, ALU.mult))
                tvd = kb.v(lambda e, qt=qt, p=p: e.tensor_tensor(
                    attn_tm[:].rearrange("p (a n) -> p a n", a=16)[:, qt, p * 128:p * 128 + 128],
                    attn_tm[:].rearrange("p (a n) -> p a n", a=16)[:, qt, p * 128:p * 128 + 128],
                    bv_bc[:, p * 128:p * 128 + 128], ALU.add), inc=True)
                av_guard[0] = tvd
                ab_guard[0] = kb.cnt["sT"]
            pair_guard[0] = kb.cnt["sT"]
        kb.barrier()
        for nm, src in [("d_attn", attn_tm), ("d_dnm", dnm[:]), ("d_epad", epad), ("d_ab", ab)]:
            if nm in dbg:
                kb.gdma(dbg[nm], src)
        kb.barrier()

        kb.gdma(fmb[0:1, :], I["bo"])
        kb.barrier()
        kb.t(lambda e: e.matmul(PA[:, 0:512], onesb[:], fmb[0:1, 0:512], start=True, stop=True))
        _tbc = kb.t(lambda e: e.matmul(PA[:, 512:1024], onesb[:], fmb[0:1, 512:1024], start=True, stop=True), inc=True)
        W("v", "sT", _tbc)
        kb.v(lambda e: e.tensor_copy(bo_bc[:], PA[:, 0:1024]), inc=True)
        kb.barrier()
        tw = kb.gdma(wsb[:].rearrange("p (c f) -> p c f", c=8),
                     I["woT"].rearrange("(c p) f -> p c f", p=128))
        W("t", "sg", tw)
        ptF = [kb.cnt["sV"], kb.cnt["sV"]]
        evF = {"A": 0, "B": 0}
        s_endF = [kb.cnt["sS"]]
        fmb_guard = [0]
        zst_guard = [0]
        xld_guard = [0]
        x1st_guard = [0]
        for t in range(16):
            src_tm = attn_tm[:].rearrange("p (a n) -> p a n", a=16)[:, t, :]
            pt = PT0 if t % 2 == 0 else PT1
            W("t", "sV", ptF[t % 2])
            tprev = None
            for cb in range(8):
                tprev = kb.t(lambda e, cb=cb, pt=pt, src_tm=src_tm: e.transpose(
                    pt[:, cb * 128:cb * 128 + 128], src_tm[:, cb * 128:cb * 128 + 128], idn[:]), inc=True)
            W("v", "sT", tprev)
            W("v", "sT", fmb_guard[0])
            tev = kb.v(lambda e, pt=pt: e.tensor_copy(fmb[:], pt[:]), inc=True)
            ptF[t % 2] = tev
            ps, pk = (PA, "A") if t % 2 == 0 else (PB, "B")
            W("t", "sV", tev)
            W("t", "sV", evF[pk])
            tmm = None
            for ec in range(8):
                for sb in range(2):
                    tmm = kb.t(lambda e, ec=ec, sb=sb, ps=ps: e.matmul(
                        ps[:, sb * 512:sb * 512 + 512],
                        fmb[:, ec * 128:ec * 128 + 128],
                        wsb[:, ec * D + sb * 512:ec * D + sb * 512 + 512],
                        start=(ec == 0), stop=(ec == 7)), inc=(ec == 7 and sb == 1))
            fmb_guard[0] = tmm
            W("y", "sV", xld_guard[0])
            tldx = kb.ydma(xtile[:, 0:D], xown_t[t])
            W("v", "sT", tmm)
            W("v", "sq", tldx)
            W("v", "sq", x1st_guard[0])
            kb.v(lambda e, ps=ps: e.tensor_add(srow[:], ps[:, 0:1024], bo_bc[:]))
            kb.v(lambda e, t=t: e.tensor_mul(srow[:], srow[:], g_tm[:, t * D:t * D + D]))
            tx1 = kb.v(lambda e: e.tensor_add(xtile[:, D:2 * D], xtile[:, 0:D], srow[:]), inc=True)
            evF[pk] = tx1
            W("y", "sV", tx1)
            tx1s = kb.ydma(x1_out.rearrange("(a p) d -> a p d", p=128)[t], xtile[:, D:2 * D])
            x1st_guard[0] = tx1s
            x1t = xtile[:, D:2 * D]
            W("v", "sS", s_endF[0])
            kb.vW(lambda e, x1t=x1t: e.reduce_sum(stat[:, 0:1], x1t, AX))
            kb.vW(lambda e: e.tensor_scalar_mul(stat[:, 0:1], stat[:, 0:1], 1.0 / D))
            tsub = kb.v(lambda e, x1t=x1t: e.tensor_scalar(xcF, x1t, stat[:, 0:1], None, ALU.subtract), inc=True)
            W("s", "sV", tsub)
            tsq = kb.s(lambda e: e.activation(sqF, xcF, AF.Square), inc=True)
            W("v", "sS", tsq)
            kb.vW(lambda e: e.reduce_sum(stat[:, 1:2], sqF, AX))
            tvar = kb.v(lambda e: e.tensor_scalar(stat[:, 1:2], stat[:, 1:2], 1.0 / D, EPS, ALU.mult, ALU.add), inc=True)
            W("s", "sV", tvar)
            kb.sW(lambda e: e.activation(stat[:, 2:3], stat[:, 1:2], AF.Ln))
            trs = kb.s(lambda e: e.activation(stat[:, 3:4], stat[:, 2:3], AF.Exp, scale=-0.5), inc=True)
            s_endF[0] = trs
            W("v", "sS", trs)
            kb.v(lambda e: e.tensor_scalar(xcF, xcF, stat[:, 3:4], None, ALU.mult))
            W("v", "sT", fmb_guard[0])
            tbf = kb.v(lambda e: e.tensor_copy(fmb[:], xcF), inc=True)
            xld_guard[0] = tbf
            pt2 = PT0 if (t + 1) % 2 == 0 else PT1
            W("t", "sV", tbf)
            W("t", "sV", ptF[(t + 1) % 2])
            tprev = None
            for cb in range(8):
                tprev = kb.t(lambda e, cb=cb, pt2=pt2: e.transpose(
                    pt2[:, cb * 128:cb * 128 + 128], fmb[:, cb * 128:cb * 128 + 128], idn[:]), inc=True)
            W("v", "sT", tprev)
            W("v", "sT", fmb_guard[0])
            tev2 = kb.v(lambda e, pt2=pt2: e.tensor_copy(fmb[:], pt2[:]), inc=True)
            ptF[(t + 1) % 2] = tev2
            W("t", "sV", tev2)
            tzm = None
            for ec in range(8):
                tzm = kb.t(lambda e, ec=ec: e.matmul(
                    PB[0:8, 1024:1152],
                    pw_sb[:, ec * 8:ec * 8 + 8],
                    fmb[:, ec * 128:ec * 128 + 128],
                    start=(ec == 0), stop=(ec == 7)), inc=(ec == 7))
            fmb_guard[0] = tzm
            W("v", "sT", tzm)
            W("v", "sq", zst_guard[0])
            tzs = kb.v(lambda e, t=t: e.tensor_scalar(
                zst[:], PB[0:8, 1024:1152], pb_sb[:], None, ALU.add), inc=True)
            W("t", "sV", tzs)
            W("y", "sV", tzs)
            zst_guard[0] = kb.ydma(zT_out[:, t * 128:t * 128 + 128], zst[:])
        kb.barrier()
        kb.run(block)
    return nc


def host_inputs(x_full, OFF, wq, bq, wk, bk, wv, bv, wo, bo, rel_table,
                w_gate, w_up, w_down, proj_w, proj_b):
    f32 = np.float32
    g = np.arange(8192)
    lag = np.clip(g - 4095 - OFF, -1023, 1023)
    ridx = 1023 - lag
    rbig = np.ascontiguousarray(
        np.concatenate([rel_table[ridx, :].T, rel_table[ridx, :].T], axis=0)).astype(f32)
    return {
        "x": np.ascontiguousarray(x_full, f32),
        "xown": np.ascontiguousarray(x_full[OFF:OFF + NOWN], f32),
        "rbig": rbig,
        "wqT": np.ascontiguousarray(wq.T, f32),
        "wkT": np.ascontiguousarray(wk.T, f32),
        "wvT": np.ascontiguousarray(wv.T, f32),
        "woT": np.ascontiguousarray(wo.T, f32),
        "bq": np.ascontiguousarray(bq.reshape(8, 128).T, f32),
        "bk": np.ascontiguousarray(bk.reshape(8, 128).T, f32),
        "bv": np.ascontiguousarray(bv.reshape(1, 1024), f32),
        "bo": np.ascontiguousarray(bo.reshape(1, 1024), f32),
        "wgT": np.ascontiguousarray(w_gate.T, f32),
        "wuT": np.ascontiguousarray(w_up.T, f32),
        "wdT": np.ascontiguousarray(w_down.T, f32),
        "pwT": np.ascontiguousarray(proj_w.T, f32),
        "pb": np.ascontiguousarray(proj_b.reshape(8, 1), f32),
        "ident": np.eye(128, dtype=f32),
    }


"""Launch 2 builder: layer 1 of mk_recurrent_d2 (Performer attn + relu-FFN + MoE).

Host per-core inputs (f32):
  x1 (4096,1024) full batch; x1own (2048,1024);
  wqT,wkT,wvT,woT (1024,1024)=W.T; bqn,bkn (128,8)=(b*norm).. NO: bq,bk raw (128,8);
  bv (128,8) f-major col chunks; bo (1,1024); projP (128,256) = proj.T stacked x2;
  w1T (1024,4096); b1 (128,32); w2T (4096,1024); b2 (1,1024);
  g2,be2,g3,be3 (1,1024); zt (16384,8) router logits; eb (1,8); cap (scalar via const);
  fc1T (8,1024,2048); f1b (8,128,16); fc2T (8,2048,1024); f2b (8,1,1024);
  sh1T (1024,4096); s1b (128,32); sh2T (4096,1024); s2b (1,1024); ident (128,128).
Output: out (2048,1024) f32.
"""

F32 = mybir.dt.float32
BF16 = mybir.dt.bfloat16
AF = mybir.ActivationFunctionType
ALU = mybir.AluOpType
AX = mybir.AxisListType.X

NFULL, NOWN, D = 4096, 2048, 1024
EPS = 1e-5
NORM = 64.0 ** -0.25
RATIO = 256.0 ** -0.5
EPS_K = 1e-4
CAP = 5120.0
NITER = 44


def build_l1(debug=(), stop_after=None):
    nc = bass.Bass("TRN2", target_bir_lowering=False)
    I = {}
    for nm, shp in [("x1", [NFULL, D]), ("x1own", [NOWN, D]),
                    ("wqT", [D, D]), ("wkT", [D, D]), ("wvT", [D, D]), ("woT", [D, D]),
                    ("bq", [128, 8]), ("bk", [128, 8]), ("bvc", [128, 8]), ("bo", [1, D]),
                    ("projP", [128, 256]),
                    ("w1T", [D, 4096]), ("b1", [128, 32]), ("w2T", [4096, D]), ("b2", [1, D]),
                    ("g2", [1, D]), ("be2", [1, D]), ("g3", [1, D]), ("be3", [1, D]),
                    ("zt", [16384, 8]), ("eb", [1, 8]),
                    ("fc1T", [8, D, 2048]), ("f1b", [8, 128, 16]),
                    ("fc2T", [8, 2048, D]), ("f2b", [8, 1, D]),
                    ("sh1T", [D, 4096]), ("s1b", [128, 32]),
                    ("sh2T", [4096, D]), ("s2b", [1, D]), ("ident", [128, 128])]:
        I[nm] = nc.dram_tensor(nm, shp, F32, kind="ExternalInput").ap()
    out_t = nc.dram_tensor("out", [NOWN, D], F32, kind="ExternalOutput").ap()
    dbg = {}
    for nm, shp in [("d_xn1", [128, 8 * NFULL]), ("d_attn", [128, 8 * NOWN]),
                    ("d_s1", [128, 16 * D]), ("d_y", [128, 16 * D]),
                    ("d_wm", [128, 1024]), ("d_tau", [8, 8]), ("d_kp", [128, 256]),
                    ("d_ctx", [128, 2 * 65]), ("d_g1", [128, 16 * D]),
                    ("d_kfm", [128, 8 * NFULL]), ("d_vtm", [128, 32 * D]),
                    ("d_qfm", [128, 8 * NOWN]), ("d_x1a", [128, 16 * D]),
                    ("d_ff1", [128, 16 * 4096]), ("d_xaf", [128, 16 * 1024]),
                    ("d_h1", [128, 2048]), ("d_oe", [128, 1024])]:
        if nm in debug:
            dbg[nm] = nc.dram_tensor(nm, shp, F32, kind="ExternalOutput").ap()

    k_sp = nc.dram_tensor("k_sp", [128, 8 * NFULL], BF16, kind="Internal").ap()
    v_sp = nc.dram_tensor("v_sp", [128, 32 * D], BF16, kind="Internal").ap()
    q_sp = nc.dram_tensor("q_sp", [128, 8 * NOWN], BF16, kind="Internal").ap()
    xn1tm_sp = nc.dram_tensor("xn1tm_sp", [128, 16 * D], F32, kind="Internal").ap()
    ff1_sp = nc.dram_tensor("ff1_sp", [128, 16 * 4096], BF16, kind="Internal").ap()
    sh1_sp = nc.dram_tensor("sh1_sp", [128, 16 * 4096], BF16, kind="Internal").ap()
    y_sp = nc.dram_tensor("y_sp", [128, 16 * D], F32, kind="Internal").ap()

    Al = lambda *a: nc.alloc_sbuf_tensor(*a)
    XB = Al("XB", [128, 49152], BF16)
    S32a = Al("S32a", [128, 16384], BF16)
    S32b = Al("S32b", [128, 16384], BF16)
    MISC = Al("MISC", [128, 5120], F32)
    idn = Al("idn", [128, 128], BF16)
    bqn = Al("bqn", [128, 8], F32)
    bkn = Al("bkn", [128, 8], F32)
    bo2 = Al("bo2", [128, D], BF16)
    g2b = Al("g2b", [128, D], BF16)
    e2b = Al("e2b", [128, D], BF16)
    g3b = Al("g3b", [128, D], BF16)
    e3b = Al("e3b", [128, D], BF16)
    b2b = Al("b2b", [128, D], BF16)
    s2bb = Al("s2bb", [128, D], BF16)
    ebb = Al("ebb", [128, 8], F32)
    onesb = Al("onesb", [1, 128], BF16)
    onesf = Al("onesf", [1, 128], F32)
    onef = Al("onef", [128, 1], F32)
    onef64 = Al("onef64", [128, 1], BF16)
    stat = Al("stat", [128, 8], F32)
    tau = Al("tau", [8, 8], F32)
    diagk = Al("diagk", [128, 32], F32)
    diagq = Al("diagq", [128, 16], F32)
    bias32 = Al("bias32", [128, 64], F32)
    mkcol = Al("mkcol", [128, 2], F32)
    stabb = Al("stabb", [128, 2], F32)
    fmb = Al("fmb", [128, 1024], BF16)
    kpq = Al("kpq", [128, 512], BF16)
    ctxs = Al("ctxs", [128, 2 * 2 * 65], BF16)
    ff1T = Al("ff1T", [128, 16 * 128], BF16)

    xn1_fm = XB[:, 0:32768]
    xn1_own = XB[:, 32768:49152]
    wsb = XB[:, 0:8192]
    wsb = S32a[:, 0:8192]
    kpair = S32a[:, 8192:12288]
    vpair = S32a[:, 12288:16384]
    attn1_fm = XB[:, 0:16384]
    qpT = S32b[:, 0:8192]
    s1_tm = S32b[:, 0:16384]
    xpsq = XB[:, 16384:20480]
    w1stg = XB[:, 0:32768]
    w2stg = XB[:, 0:32768]
    fc1stg = XB[:, 0:16384]
    fc2stg = XB[:, 16384:32768]
    sh1stg = XB[:, 0:32768]
    sh2stg = XB[:, 0:32768]
    xtile = MISC[:, 0:2048]
    tmpf = MISC[:, 2048:4096]
    tmpi = MISC[:, 4096:5120].bitcast(mybir.dt.int32)
    qpair = MISC[:, 4096:5120].bitcast(BF16)
    xaf = MISC[:, 4096:5120].bitcast(BF16)
    ztile = MISC[:, 4096:4224]
    midt = MISC[0:1, 4096:5120]
    yact = MISC[:, 0:2048]
    drow = MISC[0:1, 0:512]
    wmat = S32a[:, 0:2048].bitcast(F32)
    wkept = S32a[:, 2048:4096].bitcast(F32)
    h1T = S32a[:, 4096:8192]

    with (
        nc.psum_tensor("PA", [128, 1536], F32) as PA,
        nc.psum_tensor("PB", [128, 1536], F32) as PB,
        nc.psum_tensor("PT0", [128, 1024], BF16) as PT0,
        nc.psum_tensor("PT1", [128, 1024], BF16) as PT1,
        nc.Block() as block,
    ):
        kb = KB(nc)
        W = kb.wait

        def bcast_row(dst_bf16, src_row_f32, width=D):
            tg = kb.gdma(fmb[0:1, 0:width], src_row_f32)
            kb.barrier()
            tmm = None
            for s_ in range(0, width, 512):
                w_ = min(512, width - s_)
                tmm = kb.t(lambda e, s_=s_, w_=w_: e.matmul(
                    PA[:, s_ % 1024:s_ % 1024 + w_], onesb[:], fmb[0:1, s_:s_ + w_],
                    start=True, stop=True), inc=(s_ + 512 >= width))
            W("v", "sT", tmm)
            kb.v(lambda e, width=width: e.tensor_copy(dst_bf16[:, 0:width], PA[:, 0:width] if width <= 1024 else None), inc=True)
            kb.barrier()

        kb.v(lambda e: e.memset(onesb[:], 1.0))
        kb.v(lambda e: e.memset(onesf[:], 1.0))
        kb.v(lambda e: e.memset(onef[:], 1.0))
        kb.v(lambda e: e.memset(onef64[:], 1.0))
        kb.ydma(bqn[:], I["bq"])
        kb.ydma(bkn[:], I["bk"])
        kb.gdma(idn[:], I["ident"])
        kb.barrier()
        kb.v(lambda e: e.tensor_scalar_mul(bqn[:], bqn[:], NORM))
        kb.v(lambda e: e.tensor_scalar_mul(bkn[:], bkn[:], NORM), inc=True)
        for dst, src in [(g2b, "g2"), (e2b, "be2"), (g3b, "g3"), (e3b, "be3"),
                         (b2b, "b2"), (s2bb, "s2b")]:
            bcast_row(dst, I[src])
        tg = kb.gdma(stat[0:1, 0:8], I["eb"])
        kb.barrier()
        kb.t(lambda e: e.matmul(PA[:, 0:8], onesf[:], stat[0:1, 0:8], start=True, stop=True), inc=True)
        kb.barrier()
        kb.v(lambda e: e.tensor_copy(ebb[:], PA[:, 0:8]), inc=True)
        twv = kb.gdma(wsb[:].rearrange("p (c f) -> p c f", c=8),
                      I["woT"].rearrange("(c p) f -> p c f", p=128))
        tbv = kb.ydma(stat[:, 0:8] if False else bqn[:] if False else MISC[:, 0:8], I["bvc"])
        bvc = MISC[:, 0:8]
        kb.barrier()
        kb.v(lambda e: e.tensor_copy(fmb[:, 0:8], bvc[:, 0:8]), inc=True)
        kb.barrier()
        tmm = None
        for sb in range(2):
            for ec in range(8):
                tmm = kb.t(lambda e, ec=ec, sb=sb: e.matmul(
                    PB[0:1, sb * 512:sb * 512 + 512], fmb[:, ec:ec + 1],
                    wsb[:, ec * D + sb * 512:ec * D + sb * 512 + 512],
                    start=(ec == 0), stop=(ec == 7)), inc=(ec == 7 and sb == 1))
        kb.barrier()
        tbo = kb.gdma(fmb[0:1, 0:1024], I["bo"])
        kb.barrier()
        kb.v(lambda e: e.tensor_copy(kpq[0:1, 0:128] if False else fmb[1:2, 0:1024] if False else tmpf[0:1, 0:1024], PB[0:1, 0:1024]), inc=True)
        kb.barrier()
        kb.v(lambda e: e.tensor_tensor(tmpf[0:1, 0:1024], tmpf[0:1, 0:1024], fmb[0:1, 0:1024], ALU.add), inc=True)
        kb.barrier()
        kb.v(lambda e: e.tensor_copy(fmb[0:1, 0:1024], tmpf[0:1, 0:1024]), inc=True)
        kb.barrier()
        tmm = None
        for s_ in (0, 512):
            tmm = kb.t(lambda e, s_=s_: e.matmul(PA[:, s_:s_ + 512], onesb[:], fmb[0:1, s_:s_ + 512],
                                                 start=True, stop=True), inc=(s_ == 512))
        W("v", "sT", tmm)
        kb.v(lambda e: e.tensor_copy(bo2[:], PA[:, 0:1024]), inc=True)
        kb.barrier()

        if stop_after == "G":
            kb.run(block)
            return nc
        xf_t = I["x1"].rearrange("(a p) d -> a p d", p=128)
        xo_t = I["x1own"].rearrange("(a p) d -> a p d", p=128)
        pt_guard = [kb.cnt["sV"], kb.cnt["sV"]]
        xload_guard = [0, 0]
        s_end = [kb.cnt["sS"]]
        pti_c = [0]

        def transp8(src_tm_bf16, dst_cview, pti):
            pt = PT0 if pti % 2 == 0 else PT1
            W("t", "sV", pt_guard[pti % 2])
            ttr = None
            for cb in range(8):
                ttr = kb.t(lambda e, cb=cb, pt=pt, s=src_tm_bf16: e.transpose(
                    pt[:, cb * 128:cb * 128 + 128], s[:, cb * 128:cb * 128 + 128], idn[:]), inc=True)
            W("v", "sT", ttr)
            tev = kb.v(lambda e, pt=pt, d=dst_cview: e.tensor_copy(
                d, pt[:].rearrange("p (c n) -> p c n", c=8)), inc=True)
            pt_guard[pti % 2] = tev
            return tev

        def ln_tile(ti, src, fm_dst_col, own_t):
            buf = ti % 2
            xt = xtile[:, buf * D:(buf + 1) * D]
            W("y", "sV", xload_guard[buf])
            tld = kb.ydma(xt, src)
            W("v", "sq", tld)
            W("v", "sS", s_end[0])
            kb.vW(lambda e, xt=xt: e.reduce_sum(stat[:, 0:1], xt, AX))
            kb.vW(lambda e: e.tensor_scalar_mul(stat[:, 0:1], stat[:, 0:1], 1.0 / D))
            tsub = kb.v(lambda e, xt=xt: e.tensor_scalar(tmpf[:, 0:D], xt, stat[:, 0:1], None, ALU.subtract), inc=True)
            xload_guard[buf] = tsub
            W("s", "sV", tsub)
            tsq = kb.s(lambda e: e.activation(tmpf[:, D:2 * D], tmpf[:, 0:D], AF.Square), inc=True)
            W("v", "sS", tsq)
            kb.vW(lambda e: e.reduce_sum(stat[:, 1:2], tmpf[:, D:2 * D], AX))
            tvar = kb.v(lambda e: e.tensor_scalar(stat[:, 1:2], stat[:, 1:2], 1.0 / D, EPS, ALU.mult, ALU.add), inc=True)
            W("s", "sV", tvar)
            kb.sW(lambda e: e.activation(stat[:, 2:3], stat[:, 1:2], AF.Ln))
            trs = kb.s(lambda e: e.activation(stat[:, 3:4], stat[:, 2:3], AF.Exp, scale=-0.5), inc=True)
            s_end[0] = trs
            W("v", "sS", trs)
            kb.v(lambda e: e.tensor_scalar(tmpf[:, 0:D], tmpf[:, 0:D], stat[:, 3:4], None, ALU.mult))
            tbf = kb.v(lambda e: e.tensor_copy(fmb[:], tmpf[:, 0:D]), inc=True)
            if own_t is not None:
                W("y", "sV", tbf)
                kb.ydma(xn1tm_sp[:, own_t * D:own_t * D + D], tmpf[:, 0:D])
            W("t", "sV", tbf)
            if fm_dst_col is not None:
                dst = xn1_fm.rearrange("p (c n) -> p c n", c=8)[:, :, fm_dst_col:fm_dst_col + 128]
            else:
                dst = xn1_own.rearrange("p (c n) -> p c n", c=8)[:, :, own_t * 128:own_t * 128 + 128]
            transp8(fmb, dst, pti_c[0]); pti_c[0] += 1

        for ti in range(32):
            ln_tile(ti, xf_t[ti], ti * 128, None)
        for t in range(16):
            ln_tile(32 + t, xo_t[t], None, t)
        kb.barrier()
        if "d_xn1" in dbg:
            kb.gdma(dbg["d_xn1"], xn1_fm)
        kb.barrier()

        if stop_after == "H":
            kb.run(block)
            return nc
        ev_guard = {"A": None, "B": None}
        stg_guard = [0, 0]
        gidx = [0]
        wsb_guard = [0]
        ksg = MISC[:, 0:1536].bitcast(BF16)

        def mm_group(ps, emit_mms, evac_fn, spill_dst, eng="v"):
            pkey = "A" if ps is PA else "B"
            sbf = gidx[0] % 2
            if ev_guard[pkey]:
                W("t", ev_guard[pkey][0], ev_guard[pkey][1])
            tmm = emit_mms()
            W(eng, "sT", tmm)
            W(eng, "sq", stg_guard[sbf])
            emit = kb.v if eng == "v" else kb.s
            sem = "sV" if eng == "v" else "sS"
            tse = emit(lambda e, sbf=sbf: evac_fn(e, ksg[:, sbf * 1536:sbf * 1536 + 1536]), inc=True)
            ev_guard[pkey] = (sem, tse)
            W("y", sem, tse)
            tst = kb.ydma(spill_dst, ksg[:, sbf * 1536:sbf * 1536 + spill_dst.shape[-1]])
            stg_guard[sbf] = tst
            gidx[0] += 1

        def proj_fm(wname, src_fm, ncols, spill, bias_sb, scale):
            W("g", "sT", wsb_guard[0])
            tw = kb.gdma(wsb[:].rearrange("p (c f) -> p c f", c=8),
                         I[wname].rearrange("(c p) f -> p c f", p=128))
            W("t", "sg", tw)
            groups = [(0, 1536), (1536, 1536), (3072, 1024)] if ncols == 4096 else [(0, 1536), (1536, 512)]
            for fc in range(8):
                for (nb, nw) in groups:
                    ps = PA if gidx[0] % 2 == 0 else PB
                    def mms(fc=fc, nb=nb, nw=nw, ps=ps, src=src_fm, ncols=ncols):
                        tmm = None
                        for ec in range(8):
                            for sb in range(nw // 512):
                                tmm = kb.t(lambda e, fc=fc, ec=ec, nb=nb, sb=sb, ps=ps, src=src, ncols=ncols: e.matmul(
                                    ps[:, sb * 512:sb * 512 + 512],
                                    wsb[:, ec * D + fc * 128:ec * D + fc * 128 + 128],
                                    src[:, ec * ncols + nb + sb * 512:ec * ncols + nb + sb * 512 + 512],
                                    start=(ec == 0), stop=(ec == 7)), inc=(ec == 7 and sb == nw // 512 - 1))
                        return tmm
                    def ev(e, dst, fc=fc, nw=nw, ps=ps, bias_sb=bias_sb, scale=scale):
                        return e.tensor_scalar(dst[:, 0:nw], ps[:, 0:nw], scale,
                                               bias_sb[:, fc:fc + 1], ALU.mult, ALU.add)
                    mm_group(ps, mms, ev, spill.rearrange("p (c n) -> p c n", c=8)[:, fc, nb:nb + nw])
            wsb_guard[0] = kb.cnt["sT"]

        proj_fm("wkT", xn1_fm, NFULL, k_sp, bkn, NORM)
        kb.barrier()
        proj_fm("wqT", xn1_own, NOWN, q_sp, bqn, NORM)
        kb.barrier()
        W("g", "sT", wsb_guard[0])
        tw = kb.gdma(wsb[:].rearrange("p (c f) -> p c f", c=8),
                     I["wvT"].rearrange("(c p) f -> p c f", p=128))
        W("t", "sg", tw)
        for nt in [16, 17] + list(range(32)):
            ps = PA if gidx[0] % 2 == 0 else PB
            def mms(nt=nt, ps=ps):
                tmm = None
                for ec in range(8):
                    for sb in range(2):
                        tmm = kb.t(lambda e, nt=nt, ec=ec, sb=sb, ps=ps: e.matmul(
                            ps[:, sb * 512:sb * 512 + 512],
                            xn1_fm[:, ec * NFULL + nt * 128:ec * NFULL + nt * 128 + 128],
                            wsb[:, ec * D + sb * 512:ec * D + sb * 512 + 512],
                            start=(ec == 0), stop=(ec == 7)), inc=(ec == 7 and sb == 1))
                return tmm
            def ev(e, dst, ps=ps):
                return e.tensor_copy(dst[:, 0:1024], ps[:, 0:1024])
            mm_group(ps, mms, ev, v_sp[:, nt * D:nt * D + D])
        kb.barrier()

        for nm, src in [("d_kfm", k_sp), ("d_vtm", v_sp), ("d_qfm", q_sp)]:
            if nm in dbg:
                kb.gdma(dbg[nm], src)
        kb.barrier()

        projs = Al("projs", [128, 256], BF16)
        idf32 = Al("idf32", [128, 128], F32)
        vrhs = Al("vrhs", [128, 2 * 130], BF16)
        kb.gdma(projs[:], I["projP"])
        kb.ydma(idf32[:], I["ident"])
        kb.barrier()
        kb.v(lambda e: e.memset(vrhs[:, 64:65], 1.0))
        kb.v(lambda e: e.memset(vrhs[:, 129:130], 1.0))
        kb.v(lambda e: e.memset(vrhs[:, 194:195], 1.0))
        kb.v(lambda e: e.memset(vrhs[:, 259:260], 1.0), inc=True)
        kb.barrier()

        if stop_after == "I":
            kb.run(block)
            return nc
        pair_g = [0]
        for p in range(8):
            W("y", "sT", pair_g[0])
            W("y", "sV", kb.cnt["sV"])
            tk = kb.ydma(kpair[:], k_sp.rearrange("p (c n) -> p c n", c=8)[:, p, :])
            tv_ = kb.ydma(vpair[:].rearrange("p (k c) -> p k c", k=32),
                          v_sp.rearrange("p (k f) -> p k f", k=32)[:, :, p * 128:p * 128 + 128])
            tq_ = kb.ydma(qpair[:], q_sp.rearrange("p (c n) -> p c n", c=8)[:, p, :])
            W("s", "sq", tk)
            tks = kb.s(lambda e: e.activation(xpsq[:], kpair[:], AF.Square), inc=True)
            W("t", "sS", tks)
            for he in range(2):
                tdg = None
                for nt in range(32):
                    tdg = kb.t(lambda e, he=he, nt=nt: e.matmul(
                        (PA if he == 0 else PB)[:, 1024 + nt:1024 + nt + 1],
                        xpsq[64 * he:64 * he + 64, nt * 128:nt * 128 + 128],
                        onef64[64 * he:64 * he + 64, :],
                        start=True, stop=True), inc=(nt == 31))
                W("v", "sT", tdg)
                kb.v(lambda e, he=he: e.tensor_scalar_mul(
                    diagk[:], (PA if he == 0 else PB)[:, 1024:1056], 0.5), inc=True)
                kb.vW(lambda e: e.memset(mkcol[:, 0:1], -1e30))
                mm_prev = [0]
                for nt in range(32):
                    ps = PA if nt % 2 == 0 else PB
                    W("t", "sV", mm_prev[0])
                    tx = kb.t(lambda e, he=he, nt=nt, ps=ps: e.matmul(
                        ps[:, 0:256],
                        kpair[64 * he:64 * he + 64, nt * 128:nt * 128 + 128],
                        projs[64 * he:64 * he + 64, :],
                        start=True, stop=True), inc=True)
                    W("v", "sT", tx)
                    kb.vW(lambda e, ps=ps: e.reduce_max(stat[:, 6:7], ps[:, 0:256], AX))
                    mm_prev[0] = kb.vW(lambda e: e.tensor_max(mkcol[:, 0:1], mkcol[:, 0:1], stat[:, 6:7]))
                W("t", "sV", mm_prev[0])
                tt1 = kb.t(lambda e: e.matmul(PB[0:1, 512:640], mkcol[:, 0:1], idf32[:], start=True, stop=True), inc=True)
                W("v", "sT", tt1)
                kb.vW(lambda e: e.reduce_max(stat[0:1, 7:8], PB[0:1, 512:640], AX))
                tt2 = kb.v(lambda e: e.tensor_copy(mkcol[0:1, 1:2], stat[0:1, 7:8]), inc=True)
                W("t", "sV", tt2)
                tt3 = kb.t(lambda e: e.matmul(PB[:, 640:641], onesf[:], mkcol[0:1, 1:2], start=True, stop=True), inc=True)
                W("v", "sT", tt3)
                kb.vW(lambda e, he=he: e.tensor_copy(stabb[:, he:he + 1], PB[:, 640:641]))
                tb32 = kb.vW(lambda e, he=he: e.tensor_scalar(
                    bias32[:, he * 32:he * 32 + 32], diagk[:], stabb[:, he:he + 1], -1.0,
                    ALU.add, ALU.mult))
                kp_g = [tb32]
                for nt in range(32):
                    ps = PA if nt % 2 == 0 else PB
                    W("t", "sV", kp_g[0])
                    tx = kb.t(lambda e, he=he, nt=nt, ps=ps: e.matmul(
                        ps[:, 0:256],
                        kpair[64 * he:64 * he + 64, nt * 128:nt * 128 + 128],
                        projs[64 * he:64 * he + 64, :],
                        start=True, stop=True), inc=True)
                    W("s", "sT", tx)
                    buf = nt % 2
                    tex = kb.s(lambda e, he=he, nt=nt, ps=ps, buf=buf: e.activation(
                        kpq[:, buf * 256:buf * 256 + 256], ps[:, 0:256], AF.Exp,
                        bias=bias32[:, he * 32 + nt:he * 32 + nt + 1]), inc=True)
                    W("v", "sS", tex)
                    W("v", "sq", tv_)
                    tkp = kb.v(lambda e, buf=buf: e.tensor_scalar(
                        kpq[:, buf * 256:buf * 256 + 256], kpq[:, buf * 256:buf * 256 + 256],
                        EPS_K, RATIO, ALU.add, ALU.mult), inc=True)
                    tvr = kb.v(lambda e, he=he, nt=nt, buf=buf: e.tensor_copy(
                        vrhs[:, buf * 130 + 0:buf * 130 + 64],
                        vpair[:, nt * 128 + he * 64:nt * 128 + he * 64 + 64]), inc=True)
                    W("t", "sV", tvr)
                    tcx = None
                    for mh in range(2):
                        col = 512 if mh == 0 else 1024
                        pc = PA if he == 0 else PB
                        tcx = kb.t(lambda e, he=he, nt=nt, mh=mh, buf=buf, col=col, pc=pc: e.matmul(
                            pc[:, col:col + 65],
                            kpq[:, buf * 256 + mh * 128:buf * 256 + mh * 128 + 128],
                            vrhs[:, buf * 130:buf * 130 + 65],
                            start=(nt == 0), stop=(nt == 31)), inc=(nt == 31 and mh == 1))
                    kp_g[0] = tcx if nt >= 30 else kp_g[0]
                W("v", "sT", kp_g[0])
                for mh in range(2):
                    col = 512 if mh == 0 else 1024
                    pc = PA if he == 0 else PB
                    kb.v(lambda e, he=he, mh=mh, col=col, pc=pc: e.tensor_copy(
                        ctxs[:, (he * 2 + mh) * 65:(he * 2 + mh) * 65 + 65], pc[:, col:col + 65]), inc=(mh == 1))
            tctx = kb.cnt["sV"]
            if "d_ctx" in dbg and p == 0:
                W("g", "sV", tctx)
                kb.gdma(dbg["d_ctx"], ctxs[:, 0:130])
            W("s", "sq", tq_)
            W("s", "sV", tctx)
            tqs = kb.s(lambda e: e.activation(xpsq[:, 0:2048], qpair[:], AF.Square), inc=True)
            W("t", "sS", tqs)
            for he in range(2):
                tdg = None
                for nt in range(16):
                    tdg = kb.t(lambda e, he=he, nt=nt: e.matmul(
                        (PA if he == 0 else PB)[:, 1056 + nt:1056 + nt + 1],
                        xpsq[64 * he:64 * he + 64, nt * 128:nt * 128 + 128],
                        onef64[64 * he:64 * he + 64, :],
                        start=True, stop=True), inc=(nt == 15))
                W("v", "sT", tdg)
                kb.v(lambda e, he=he: e.tensor_scalar_mul(
                    diagq[:], (PA if he == 0 else PB)[:, 1056:1072], 0.5), inc=True)
                qp_g = [kb.cnt["sV"]]
                ptj = [kb.cnt["sV"], kb.cnt["sV"]]
                for nt in range(16):
                    ps = PA if nt % 2 == 0 else PB
                    W("t", "sV", qp_g[0])
                    tx = kb.t(lambda e, he=he, nt=nt, ps=ps: e.matmul(
                        ps[:, 0:256],
                        qpair[64 * he:64 * he + 64, nt * 128:nt * 128 + 128],
                        projs[64 * he:64 * he + 64, :],
                        start=True, stop=True), inc=True)
                    W("v", "sT", tx)
                    kb.vW(lambda e, ps=ps: e.reduce_max(stat[:, 6:7], ps[:, 0:256], AX))
                    trm = kb.vW(lambda e, he=he, nt=nt: e.tensor_scalar(
                        stat[:, 5:6], diagq[:, nt:nt + 1], stat[:, 6:7], -1.0, ALU.add, ALU.mult))
                    W("s", "sV", trm)
                    tex = kb.s(lambda e, ps=ps: e.activation(
                        kpq[:, 0:256], ps[:, 0:256], AF.Exp, bias=stat[:, 5:6]), inc=True)
                    W("v", "sS", tex)
                    tqp = kb.v(lambda e: e.tensor_scalar(
                        kpq[:, 0:256], kpq[:, 0:256], EPS_K, RATIO, ALU.add, ALU.mult), inc=True)
                    qp_g[0] = tqp
                    pt = PT0 if nt % 2 == 0 else PT1
                    W("t", "sV", ptj[nt % 2])
                    W("t", "sV", tqp)
                    ttr = None
                    for mh in range(2):
                        ttr = kb.t(lambda e, mh=mh, pt=pt: e.transpose(
                            pt[:, mh * 128:mh * 128 + 128], kpq[:, mh * 128:mh * 128 + 128], idn[:]), inc=True)
                    W("v", "sT", ttr)
                    tev = kb.v(lambda e, he=he, nt=nt, pt=pt: e.tensor_copy(
                        qpT[:].rearrange("p (h m n) -> p h m n", h=2, m=2)[:, he, :, nt * 128:nt * 128 + 128],
                        pt[:, 0:256].rearrange("p (m n) -> p m n", m=2)), inc=True)
                    ptj[nt % 2] = tev
                W("t", "sV", ptj[0])
                W("t", "sV", ptj[1])
                for qc in range(4):
                    tom = None
                    for mh in range(2):
                        tom = kb.t(lambda e, he=he, mh=mh, qc=qc: e.matmul(
                            PA[0:65, 0:512],
                            ctxs[:, (he * 2 + mh) * 65:(he * 2 + mh) * 65 + 65],
                            qpT[:].rearrange("p (h m n) -> p h m n", h=2, m=2)[:, he, mh, qc * 512:qc * 512 + 512],
                            start=(mh == 0), stop=(mh == 1)), inc=(mh == 1))
                    W("v", "sT", tom)
                    trc = kb.vW(lambda e: e.reciprocal(drow[0:1, 0:512], PA[64:65, 0:512]))
                    W("t", "sV", trc)
                    tbc = kb.t(lambda e: e.matmul(PB[0:64, 0:512], onesf[0:1, 0:64], drow[0:1, 0:512],
                                                  start=True, stop=True), inc=True)
                    W("v", "sT", tbc)
                    tbe = kb.vW(lambda e: e.tensor_copy(fmb[0:64, 0:512], PB[0:64, 0:512]))
                    tdv = kb.v(lambda e, p=p, he=he, qc=qc: e.tensor_tensor(
                        attn1_fm[64 * he:64 * he + 64, p * 2048 + qc * 512:p * 2048 + qc * 512 + 512],
                        PA[0:64, 0:512], fmb[0:64, 0:512], ALU.mult), inc=True)
                    W("t", "sV", tdv)
            pair_g[0] = kb.cnt["sT"]
        kb.barrier()
        if "d_attn" in dbg:
            kb.gdma(dbg["d_attn"], attn1_fm)
        kb.barrier()

        if stop_after == "J":
            kb.run(block)
            return nc
        tw = kb.gdma(wsb[:].rearrange("p (c f) -> p c f", c=8),
                     I["woT"].rearrange("(c p) f -> p c f", p=128))
        x1a_sp = nc.dram_tensor("x1a_sp", [128, 16 * D], F32, kind="Internal").ap()
        x1afm_sp = nc.dram_tensor("x1afm_sp", [128, 16 * 1024], BF16, kind="Internal").ap()
        W("t", "sg", tw)
        evK = {"A": None, "B": None}
        s_end2 = [kb.cnt["sS"]]
        xg2 = [0, 0]
        ptk = [kb.cnt["sV"], kb.cnt["sV"]]
        for t in range(16):
            ps, pk = (PA, "A") if t % 2 == 0 else (PB, "B")
            if evK[pk]:
                W("t", evK[pk][0], evK[pk][1])
            tmm = None
            for ec in range(8):
                for sb in range(2):
                    tmm = kb.t(lambda e, t=t, ec=ec, sb=sb, ps=ps: e.matmul(
                        ps[:, sb * 512:sb * 512 + 512],
                        attn1_fm[:, ec * 2048 + t * 128:ec * 2048 + t * 128 + 128],
                        wsb[:, ec * D + sb * 512:ec * D + sb * 512 + 512],
                        start=(ec == 0), stop=(ec == 7)), inc=(ec == 7 and sb == 1))
            W("y", "sV", xg2[t % 2])
            tldv = kb.ydma(xtile[:, (t % 2) * D:(t % 2) * D + D], xn1tm_sp[:, t * D:t * D + D])
            W("v", "sT", tmm)
            W("v", "sq", tldv)
            vt = xtile[:, (t % 2) * D:(t % 2) * D + D]
            kb.v(lambda e, ps=ps: e.tensor_add(tmpf[:, 0:D], ps[:, 0:1024], bo2[:]))
            tx1a = kb.v(lambda e, vt=vt: e.tensor_add(tmpf[:, 0:D], tmpf[:, 0:D], vt), inc=True)
            evK[pk] = ("sV", tx1a)
            xg2[t % 2] = tx1a
            W("v", "sS", s_end2[0])
            kb.vW(lambda e: e.reduce_sum(stat[:, 0:1], tmpf[:, 0:D], AX))
            kb.vW(lambda e: e.tensor_scalar_mul(stat[:, 0:1], stat[:, 0:1], 1.0 / D))
            tsub = kb.v(lambda e: e.tensor_scalar(tmpf[:, 0:D], tmpf[:, 0:D], stat[:, 0:1], None, ALU.subtract), inc=True)
            W("s", "sV", tsub)
            tsq = kb.s(lambda e: e.activation(tmpf[:, D:2 * D], tmpf[:, 0:D], AF.Square), inc=True)
            W("v", "sS", tsq)
            kb.vW(lambda e: e.reduce_sum(stat[:, 1:2], tmpf[:, D:2 * D], AX))
            tvar = kb.v(lambda e: e.tensor_scalar(stat[:, 1:2], stat[:, 1:2], 1.0 / D, EPS, ALU.mult, ALU.add), inc=True)
            W("s", "sV", tvar)
            kb.sW(lambda e: e.activation(stat[:, 2:3], stat[:, 1:2], AF.Ln))
            trs = kb.s(lambda e: e.activation(stat[:, 3:4], stat[:, 2:3], AF.Exp, scale=-0.5), inc=True)
            s_end2[0] = trs
            W("v", "sS", trs)
            kb.v(lambda e: e.tensor_scalar(tmpf[:, 0:D], tmpf[:, 0:D], stat[:, 3:4], None, ALU.mult))
            kb.v(lambda e: e.tensor_mul(tmpf[:, 0:D], tmpf[:, 0:D], g2b[:]))
            tx1b = kb.v(lambda e: e.tensor_add(tmpf[:, 0:D], tmpf[:, 0:D], e2b[:]), inc=True)
            W("y", "sV", tx1b)
            tsp = kb.ydma(x1a_sp[:, t * D:t * D + D], tmpf[:, 0:D])
            tbf = kb.v(lambda e: e.tensor_copy(fmb[:], tmpf[:, 0:D]), inc=True)
            pt = PT0 if t % 2 == 0 else PT1
            W("t", "sV", tbf)
            W("t", "sV", ptk[t % 2])
            ttr = None
            for cb in range(8):
                ttr = kb.t(lambda e, cb=cb, pt=pt: e.transpose(
                    pt[:, cb * 128:cb * 128 + 128], fmb[:, cb * 128:cb * 128 + 128], idn[:]), inc=True)
            W("v", "sT", ttr)
            tev = kb.v(lambda e, pt=pt: e.tensor_copy(fmb[:], pt[:]), inc=True)
            ptk[t % 2] = tev
            W("y", "sV", tev)
            kb.ydma(x1afm_sp[:, t * 1024:t * 1024 + 1024], fmb[:])
        kb.barrier()
        if "d_x1a" in dbg:
            kb.gdma(dbg["d_x1a"], x1a_sp)
        kb.barrier()
        if stop_after == "K1":
            kb.run(block)
            return nc
        b1sb = Al("b1sb", [128, 32], F32)
        s1bsb = Al("s1bsb", [128, 32], F32)
        kb.ydma(b1sb[:], I["b1"])
        kb.ydma(s1bsb[:], I["s1b"])
        tw1 = kb.gdma(w1stg[:].rearrange("p (c f) -> p c f", c=8),
                      I["w1T"].rearrange("(c p) f -> p c f", p=128))
        kb.barrier()

        HSLOTS = [PA[:, 0:128], PA[:, 512:640], PA[:, 1024:1152], PB[:, 0:128]]

        def mlp_h_pass(wstg_v, bias_sb, act, src_fm_loader, spill, nht, slots=HSLOTS):
            slot_g = [0, 0, 0, 0]
            sp_g = [0]
            for t in range(16):
                tldx = src_fm_loader(t)
                W("t", "sg", tldx)
                for ht in range(nht):
                    sl = ht % 4
                    W("t", "sS", slot_g[sl])
                    tmm = None
                    for ec in range(8):
                        tmm = kb.t(lambda e, ht=ht, ec=ec, sl=sl, wstg_v=wstg_v, nht=nht, slots=slots: e.matmul(
                            slots[sl],
                            wstg_v[:, ec * (nht * 128) + ht * 128:ec * (nht * 128) + ht * 128 + 128],
                            fmb[:, ec * 128:ec * 128 + 128],
                            start=(ec == 0), stop=(ec == 7)), inc=(ec == 7))
                    W("s", "sT", tmm)
                    if ht % 16 == 0:
                        W("s", "sg", sp_g[0])
                    tse = kb.s(lambda e, ht=ht, sl=sl, bias_sb=bias_sb, act=act, slots=slots: e.activation(
                        ff1T[:, (ht % 16) * 128:(ht % 16) * 128 + 128],
                        slots[sl], act,
                        bias=bias_sb[:, ht:ht + 1]), inc=True)
                    slot_g[sl] = tse
                    if ht % 16 == 15:
                        W("g", "sS", tse)
                        sp_g[0] = kb.gdma(spill[:, t * (nht * 128) + (ht // 16) * 2048:
                                                t * (nht * 128) + (ht // 16) * 2048 + 2048], ff1T[:])

        def mk_loader(sp):
            def load(t, sp=sp):
                W("g", "sT", kb.cnt["sT"])
                return kb.gdma(fmb[:], sp[:, t * 1024:t * 1024 + 1024])
            return load

        mlp_h_pass(w1stg, b1sb, AF.Relu, mk_loader(x1afm_sp), ff1_sp, 32)
        kb.barrier()
        if "d_ff1" in dbg:
            kb.gdma(dbg["d_ff1"], ff1_sp)
        if "d_xaf" in dbg:
            kb.gdma(dbg["d_xaf"], x1afm_sp)
        kb.barrier()
        if stop_after == "K2":
            kb.run(block)
            return nc
        tw2 = kb.gdma(w2stg[:].rearrange("p (c f) -> p c f", c=32),
                      I["w2T"].rearrange("(c p) f -> p c f", p=128))
        W("t", "sg", tw2)
        evK2 = {"A": None, "B": None}
        s_end3 = [kb.cnt["sS"]]
        ld_g = [0]
        xr3 = [0]
        for t in range(16):
            W("g", "sV", ld_g[0])
            tldf = kb.gdma(h1T[:], ff1_sp[:, t * 4096:t * 4096 + 4096])
            W("t", "sg", tldf)
            ps, pk = (PA, "A") if t % 2 == 0 else (PB, "B")
            if evK2[pk]:
                W("t", evK2[pk][0], evK2[pk][1])
            tmm = None
            for hc in range(32):
                for sb in range(2):
                    tmm = kb.t(lambda e, hc=hc, sb=sb, ps=ps: e.matmul(
                        ps[:, sb * 512:sb * 512 + 512],
                        h1T[:, hc * 128:hc * 128 + 128],
                        w2stg[:, hc * 1024 + sb * 512:hc * 1024 + sb * 512 + 512],
                        start=(hc == 0), stop=(hc == 31)), inc=(hc == 31 and sb == 1))
            ld_g[0] = kb.cnt["sV"] + 1
            W("y", "sV", xr3[0])
            tldx = kb.ydma(xtile[:, 0:D], x1a_sp[:, t * D:t * D + D])
            W("v", "sT", tmm)
            W("v", "sq", tldx)
            kb.v(lambda e, ps=ps: e.tensor_add(tmpf[:, 0:D], ps[:, 0:1024], b2b[:]))
            tadd = kb.v(lambda e: e.tensor_add(tmpf[:, 0:D], tmpf[:, 0:D], xtile[:, 0:D]), inc=True)
            evK2[pk] = ("sV", tadd)
            xr3[0] = tadd
            W("v", "sS", s_end3[0])
            kb.vW(lambda e: e.reduce_sum(stat[:, 0:1], tmpf[:, 0:D], AX))
            kb.vW(lambda e: e.tensor_scalar_mul(stat[:, 0:1], stat[:, 0:1], 1.0 / D))
            tsub = kb.v(lambda e: e.tensor_scalar(tmpf[:, 0:D], tmpf[:, 0:D], stat[:, 0:1], None, ALU.subtract), inc=True)
            W("s", "sV", tsub)
            tsq = kb.s(lambda e: e.activation(tmpf[:, D:2 * D], tmpf[:, 0:D], AF.Square), inc=True)
            W("v", "sS", tsq)
            kb.vW(lambda e: e.reduce_sum(stat[:, 1:2], tmpf[:, D:2 * D], AX))
            tvar = kb.v(lambda e: e.tensor_scalar(stat[:, 1:2], stat[:, 1:2], 1.0 / D, EPS, ALU.mult, ALU.add), inc=True)
            W("s", "sV", tvar)
            kb.sW(lambda e: e.activation(stat[:, 2:3], stat[:, 1:2], AF.Ln))
            trs = kb.s(lambda e: e.activation(stat[:, 3:4], stat[:, 2:3], AF.Exp, scale=-0.5), inc=True)
            s_end3[0] = trs
            W("v", "sS", trs)
            kb.v(lambda e: e.tensor_scalar(tmpf[:, 0:D], tmpf[:, 0:D], stat[:, 3:4], None, ALU.mult))
            kb.v(lambda e: e.tensor_mul(tmpf[:, 0:D], tmpf[:, 0:D], g3b[:]))
            kb.v(lambda e: e.tensor_add(tmpf[:, 0:D], tmpf[:, 0:D], e3b[:]))
            kb.v(lambda e, t=t: e.tensor_copy(s1_tm[:, t * D:t * D + D], tmpf[:, 0:D]), inc=True)
        kb.barrier()
        if "d_s1" in dbg:
            kb.gdma(dbg["d_s1"], s1_tm)
        kb.barrier()

        if stop_after == "K":
            kb.run(block)
            return nc
        zt_t = I["zt"].rearrange("(a p) d -> a p d", p=128)
        zl_g = [0]
        for zt_i in range(128):
            W("y", "sV", zl_g[0])
            tld = kb.ydma(ztile[:, 0:8], zt_t[zt_i])
            W("v", "sq", tld)
            W("s", "sq", tld)
            W("s", "sV", kb.cnt["sV"])
            tsg = kb.s(lambda e: e.activation(stat[:, 0:1] if False else tmpf[:, 0:8], ztile[:, 0:8], AF.Sigmoid), inc=True)
            W("v", "sS", tsg)
            kb.vW(lambda e: e.tensor_add(tmpf[:, 0:8], tmpf[:, 0:8], ebb[:]))
            kb.vW(lambda e: e.tensor_scalar_max(tmpf[:, 0:8], tmpf[:, 0:8], 1e-12))
            kb.vW(lambda e: e.reduce_max(stat[:, 0:1], tmpf[:, 0:8], AX))
            kb.vW(lambda e: e.tensor_scalar(tmpf[:, 8:16], tmpf[:, 0:8], stat[:, 0:1], None, ALU.is_ge))
            kb.vW(lambda e: e.scalar_tensor_tensor(tmpf[:, 16:24], tmpf[:, 8:16], -1e30, tmpf[:, 0:8], ALU.mult, ALU.add))
            kb.vW(lambda e: e.reduce_max(stat[:, 1:2], tmpf[:, 16:24], AX))
            kb.vW(lambda e: e.tensor_scalar(tmpf[:, 24:32], tmpf[:, 16:24], stat[:, 1:2], None, ALU.is_ge))
            kb.vW(lambda e: e.tensor_add(stat[:, 2:3], stat[:, 0:1], stat[:, 1:2]))
            kb.vW(lambda e: e.reciprocal(stat[:, 3:4], stat[:, 2:3]))
            kb.vW(lambda e: e.tensor_scalar(stat[:, 4:5], stat[:, 0:1], stat[:, 3:4], None, ALU.mult))
            kb.vW(lambda e: e.tensor_scalar(stat[:, 5:6], stat[:, 1:2], stat[:, 3:4], None, ALU.mult))
            kb.vW(lambda e: e.tensor_scalar(tmpf[:, 8:16], tmpf[:, 8:16], stat[:, 4:5], None, ALU.mult))
            kb.vW(lambda e: e.scalar_tensor_tensor(tmpf[:, 24:32], tmpf[:, 24:32], 1.0, tmpf[:, 8:16], ALU.bypass, ALU.mult) if False else
                  e.tensor_scalar(tmpf[:, 24:32], tmpf[:, 24:32], stat[:, 5:6], None, ALU.mult))
            zl_g[0] = kb.vW(lambda e, zt_i=zt_i: e.tensor_add(wmat[:, zt_i * 8:zt_i * 8 + 8], tmpf[:, 8:16], tmpf[:, 24:32]))
        kb.barrier()
        if "d_wm" in dbg:
            kb.gdma(dbg["d_wm"], wmat[:])
        kb.vW(lambda e: e.memset(tau[:, 0:1], 0.0))
        kb.vW(lambda e: e.memset(tau[:, 1:2], 1.0))
        for it in range(NITER):
            kb.vW(lambda e: e.tensor_add(tau[:, 2:3], tau[:, 0:1], tau[:, 1:2]))
            tmid = kb.vW(lambda e: e.tensor_scalar_mul(tau[:, 2:3], tau[:, 2:3], 0.5))
            W("t", "sV", tmid)
            tt = kb.t(lambda e: e.matmul(PB[0:1, 512:520], tau[0:8, 2:3], idf32[0:8, 0:8], start=True, stop=True), inc=True)
            W("v", "sT", tt)
            kb.vW(lambda e: e.tensor_copy(midt[0:1, 0:8], PB[0:1, 512:520]))
            for db in range(7):
                w_ = 8 << db
                kb.vW(lambda e, w_=w_: e.tensor_copy(midt[0:1, w_:2 * w_], midt[0:1, 0:w_]))
            tmr = kb.cnt["sV"]
            W("t", "sV", tmr)
            tbc = None
            for s_ in (0, 512):
                tbc = kb.t(lambda e, s_=s_: e.matmul(PB[:, s_:s_ + 512], onesf[:], midt[0:1, s_:s_ + 512], start=True, stop=True), inc=(s_ == 512))
            W("v", "sT", tbc)
            kb.vW(lambda e: e.tensor_copy(tmpf[:, 0:1024], PB[:, 0:1024]))
            kb.vW(lambda e: e.tensor_tensor(wkept[:], wmat[:], tmpf[:, 0:1024], ALU.is_ge))
            kb.vW(lambda e: e.tensor_reduce(
                stat[:, 0:8], wkept.rearrange("p (t x) -> p x t", x=8), AX, ALU.add))
            tcn = kb.cnt["sV"]
            W("t", "sV", tcn)
            tct = kb.t(lambda e: e.matmul(PB[0:8, 520:521], stat[:, 0:8], onef[:], start=True, stop=True), inc=True)
            W("v", "sT", tct)
            kb.vW(lambda e: e.tensor_copy(tau[:, 3:4], PB[0:8, 520:521]))
            kb.vW(lambda e: e.tensor_scalar(tau[:, 4:5], tau[:, 3:4], CAP, None, ALU.is_ge))
            kb.vW(lambda e: e.tensor_sub(tau[:, 5:6], tau[:, 2:3], tau[:, 0:1]))
            kb.vW(lambda e: e.scalar_tensor_tensor(tau[:, 5:6], tau[:, 5:6], 1.0, tau[:, 4:5], ALU.mult, ALU.mult))
            kb.vW(lambda e: e.tensor_add(tau[:, 0:1], tau[:, 0:1], tau[:, 5:6]))
            kb.vW(lambda e: e.tensor_sub(tau[:, 6:7], tau[:, 2:3], tau[:, 1:2]))
            kb.vW(lambda e: e.tensor_scalar(tau[:, 7:8], tau[:, 4:5], -1.0, 1.0, ALU.mult, ALU.add))
            kb.vW(lambda e: e.scalar_tensor_tensor(tau[:, 6:7], tau[:, 6:7], 1.0, tau[:, 7:8], ALU.mult, ALU.mult))
            kb.vW(lambda e: e.tensor_add(tau[:, 1:2], tau[:, 1:2], tau[:, 6:7]))
        ttf = kb.cnt["sV"]
        W("t", "sV", ttf)
        tt = kb.t(lambda e: e.matmul(PB[0:1, 512:520], tau[0:8, 0:1], idf32[0:8, 0:8], start=True, stop=True), inc=True)
        W("v", "sT", tt)
        kb.vW(lambda e: e.tensor_copy(midt[0:1, 0:8], PB[0:1, 512:520]))
        for db in range(7):
            w_ = 8 << db
            kb.vW(lambda e, w_=w_: e.tensor_copy(midt[0:1, w_:2 * w_], midt[0:1, 0:w_]))
        tmr = kb.cnt["sV"]
        W("t", "sV", tmr)
        tbc = None
        for s_ in (0, 512):
            tbc = kb.t(lambda e, s_=s_: e.matmul(PB[:, s_:s_ + 512], onesf[:], midt[0:1, s_:s_ + 512], start=True, stop=True), inc=(s_ == 512))
        W("v", "sT", tbc)
        kb.vW(lambda e: e.tensor_copy(tmpf[:, 0:1024], PB[:, 0:1024]))
        kb.vW(lambda e: e.tensor_tensor(wkept[:], wmat[:], tmpf[:, 0:1024], ALU.is_ge))
        kb.vW(lambda e: e.tensor_mul(wkept[:], wkept[:], wmat[:]))
        kb.barrier()
        if "d_tau" in dbg:
            kb.gdma(dbg["d_tau"], tau[:])
        kb.barrier()

        if stop_after == "Lcap":
            kb.run(block)
            return nc
        f1bs = Al("f1bs", [128, 8 * 16], F32)
        kb.ydma(f1bs[:].rearrange("p (a c) -> p a c", a=8),
                I["f1b"].rearrange("a p c -> p a c"))
        f2b8 = Al("f2b8", [8, 1024], BF16)
        kb.gdma(f2b8[:], I["f2b"].rearrange("a x d -> (a x) d"))
        kb.barrier()
        wk8T = kpq[0:8, 0:128]
        yib = [0, 0]
        for t in range(16):
            W("t", "sV", yib[0])
            tt = kb.t(lambda e, t=t: e.matmul(PA[0:8, 1024:1152], wkept[:, t * 8:t * 8 + 8],
                                              idf32[:], start=True, stop=True), inc=True)
            W("v", "sT", tt)
            tv = kb.v(lambda e: e.tensor_copy(wk8T, PA[0:8, 1024:1152]), inc=True)
            W("t", "sV", tv)
            tmm = None
            for sb in range(2):
                tmm = kb.t(lambda e, sb=sb: e.matmul(PB[:, sb * 512:sb * 512 + 512],
                                                     wk8T, f2b8[:, sb * 512:sb * 512 + 512],
                                                     start=True, stop=True), inc=(sb == 1))
            W("v", "sT", tmm)
            W("v", "sg", yib[1])
            tv2 = kb.v(lambda e: e.tensor_copy(yact[:, 0:D], PB[:, 0:1024]), inc=True)
            yib[0] = tv2
            W("g", "sV", tv2)
            yib[1] = kb.gdma(y_sp[:, t * D:t * D + D], yact[:, 0:D])
        kb.barrier()
        eg = [0]
        for e_ in range(8):
            W("g", "sT", eg[0])
            tw1e = kb.gdma(fc1stg[:].rearrange("p (c f) -> p c f", c=8),
                           I["fc1T"][e_].rearrange("(c p) f -> p c f", p=128))
            tw2e = kb.gdma(fc2stg[:].rearrange("p (c f) -> p c f", c=16),
                           I["fc2T"][e_].rearrange("(c p) f -> p c f", p=128))
            W("t", "sg", tw2e)
            slot_g = [0, 0, 0, 0]
            y_g = [0]
            ESLOTS = [PA[:, 0:128], PA[:, 512:640], PA[:, 1024:1152], PB[:, 1024:1152]]
            for t in range(16):
                for ht in range(16):
                    sl = ht % 4
                    W("t", "sS", slot_g[sl])
                    tmm = None
                    for ec in range(8):
                        tmm = kb.t(lambda e, ht=ht, ec=ec, sl=sl, t=t, ESLOTS=ESLOTS: e.matmul(
                            ESLOTS[sl],
                            fc1stg[:, ec * 2048 + ht * 128:ec * 2048 + ht * 128 + 128],
                            xn1_own[:, ec * 2048 + t * 128:ec * 2048 + t * 128 + 128],
                            start=(ec == 0), stop=(ec == 7)), inc=(ec == 7))
                    W("s", "sT", tmm)
                    tse = kb.s(lambda e, ht=ht, sl=sl, e_=e_, ESLOTS=ESLOTS: e.activation(
                        h1T[:, ht * 128:ht * 128 + 128],
                        ESLOTS[sl], AF.Gelu,
                        bias=f1bs[:, e_ * 16 + ht:e_ * 16 + ht + 1]), inc=True)
                    slot_g[sl] = tse
                if "d_h1" in dbg and e_ == 1 and t == 0:
                    W("g", "sS", max(slot_g))
                    kb.gdma(dbg["d_h1"], h1T[:, 0:2048])
                    kb.barrier()
                W("t", "sS", slot_g[0]); W("t", "sS", slot_g[1])
                W("t", "sS", slot_g[2]); W("t", "sS", slot_g[3])
                tmm = None
                for hc in range(16):
                    for sb in range(2):
                        tmm = kb.t(lambda e, hc=hc, sb=sb: e.matmul(
                            PB[:, sb * 512:sb * 512 + 512],
                            h1T[:, hc * 128:hc * 128 + 128],
                            fc2stg[:, hc * 1024 + sb * 512:hc * 1024 + sb * 512 + 512],
                            start=(hc == 0), stop=(hc == 15)), inc=(hc == 15 and sb == 1))
                W("v", "sT", tmm)
                W("v", "sg", y_g[0])
                tya = kb.v(lambda e, e_=e_, t=t: e.tensor_scalar(
                    yact[:, 0:D], PB[:, 0:1024], wkept[:, t * 8 + e_:t * 8 + e_ + 1], None, ALU.mult), inc=True)
                W("g", "sV", tya)
                ty = kb._emit("g", lambda e, t=t: e.dma_start(
                    out=y_sp[:, t * D:t * D + D], in_=yact[:, 0:D],
                    accum_op=ALU.add), "sg", 16)
                y_g[0] = ty
                W("t", "sV", tya)
            eg[0] = kb.cnt["sT"]
        kb.barrier()
        if "d_y" in dbg:
            kb.gdma(dbg["d_y"], y_sp)
        kb.barrier()

        if stop_after == "L":
            kb.run(block)
            return nc
        tw1 = kb.gdma(sh1stg[:].rearrange("p (c f) -> p c f", c=8),
                      I["sh1T"].rearrange("(c p) f -> p c f", p=128))
        kb.barrier()

        def own_loader(t):
            def f(e, t=t):
                return e.tensor_copy(fmb[:], xn1_own[:, 0:1024])
            W("v", "sT", kb.cnt["sT"])
            return kb.v(lambda e, t=t: e.tensor_copy(
                fmb[:].rearrange("p (c n) -> p c n", c=8),
                xn1_own.rearrange("p (c n) -> p c n", c=8)[:, :, t * 128:t * 128 + 128]), inc=True)

        def mk_own_loader():
            def load(t):
                tk = own_loader(t)
                W("g", "sV", tk)
                return kb.cnt["sg"]
            return load

        def mlp_h_pass_own(wstg_v, bias_sb, act, spill, nht):
            slot_g = [0, 0, 0, 0]
            sp_g = [0]
            for t in range(16):
                W("v", "sT", kb.cnt["sT"])
                tk = kb.v(lambda e, t=t: e.tensor_copy(
                    fmb[:].rearrange("p (c n) -> p c n", c=8),
                    xn1_own.rearrange("p (c n) -> p c n", c=8)[:, :, t * 128:t * 128 + 128]), inc=True)
                W("t", "sV", tk)
                for ht in range(nht):
                    sl = ht % 4
                    W("t", "sS", slot_g[sl])
                    tmm = None
                    for ec in range(8):
                        tmm = kb.t(lambda e, ht=ht, ec=ec, sl=sl, wstg_v=wstg_v, nht=nht: e.matmul(
                            HSLOTS[sl],
                            wstg_v[:, ec * (nht * 128) + ht * 128:ec * (nht * 128) + ht * 128 + 128],
                            fmb[:, ec * 128:ec * 128 + 128],
                            start=(ec == 0), stop=(ec == 7)), inc=(ec == 7))
                    W("s", "sT", tmm)
                    if ht % 16 == 0:
                        W("s", "sg", sp_g[0])
                    tse = kb.s(lambda e, ht=ht, sl=sl, bias_sb=bias_sb, act=act: e.activation(
                        ff1T[:, (ht % 16) * 128:(ht % 16) * 128 + 128],
                        HSLOTS[sl], act,
                        bias=bias_sb[:, ht:ht + 1]), inc=True)
                    slot_g[sl] = tse
                    if ht % 16 == 15:
                        W("g", "sS", tse)
                        sp_g[0] = kb.gdma(spill[:, t * (nht * 128) + (ht // 16) * 2048:
                                                t * (nht * 128) + (ht // 16) * 2048 + 2048], ff1T[:])

        mlp_h_pass_own(sh1stg, s1bsb, AF.Gelu, sh1_sp, 32)
        kb.barrier()
        tw2 = kb.gdma(sh2stg[:].rearrange("p (c f) -> p c f", c=32),
                      I["sh2T"].rearrange("(c p) f -> p c f", p=128))
        W("t", "sg", tw2)
        evM = {"A": None, "B": None}
        ld_g = [0]
        og = [0]
        for t in range(16):
            W("g", "sV", ld_g[0])
            tldf = kb.gdma(h1T[:], sh1_sp[:, t * 4096:t * 4096 + 4096])
            W("t", "sg", tldf)
            ps, pk = (PA, "A") if t % 2 == 0 else (PB, "B")
            if evM[pk]:
                W("t", evM[pk][0], evM[pk][1])
            tmm = None
            for hc in range(32):
                for sb in range(2):
                    tmm = kb.t(lambda e, hc=hc, sb=sb, ps=ps: e.matmul(
                        ps[:, sb * 512:sb * 512 + 512],
                        h1T[:, hc * 128:hc * 128 + 128],
                        sh2stg[:, hc * 1024 + sb * 512:hc * 1024 + sb * 512 + 512],
                        start=(hc == 0), stop=(hc == 31)), inc=(hc == 31 and sb == 1))
            tldy = kb.ydma(xtile[:, 0:D], y_sp[:, t * D:t * D + D])
            tldx = kb.ydma(xtile[:, D:2 * D], xo_t[t])
            W("v", "sT", tmm)
            W("v", "sq", tldx)
            kb.v(lambda e, ps=ps: e.tensor_add(tmpf[:, 0:D], ps[:, 0:1024], s2bb[:]))
            tsh = kb.v(lambda e: e.tensor_add(tmpf[:, 0:D], tmpf[:, 0:D], xtile[:, 0:D]), inc=True)
            evM[pk] = ("sV", tsh)
            ld_g[0] = tsh
            W("s", "sV", tsh)
            tg1 = kb.s(lambda e: e.activation(tmpf[:, D:2 * D], tmpf[:, 0:D], AF.Silu), inc=True)
            W("v", "sS", tg1)
            kb.v(lambda e, t=t: e.tensor_mul(tmpf[:, D:2 * D], tmpf[:, D:2 * D], s1_tm[:, t * D:t * D + D]))
            tfin = kb.v(lambda e: e.tensor_add(tmpf[:, 0:D], tmpf[:, D:2 * D], xtile[:, D:2 * D]), inc=True)
            W("y", "sV", tfin)
            W("y", "sq", og[0])
            og[0] = kb.ydma(out_t.rearrange("(a p) d -> a p d", p=128)[t], tmpf[:, 0:D])
            if "d_g1" in dbg:
                pass
        kb.barrier()
        kb.run(block)
    return nc


def host_inputs_l1(x1_full, OFF, GOFF, wq, bq, wk, bk, wv, bv, wo, bo, proj,
                   w1, b1, w2, b2, g2, be2, g3, be3, z_all,
                   eb, fc1, f1b, fc2, f2b, sh1, s1b, sh2, s2b):
    f32 = np.float32
    zrot = np.concatenate([z_all[GOFF:GOFF + NOWN], z_all[:GOFF], z_all[GOFF + NOWN:]], 0)
    return {
        "x1": np.ascontiguousarray(x1_full, f32),
        "x1own": np.ascontiguousarray(x1_full[OFF:OFF + NOWN], f32),
        "wqT": np.ascontiguousarray(wq.T, f32),
        "wkT": np.ascontiguousarray(wk.T, f32),
        "wvT": np.ascontiguousarray(wv.T, f32),
        "woT": np.ascontiguousarray(wo.T, f32),
        "bq": np.ascontiguousarray(bq.reshape(8, 128).T, f32),
        "bk": np.ascontiguousarray(bk.reshape(8, 128).T, f32),
        "bvc": np.ascontiguousarray(bv.reshape(8, 128).T, f32),
        "bo": np.ascontiguousarray(bo.reshape(1, 1024), f32),
        "projP": np.ascontiguousarray(np.concatenate([proj.T, proj.T], 0), f32),
        "w1T": np.ascontiguousarray(w1.T, f32),
        "b1": np.ascontiguousarray(b1.reshape(32, 128).T, f32),
        "w2T": np.ascontiguousarray(w2.T, f32),
        "b2": np.ascontiguousarray(b2.reshape(1, 1024), f32),
        "g2": np.ascontiguousarray(g2.reshape(1, 1024), f32),
        "be2": np.ascontiguousarray(be2.reshape(1, 1024), f32),
        "g3": np.ascontiguousarray(g3.reshape(1, 1024), f32),
        "be3": np.ascontiguousarray(be3.reshape(1, 1024), f32),
        "zt": np.ascontiguousarray(zrot, f32),
        "eb": np.ascontiguousarray(eb.reshape(1, 8), f32),
        "fc1T": np.ascontiguousarray(fc1.transpose(0, 2, 1), f32),
        "f1b": np.ascontiguousarray(f1b.reshape(8, 16, 128).transpose(0, 2, 1), f32),
        "fc2T": np.ascontiguousarray(fc2.transpose(0, 2, 1), f32),
        "f2b": np.ascontiguousarray(f2b.reshape(8, 1, 1024), f32),
        "sh1T": np.ascontiguousarray(sh1.T, f32),
        "s1b": np.ascontiguousarray(s1b.reshape(32, 128).T, f32),
        "sh2T": np.ascontiguousarray(sh2.T, f32),
        "s2b": np.ascontiguousarray(s2b.reshape(1, 1024), f32),
        "ident": np.eye(128, dtype=f32),
    }


from concourse.bass_utils import run_bass_kernel_spmd

_NC_CACHE = {}


def _programs():
    if "l0" not in _NC_CACHE:
        _NC_CACHE["l0"] = build_l0()
        _NC_CACHE["l1"] = build_l1()
    return _NC_CACHE["l0"], _NC_CACHE["l1"]


def kernel(**inputs):
    f32 = np.float32
    g = {k: np.asarray(v, f32) for k, v in inputs.items()}
    X = g["L0_attn_x"]
    nc1, nc2 = _programs()

    in1 = []
    for c in range(8):
        b, off = c // 2, (c % 2) * 2048
        in1.append(host_inputs(
            X[b], off,
            g["L0_attn_Wq"], g["L0_attn_bq"], g["L0_attn_Wk"], g["L0_attn_bk"],
            g["L0_attn_Wv"], g["L0_attn_bv"], g["L0_attn_Wo"], g["L0_attn_bo"],
            g["L0_attn_rel_table"],
            g["L0_ffn_w_gate"], g["L0_ffn_w_up"], g["L0_ffn_w_down"],
            g["L1_ffn_proj_w"], g["L1_ffn_proj_b"]))
    r1 = run_bass_kernel_spmd(nc1, in1, core_ids=list(range(8)))

    x1 = np.empty((4, 4096, 1024), f32)
    z_all = np.empty((16384, 8), f32)
    for c in range(8):
        b, off = c // 2, (c % 2) * 2048
        o = r1.results[c]
        x1[b, off:off + 2048] = o["x1"]
        z_all[b * 4096 + off:b * 4096 + off + 2048] = np.asarray(o["zT"], f32).T

    in2 = []
    for c in range(8):
        b, off = c // 2, (c % 2) * 2048
        goff = b * 4096 + off
        in2.append(host_inputs_l1(
            x1[b], off, goff,
            g["L1_attn_Wq"], g["L1_attn_bq"], g["L1_attn_Wk"], g["L1_attn_bk"],
            g["L1_attn_Wv"], g["L1_attn_bv"], g["L1_attn_Wo"], g["L1_attn_bo"],
            g["L1_attn_proj"], g["L1_attn_W1"], g["L1_attn_b1"],
            g["L1_attn_W2"], g["L1_attn_b2"], g["L1_attn_g2"], g["L1_attn_be2"],
            g["L1_attn_g3"], g["L1_attn_be3"], z_all,
            g["L1_ffn_expert_bias"], g["L1_ffn_fc1_w"], g["L1_ffn_fc1_b"],
            g["L1_ffn_fc2_w"], g["L1_ffn_fc2_b"], g["L1_ffn_sh1_w"], g["L1_ffn_sh1_b"],
            g["L1_ffn_sh2_w"], g["L1_ffn_sh2_b"]))
    r2 = run_bass_kernel_spmd(nc2, in2, core_ids=list(range(8)))

    out = np.empty((4, 4096, 1024), f32)
    for c in range(8):
        b, off = c // 2, (c % 2) * 2048
        out[b, off:off + 2048] = r2.results[c]["out"]
    return out


# version 03
# speedup vs baseline: 1.0529x; 1.0529x over previous
import numpy as np
import concourse.bass as bass
import concourse.mybir as mybir
from concourse.ap import AP

F32 = mybir.dt.float32
BF16 = mybir.dt.bfloat16
I32 = mybir.dt.int32
AF = mybir.ActivationFunctionType
ALU = mybir.AluOpType
AX = mybir.AxisListType.X

NFULL, NOWN, D = 4096, 2048, 1024
EPS = 1e-5
WPAD = 4224


class KB:
    def __init__(self, nc):
        self.nc = nc
        self.st = {k: [] for k in ("t", "s", "v", "y", "g")}
        self.sem = {}
        self.cnt = {}
        for name in ("sq", "sg", "sT", "sS", "sV"):
            self.sem[name] = nc.semaphore(name).__enter__()
            self.cnt[name] = 0

    def _emit(self, stream, fn, sem=None, k=1):
        if sem is not None:
            self.cnt[sem] += k
            h = self.sem[sem]
            def f(e, fn=fn, h=h, k=k):
                fn(e).then_inc(h, k)
        else:
            def f(e, fn=fn):
                fn(e)
        self.st[stream].append(f)
        return self.cnt[sem] if sem else None

    def t(self, fn, inc=False):  return self._emit("t", fn, "sT" if inc else None)
    def vW(self, fn):
        tk = self._emit("v", fn, "sV")
        self.wait("v", "sV", tk)
        return tk
    def sW(self, fn):
        tk = self._emit("s", fn, "sS")
        self.wait("s", "sS", tk)
        return tk
    def s(self, fn, inc=False):  return self._emit("s", fn, "sS" if inc else None)
    def v(self, fn, inc=False):  return self._emit("v", fn, "sV" if inc else None)
    def ydma(self, out, in_):
        return self._emit("y", lambda e, o=out, i=in_: e.dma_start(out=o, in_=i), "sq", 16)
    def gdma(self, out, in_):
        return self._emit("g", lambda e, o=out, i=in_: e.dma_start(out=o, in_=i), "sg", 16)
    def wait(self, stream, sem, val):
        if not val:
            return
        h = self.sem[sem]
        self.st[stream].append(lambda e, h=h, v=val: e.wait_ge(h, v))
    def barrier(self):
        snap = dict(self.cnt)
        for s in ("t", "s", "v", "y", "g"):
            for sem in ("sq", "sg", "sT", "sS", "sV"):
                self.wait(s, sem, snap[sem])
    def run(self, block):
        for name, key in [("tensor", "t"), ("scalar", "s"), ("vector", "v"),
                          ("sync", "y"), ("gpsimd", "g")]:
            lst = self.st[key]
            def mk(lst):
                def body(eng):
                    for f in lst:
                        f(eng)
                return body
            getattr(block, name)(mk(lst))


def build_l0(debug=()):
    nc = bass.Bass("TRN2", target_bir_lowering=False)
    I = {}
    for nm, shp in [("x", [NFULL, D]), ("xown", [NOWN, D]), ("rbig", [128, 8192]),
                    ("wqT", [D, D]), ("wkT", [D, D]), ("wvT", [D, D]), ("woT", [D, D]),
                    ("bq", [128, 8]), ("bk", [128, 8]), ("bv", [1, D]), ("bo", [1, D]),
                    ("wgT", [D, 4096]), ("wuT", [D, 4096]), ("wdT", [4096, D]),
                    ("pwT", [D, 8]), ("pb", [8, 1]), ("ident", [128, 128])]:
        I[nm] = nc.dram_tensor(nm, shp, F32, kind="ExternalInput").ap()
    x1_out = nc.dram_tensor("x1", [NOWN, D], F32, kind="ExternalOutput").ap()
    zT_out = nc.dram_tensor("zT", [8, NOWN], F32, kind="ExternalOutput").ap()
    dbg = {}
    for nm, shp in [("d_xnfm", [128, 8 * NFULL]), ("d_xnown", [128, 8 * NOWN]),
                    ("d_gtm", [128, 16 * D]), ("d_kfm", [128, 8 * NFULL]),
                    ("d_attn", [128, 16 * D]), ("d_xint", [128, 8 * NOWN]),
                    ("d_vtm", [128, 32 * D]), ("d_qfm", [128, 8 * NOWN]),
                    ("d_dnm", [128, 32]), ("d_sx", [128, 48]),
                    ("d_epad", [128, 2 * WPAD]), ("d_ab", [128, 2 * NFULL]),
                    ("d_gs", [128, 16 * 4096]), ("d_gi", [128, 16 * 4096])]:
        if nm in debug:
            dbg[nm] = nc.dram_tensor(nm, shp, F32, kind="ExternalOutput").ap()

    k_sp = nc.dram_tensor("k_sp", [128, 8 * NFULL], BF16, kind="Internal").ap()
    v_sp = nc.dram_tensor("v_sp", [128, 32 * D], BF16, kind="Internal").ap()
    q_sp = nc.dram_tensor("q_sp", [128, 8 * NOWN], BF16, kind="Internal").ap()
    gs_sp = nc.dram_tensor("gs_sp", [128, 16 * 4096], BF16, kind="Internal").ap()
    gi_sp = nc.dram_tensor("gi_sp", [128, 16 * 4096], BF16, kind="Internal").ap()

    Al = lambda *a: nc.alloc_sbuf_tensor(*a)
    XB = Al("XB", [128, 49152], BF16)
    S16 = Al("S16", [128, 8192], BF16)
    S32 = Al("S32", [128, 16384], BF16)
    g_tm = Al("g_tm", [128, 16 * D], BF16)
    MISC = Al("MISC", [128, 5120], F32)
    idn = Al("idn", [128, 128], BF16)
    bq8 = Al("bq8", [128, 8], F32)
    bk_sb = Al("bk_sb", [128, 8], F32)

    pw_sb = Al("pw_sb", [128, 64], BF16)
    pb_sb = Al("pb_sb", [8, 1], F32)
    onesb = Al("onesb", [1, 128], BF16)
    onef = Al("onef", [128, 1], F32)
    sxraw = Al("sxraw", [128, 16], F32)
    sxg = Al("sxg", [128, 16], F32)
    sxu = Al("sxu", [128, 16], F32)
    sgd = Al("sgd", [128, 16], F32)
    swm = Al("swm", [128, 8], F32)
    stat = Al("stat", [128, 8], F32)
    sml = Al("sml", [1, 8], F32)
    dnm = Al("dnm", [128, 32], F32)
    dnr = Al("dnr", [128, 32], F32)
    zst = Al("zst", [8, 128], F32)
    fmb = Al("fmb", [128, 1024], BF16)

    xn_fm = XB[:, 0:32768]
    xn_own = XB[:, 32768:49152]
    wint = XB[:, 0:32768]
    gsil_t = XB[:, 32768:36864]
    up_t = XB[:, 36864:40960]
    guint_tm = XB[:, 40960:45056]
    guint_fm = XB[:, 45056:49152]
    epad = XB[:, 0:2 * WPAD]
    esh = XB[:, 2 * WPAD:2 * WPAD + 2 * NFULL]
    a0 = XB[:, 16640:16640 + 2 * NFULL]
    ab = XB[:, 24832:24832 + 2 * NFULL]
    atb = XB[:, 33024:33024 + 32 * 128]
    bv_bc = XB[:, 37120:38144]
    bo_bc = XB[:, 37120:38144]
    rb = XB[:, 38144:46336]
    wsb = S16[:, 0:8192]
    kpair = S16[:, 0:4096]
    vpair = S16[:, 4096:8192]
    xint_fm = S32[:, 0:16384]
    attn_tm = S32[:, 0:16384]
    xtile = MISC[:, 0:2048]
    tmpf = MISC[:, 2048:4096]
    tmpi = MISC[:, 4096:5120].bitcast(I32)
    ksg = MISC[:, 0:1536].bitcast(BF16)
    wraw = MISC[:, 0:4096]
    tmpi_d = MISC[:, 4096:5120].bitcast(I32)
    qpair = MISC[:, 0:1024].bitcast(BF16)
    srow = MISC[:, 2048:3072]
    xcF = MISC[:, 3072:4096]
    sqF = MISC[:, 4096:5120]

    with (
        nc.psum_tensor("PA", [128, 1536], F32) as PA,
        nc.psum_tensor("PB", [128, 1536], F32) as PB,
        nc.psum_tensor("PT0", [128, 1024], BF16) as PT0,
        nc.psum_tensor("PT1", [128, 1024], BF16) as PT1,
        nc.Block() as block,
    ):
        kb = KB(nc)
        W = kb.wait

        kb.v(lambda e: e.memset(onesb[:], 1.0))
        kb.v(lambda e: e.memset(onef[:], 1.0))
        kb.ydma(bq8[:], I["bq"])
        kb.ydma(bk_sb[:], I["bk"])
        kb.ydma(pb_sb[:], I["pb"])
        kb.gdma(rb[:], I["rbig"])
        kb.gdma(idn[:], I["ident"])
        kb.gdma(pw_sb[:].rearrange("p (c e) -> p c e", c=8),
                I["pwT"].rearrange("(c p) e -> p c e", p=128))
        kb.barrier()
        kb.v(lambda e: e.tensor_scalar_mul(bq8[:], bq8[:], 0.125), inc=True)
        kb.barrier()

        xfull_t = I["x"].rearrange("(a p) d -> a p d", p=128)
        xown_t = I["xown"].rearrange("(a p) d -> a p d", p=128)
        pt_guard = [kb.cnt["sV"], kb.cnt["sV"]]
        xload_guard = [0, 0]
        s_end = [kb.cnt["sS"]]

        def transp8(src_tm_bf16, dst_cview, pti):
            pt = PT0 if pti % 2 == 0 else PT1
            W("t", "sV", pt_guard[pti % 2])
            ttr = None
            for cb in range(8):
                ttr = kb.t(lambda e, cb=cb, pt=pt, s=src_tm_bf16: e.transpose(
                    pt[:, cb * 128:cb * 128 + 128], s[:, cb * 128:cb * 128 + 128], idn[:]), inc=True)
            W("v", "sT", ttr)
            tev = kb.v(lambda e, pt=pt, d=dst_cview: e.tensor_copy(
                d, pt[:].rearrange("p (c n) -> p c n", c=8)), inc=True)
            pt_guard[pti % 2] = tev
            return tev

        pti_c = [0]
        def ln_tile(ti, src, fm_dst_col, own_t):
            buf = ti % 2
            xt = xtile[:, buf * D:(buf + 1) * D]
            W("y", "sV", xload_guard[buf])
            tld = kb.ydma(xt, src)
            W("v", "sq", tld)
            W("v", "sS", s_end[0])
            kb.vW(lambda e, xt=xt: e.reduce_sum(stat[:, 0:1], xt, AX))
            kb.vW(lambda e: e.tensor_scalar_mul(stat[:, 0:1], stat[:, 0:1], 1.0 / D))
            tsub = kb.v(lambda e, xt=xt: e.tensor_scalar(tmpf[:, 0:D], xt, stat[:, 0:1], None, ALU.subtract), inc=True)
            xload_guard[buf] = tsub
            W("s", "sV", tsub)
            tsq = kb.s(lambda e: e.activation(tmpf[:, D:2 * D], tmpf[:, 0:D], AF.Square), inc=True)
            W("v", "sS", tsq)
            kb.vW(lambda e: e.reduce_sum(stat[:, 1:2], tmpf[:, D:2 * D], AX))
            tvar = kb.v(lambda e: e.tensor_scalar(stat[:, 1:2], stat[:, 1:2], 1.0 / D, EPS, ALU.mult, ALU.add), inc=True)
            W("s", "sV", tvar)
            kb.sW(lambda e: e.activation(stat[:, 2:3], stat[:, 1:2], AF.Ln))
            trs = kb.s(lambda e: e.activation(stat[:, 3:4], stat[:, 2:3], AF.Exp, scale=-0.5), inc=True)
            s_end[0] = trs
            W("v", "sS", trs)
            kb.v(lambda e: e.tensor_scalar(tmpf[:, 0:D], tmpf[:, 0:D], stat[:, 3:4], None, ALU.mult))
            tbf = kb.v(lambda e: e.tensor_copy(fmb[:], tmpf[:, 0:D]), inc=True)
            W("t", "sV", tbf)
            if fm_dst_col is not None:
                dst = xn_fm.rearrange("p (c n) -> p c n", c=8)[:, :, fm_dst_col:fm_dst_col + 128]
            else:
                dst = xn_own.rearrange("p (c n) -> p c n", c=8)[:, :, own_t * 128:own_t * 128 + 128]
            transp8(fmb, dst, pti_c[0]); pti_c[0] += 1
            if own_t is not None:
                t = own_t
                kb.vW(lambda e: e.tensor_reduce(stat[:, 4:5], tmpf[:, 0:D], AX, ALU.max, apply_absolute_value=True))
                kb.vW(lambda e: e.tensor_scalar_max(stat[:, 4:5], stat[:, 4:5], 1e-5))
                kb.vW(lambda e, t=t: e.tensor_scalar_mul(sxraw[:, t:t + 1], stat[:, 4:5], 1.0 / 127.0))
                kb.vW(lambda e: e.reciprocal(stat[:, 5:6], stat[:, 4:5]))
                kb.vW(lambda e: e.tensor_scalar_mul(stat[:, 5:6], stat[:, 5:6], 127.0))
                kb.v(lambda e: e.tensor_scalar(tmpf[:, D:2 * D], tmpf[:, 0:D], stat[:, 5:6], None, ALU.mult))
                kb.v(lambda e: e.tensor_copy(tmpi[:], tmpf[:, D:2 * D]))
                tbi = kb.v(lambda e: e.tensor_copy(fmb[:], tmpi[:]), inc=True)
                W("t", "sV", tbi)
                dsti = xint_fm.rearrange("p (c n) -> p c n", c=8)[:, :, t * 128:t * 128 + 128]
                transp8(fmb, dsti, pti_c[0]); pti_c[0] += 1

        for ti in range(32):
            ln_tile(ti, xfull_t[ti], ti * 128, None)
        for t in range(16):
            ln_tile(32 + t, xown_t[t], None, t)
        kb.barrier()
        for nm, src in [("d_xnfm", xn_fm), ("d_xnown", xn_own), ("d_xint", xint_fm)]:
            if nm in dbg:
                kb.gdma(dbg[nm], src)
        kb.barrier()

        ev_guard = {"A": None, "B": None}
        stg_guard = [0, 0]
        gidx = [0]
        wsb_guard = [0]

        def mm_group(ps, emit_mms, evac_fn, spill_dst, eng="v"):
            pkey = "A" if ps is PA else "B"
            sbf = gidx[0] % 2
            if ev_guard[pkey]:
                W("t", ev_guard[pkey][0], ev_guard[pkey][1])
            tmm = emit_mms()
            W(eng, "sT", tmm)
            W(eng, "sq", stg_guard[sbf])
            emit = kb.v if eng == "v" else kb.s
            sem = "sV" if eng == "v" else "sS"
            tse = emit(lambda e, sbf=sbf: evac_fn(e, ksg[:, sbf * 1536:sbf * 1536 + 1536]), inc=True)
            ev_guard[pkey] = (sem, tse)
            W("y", sem, tse)
            tst = kb.ydma(spill_dst, ksg[:, sbf * 1536:sbf * 1536 + spill_dst.shape[-1]])
            stg_guard[sbf] = tst
            gidx[0] += 1

        def proj_fm(wname, src_fm, ncols, spill, bias_sb, scale):
            W("g", "sT", wsb_guard[0])
            tw = kb.gdma(wsb[:].rearrange("p (c f) -> p c f", c=8),
                         I[wname].rearrange("(c p) f -> p c f", p=128))
            W("t", "sg", tw)
            groups = [(0, 1536), (1536, 1536), (3072, 1024)] if ncols == 4096 else [(0, 1536), (1536, 512)]
            for fc in range(8):
                for (nb, nw) in groups:
                    ps = PA if gidx[0] % 2 == 0 else PB
                    def mms(fc=fc, nb=nb, nw=nw, ps=ps, src=src_fm, ncols=ncols):
                        tmm = None
                        for ec in range(8):
                            for sb in range(nw // 512):
                                tmm = kb.t(lambda e, fc=fc, ec=ec, nb=nb, sb=sb, ps=ps, src=src, ncols=ncols: e.matmul(
                                    ps[:, sb * 512:sb * 512 + 512],
                                    wsb[:, ec * D + fc * 128:ec * D + fc * 128 + 128],
                                    src[:, ec * ncols + nb + sb * 512:ec * ncols + nb + sb * 512 + 512],
                                    start=(ec == 0), stop=(ec == 7)), inc=(ec == 7 and sb == nw // 512 - 1))
                        return tmm
                    def ev(e, dst, fc=fc, nw=nw, ps=ps, bias_sb=bias_sb, scale=scale):
                        return e.tensor_scalar(dst[:, 0:nw], ps[:, 0:nw], scale,
                                               bias_sb[:, fc:fc + 1], ALU.mult, ALU.add)
                    mm_group(ps, mms, ev, spill.rearrange("p (c n) -> p c n", c=8)[:, fc, nb:nb + nw])
            wsb_guard[0] = kb.cnt["sT"]

        proj_fm("wkT", xn_fm, NFULL, k_sp, bk_sb, 1.0)
        kb.barrier()
        proj_fm("wqT", xn_own, NOWN, q_sp, bq8, 0.125)
        kb.barrier()
        W("g", "sT", wsb_guard[0])
        tw = kb.gdma(wsb[:].rearrange("p (c f) -> p c f", c=8),
                     I["wvT"].rearrange("(c p) f -> p c f", p=128))
        W("t", "sg", tw)
        for nt in list(range(16, 32)) + list(range(16)):
            ps = PA if gidx[0] % 2 == 0 else PB
            def mms(nt=nt, ps=ps):
                tmm = None
                for ec in range(8):
                    for sb in range(2):
                        tmm = kb.t(lambda e, nt=nt, ec=ec, sb=sb, ps=ps: e.matmul(
                            ps[:, sb * 512:sb * 512 + 512],
                            xn_fm[:, ec * NFULL + nt * 128:ec * NFULL + nt * 128 + 128],
                            wsb[:, ec * D + sb * 512:ec * D + sb * 512 + 512],
                            start=(ec == 0), stop=(ec == 7)), inc=(ec == 7 and sb == 1))
                return tmm
            def ev(e, dst, ps=ps):
                return e.activation(dst[:, 0:1024], ps[:, 0:1024], AF.Copy)
            mm_group(ps, mms, ev, v_sp[:, nt * D:nt * D + D], eng="s")
        kb.barrier()
        for nm, src in [("d_kfm", k_sp), ("d_vtm", v_sp), ("d_qfm", q_sp)]:
            if nm in dbg:
                kb.gdma(dbg[nm], src)
        kb.barrier()

        onefr = Al("onefr", [1, 128], F32)
        idf = Al("idf", [4, 4], F32)
        m3 = Al("m3", [4, 4], F32)
        smr = Al("smr", [1, 8], F32)
        kb.v(lambda e: e.memset(onefr[:], 1.0))
        kb.ydma(idf[:], I["ident"][0:4, 0:4])
        wsrc = {"g": I["wgT"], "u": I["wuT"], "d": I["wdT"]}
        nch = {"g": 8, "u": 8, "d": 32}
        ccol = {"g": 4096, "u": 4096, "d": 1024}
        vprev = [kb.cnt["sV"]]
        for mi, m in enumerate(("g", "u", "d")):
            for ch in range(nch[m]):
                W("y", "sV", vprev[0])
                tld = kb.ydma(wraw[:, 0:ccol[m]],
                              wsrc[m].rearrange("(c p) f -> p c f", p=128)[:, ch, :])
                W("v", "sq", tld)
                if ch == 0:
                    tv = kb.vW(lambda e, mi=mi, cc=ccol[m]: e.tensor_reduce(
                        dnm[:, mi:mi + 1], wraw[:, 0:cc], AX, ALU.add, apply_absolute_value=True))
                else:
                    kb.vW(lambda e, cc=ccol[m]: e.tensor_reduce(
                        stat[:, 7:8], wraw[:, 0:cc], AX, ALU.add, apply_absolute_value=True))
                    tv = kb.vW(lambda e, mi=mi: e.tensor_add(dnm[:, mi:mi + 1], dnm[:, mi:mi + 1], stat[:, 7:8]))
                vprev[0] = tv
        kb.barrier()
        kb.t(lambda e: e.matmul(PA[0:3, 0:1], dnm[:, 0:3], onef[:], start=True, stop=True), inc=True)
        kb.barrier()
        kb.vW(lambda e: e.tensor_copy(m3[0:3, 0:1], PA[0:3, 0:1]))
        kb.vW(lambda e: e.tensor_scalar_mul(m3[0:3, 0:1], m3[0:3, 0:1], 1.0 / (1024.0 * 4096.0)))
        kb.vW(lambda e: e.tensor_scalar_max(m3[0:3, 0:1], m3[0:3, 0:1], 1e-5))
        kb.v(lambda e: e.reciprocal(m3[0:3, 1:2], m3[0:3, 0:1]), inc=True)
        kb.barrier()
        kb.t(lambda e: e.matmul(PB[0:1, 0:3], m3[0:3, 0:1], idf[0:3, 0:3], start=True, stop=True))
        kb.t(lambda e: e.matmul(PB[0:1, 4:7], m3[0:3, 1:2], idf[0:3, 0:3], start=True, stop=True), inc=True)
        kb.barrier()
        kb.v(lambda e: e.tensor_copy(smr[0:1, 0:3], PB[0:1, 0:3]))
        kb.v(lambda e: e.tensor_copy(smr[0:1, 3:6], PB[0:1, 4:7]), inc=True)
        kb.barrier()
        kb.t(lambda e: e.matmul(PA[:, 0:3], onefr[:], smr[0:1, 0:3], start=True, stop=True))
        kb.t(lambda e: e.matmul(PA[:, 4:7], onefr[:], smr[0:1, 3:6], start=True, stop=True), inc=True)
        kb.barrier()
        kb.v(lambda e: e.tensor_copy(swm[:, 0:3], PA[:, 0:3]))
        kb.v(lambda e: e.tensor_copy(swm[:, 3:6], PA[:, 4:7]))
        kb.v(lambda e: e.tensor_scalar(sxg[:], sxraw[:], swm[:, 0:1], None, ALU.mult))
        kb.v(lambda e: e.tensor_scalar(sxu[:], sxraw[:], swm[:, 1:2], None, ALU.mult), inc=True)
        kb.barrier()

        def quantize_resident(m, mi):
            vq = [kb.cnt["sV"]]
            for ch in range(nch[m]):
                cc = ccol[m]
                W("y", "sV", vq[0])
                tld = kb.ydma(wraw[:, 0:cc],
                              wsrc[m].rearrange("(c p) f -> p c f", p=128)[:, ch, :])
                W("v", "sq", tld)
                for qq in range(0, cc, 1024):
                    kb.v(lambda e, qq=qq, mi=mi: e.tensor_scalar(
                        tmpi_d[:, 0:1024], wraw[:, qq:qq + 1024], swm[:, 3 + mi:4 + mi], None, ALU.mult))
                    vq[0] = kb.v(lambda e, qq=qq, cc=cc, ch=ch: e.tensor_scalar(
                        wint[:, ch * cc + qq:ch * cc + qq + 1024], tmpi_d[:, 0:1024], 1, -1, ALU.min, ALU.max), inc=True)
            kb.barrier()

        quantize_resident("g", 0)
        evg = {"A": 0, "B": 0}
        sg_store = [0]
        for t in range(16):
            for gi, (ob, ow) in enumerate([(0, 1536), (1536, 1536), (3072, 1024)]):
                ps, pk = (PA, "A") if (t * 3 + gi) % 2 == 0 else (PB, "B")
                W("t", "sS", evg[pk])
                tmm = None
                for ec in range(8):
                    for sb in range(ow // 512):
                        tmm = kb.t(lambda e, t=t, ec=ec, ob=ob, sb=sb, ps=ps: e.matmul(
                            ps[:, sb * 512:sb * 512 + 512],
                            xint_fm[:, ec * NOWN + t * 128:ec * NOWN + t * 128 + 128],
                            wint[:, ec * 4096 + ob + sb * 512:ec * 4096 + ob + sb * 512 + 512],
                            start=(ec == 0), stop=(ec == 7)), inc=(ec == 7 and sb == ow // 512 - 1))
                W("s", "sT", tmm)
                W("s", "sg", sg_store[0])
                tse = kb.s(lambda e, t=t, ob=ob, ow=ow, ps=ps: e.activation(
                    gsil_t[:, ob:ob + ow], ps[:, 0:ow], AF.Silu, scale=sxg[:, t:t + 1]), inc=True)
                evg[pk] = tse
            W("g", "sS", evg["A"])
            W("g", "sS", evg["B"])
            tst = kb.gdma(gs_sp[:, t * 4096:t * 4096 + 4096], gsil_t[:])
            sg_store[0] = tst
        kb.barrier()
        kb.barrier()
        quantize_resident("u", 1)
        evg = {"A": 0, "B": 0}
        gu_guard = [0]
        ptd_guard = [kb.cnt["sV"], kb.cnt["sV"]]
        for t in range(16):
            W("g", "sV", gu_guard[0])
            tldg = kb.gdma(gsil_t[:], gs_sp[:, t * 4096:t * 4096 + 4096])
            for gi, (ob, ow) in enumerate([(0, 1536), (1536, 1536), (3072, 1024)]):
                ps, pk = (PA, "A") if (t * 3 + gi) % 2 == 0 else (PB, "B")
                W("t", "sV", evg[pk])
                tmm = None
                for ec in range(8):
                    for sb in range(ow // 512):
                        tmm = kb.t(lambda e, t=t, ec=ec, ob=ob, sb=sb, ps=ps: e.matmul(
                            ps[:, sb * 512:sb * 512 + 512],
                            xint_fm[:, ec * NOWN + t * 128:ec * NOWN + t * 128 + 128],
                            wint[:, ec * 4096 + ob + sb * 512:ec * 4096 + ob + sb * 512 + 512],
                            start=(ec == 0), stop=(ec == 7)), inc=(ec == 7 and sb == ow // 512 - 1))
                W("v", "sT", tmm)
                tse = kb.v(lambda e, t=t, ob=ob, ow=ow, ps=ps: e.tensor_scalar(
                    up_t[:, ob:ob + ow], ps[:, 0:ow], sxu[:, t:t + 1], None, ALU.mult), inc=True)
                evg[pk] = tse
            W("v", "sV", max(evg["A"], evg["B"])) if False else None
            W("v", "sg", tldg)
            kb.v(lambda e: e.tensor_mul(up_t[:], gsil_t[:], up_t[:]))
            kb.vW(lambda e: e.tensor_reduce(stat[:, 0:1], up_t[:], AX, ALU.max, apply_absolute_value=True))
            kb.vW(lambda e: e.tensor_scalar_max(stat[:, 0:1], stat[:, 0:1], 1e-5))
            kb.vW(lambda e, t=t: e.tensor_scalar_mul(stat[:, 1:2], stat[:, 0:1], 1.0 / 127.0))
            kb.vW(lambda e, t=t: e.tensor_scalar(sgd[:, t:t + 1], stat[:, 1:2], swm[:, 2:3], None, ALU.mult))
            kb.vW(lambda e: e.reciprocal(stat[:, 2:3], stat[:, 1:2]))
            tgu = None
            for qq in range(0, 4096, 1024):
                kb.v(lambda e, qq=qq: e.tensor_scalar(tmpi_d[:, 0:1024], up_t[:, qq:qq + 1024], stat[:, 2:3], None, ALU.mult))
                tgu = kb.v(lambda e, qq=qq: e.tensor_copy(guint_tm[:, qq:qq + 1024], tmpi_d[:, 0:1024]), inc=True)
            gu_guard[0] = tgu
            tprev = None
            for grp in range(4):
                pt = PT0 if grp % 2 == 0 else PT1
                W("t", "sV", ptd_guard[grp % 2])
                if grp == 0:
                    W("t", "sV", tgu)
                for cb in range(8):
                    tprev = kb.t(lambda e, grp=grp, cb=cb, pt=pt: e.transpose(
                        pt[:, cb * 128:cb * 128 + 128],
                        guint_tm[:, grp * 1024 + cb * 128:grp * 1024 + cb * 128 + 128], idn[:]), inc=True)
                W("v", "sT", tprev)
                tev = kb.v(lambda e, grp=grp, pt=pt: e.tensor_copy(
                    guint_fm[:, grp * 1024:grp * 1024 + 1024], pt[:]), inc=True)
                ptd_guard[grp % 2] = tev
            W("g", "sV", ptd_guard[0])
            W("g", "sV", ptd_guard[1])
            kb.gdma(gi_sp[:, t * 4096:t * 4096 + 4096], guint_fm[:])
        kb.barrier()
        kb.barrier()
        quantize_resident("d", 2)
        evg = {"A": 0, "B": 0}
        dmm_guard = [0]
        for t in range(16):
            W("g", "sT", dmm_guard[0])
            tldg = kb.gdma(guint_fm[:], gi_sp[:, t * 4096:t * 4096 + 4096])
            W("t", "sg", tldg)
            for gi, (ob, ow) in enumerate([(0, 512), (512, 512)]):
                ps, pk = (PA, "A") if (t * 2 + gi) % 2 == 0 else (PB, "B")
                W("t", "sS", evg[pk])
                tmm = None
                for oc in range(32):
                    tmm = kb.t(lambda e, t=t, oc=oc, ob=ob, ps=ps: e.matmul(
                        ps[:, 0:512],
                        guint_fm[:, oc * 128:oc * 128 + 128],
                        wint[:, oc * 1024 + ob:oc * 1024 + ob + 512],
                        start=(oc == 0), stop=(oc == 31)), inc=(oc == 31))
                dmm_guard[0] = tmm
                W("s", "sT", tmm)
                tse = kb.s(lambda e, t=t, ob=ob, ps=ps: e.activation(
                    g_tm[:, t * D + ob:t * D + ob + 512], ps[:, 0:512], AF.Silu,
                    scale=sgd[:, t:t + 1]), inc=True)
                evg[pk] = tse
        kb.barrier()
        if "d_gtm" in dbg:
            kb.gdma(dbg["d_gtm"], g_tm[:])
        if "d_sx" in dbg:
            kb.gdma(dbg["d_sx"][:, 0:16], sxraw[:])
            kb.gdma(dbg["d_sx"][:, 16:32], sxg[:])
            kb.gdma(dbg["d_sx"][:, 32:40], swm[:])
            kb.gdma(dbg["d_sx"][:, 40:48], sgd[:, 0:8])
        if "d_gs" in dbg:
            kb.gdma(dbg["d_gs"], gs_sp)
        if "d_gi" in dbg:
            kb.gdma(dbg["d_gi"], gi_sp)
        kb.barrier()

        kb.gdma(rb[:], I["rbig"])
        kb.gdma(fmb[0:1, :], I["bv"])
        kb.barrier()
        kb.t(lambda e: e.matmul(PA[:, 0:512], onesb[:], fmb[0:1, 0:512], start=True, stop=True))
        _tbc = kb.t(lambda e: e.matmul(PA[:, 512:1024], onesb[:], fmb[0:1, 512:1024], start=True, stop=True), inc=True)
        W("v", "sT", _tbc)
        kb.v(lambda e: e.tensor_copy(bv_bc[:], PA[:, 0:1024]), inc=True)
        kb.barrier()
        pair_guard = [0]
        ptE = [kb.cnt["sV"], kb.cnt["sV"]]
        psE = {"A": 0, "B": 0}
        epad_guard = [0]
        a0_guard = [0]
        ab_guard = [0]
        atb_guard = [0]
        av_guard = [0]
        SEGS_P = [(0, 1536), (1536, 1536), (3072, 1152)]
        SEGS_C = [(0, 1536), (1536, 1536), (3072, 1024)]
        XBROW = 49152
        for p in range(8):
            W("y", "sT", pair_guard[0])
            tk = kb.ydma(kpair[:], k_sp.rearrange("p (c n) -> p c n", c=8)[:, p, :])
            tv_ = kb.ydma(vpair[:].rearrange("p (k c) -> p k c", k=32),
                          v_sp.rearrange("p (k f) -> p k f", k=32)[:, :, p * 128:p * 128 + 128])
            tq_ = kb.ydma(qpair[:], q_sp.rearrange("p (c n) -> p c n", c=8)[:, p, :])
            for qt in range(16):
                q0 = qt * 128
                g0col = 3968 - q0
                if qt == 0:
                    W("t", "sq", tq_)
                W("t", "sV", av_guard[0])
                for (sb_, sw) in SEGS_P:
                    for he, ps, pk in ((0, PA, "A"), (1, PB, "B")):
                        W("t", "sS", psE[pk])
                        nmm = (sw + 511) // 512
                        tmm = None
                        for s_ in range(nmm):
                            w_ = min(512, sw - s_ * 512)
                            tmm = kb.t(lambda e, he=he, q0=q0, g0col=g0col, sb_=sb_, s_=s_, w_=w_, ps=ps: e.matmul(
                                ps[:, s_ * 512:s_ * 512 + w_],
                                qpair[64 * he:64 * he + 64, q0:q0 + 128],
                                rb[64 * he:64 * he + 64, g0col + sb_ + s_ * 512:g0col + sb_ + s_ * 512 + w_],
                                start=True, stop=True), inc=(s_ == nmm - 1))
                        W("s", "sT", tmm)
                        W("s", "sg", epad_guard[0])
                        tse = kb.s(lambda e, he=he, sb_=sb_, sw=sw, ps=ps: e.activation(
                            epad[:, he * WPAD + sb_:he * WPAD + sb_ + sw], ps[:, 0:sw], AF.Exp), inc=True)
                        psE[pk] = tse
                W("g", "sS", kb.cnt["sS"])
                W("g", "sV", a0_guard[0])
                tsh = None
                for he in range(2):
                    src = AP(XB, he * WPAD + 127, [[XBROW - 1, 128], [1, NFULL]])
                    tsh = kb.gdma(esh[:, he * NFULL:he * NFULL + NFULL], src)
                epad_guard[0] = tsh
                for (sb_, sw) in SEGS_C:
                    for he, ps, pk in ((0, PA, "A"), (1, PB, "B")):
                        W("t", "sS", psE[pk])
                        tmm = None
                        for s_ in range(sw // 512):
                            tmm = kb.t(lambda e, he=he, q0=q0, sb_=sb_, s_=s_, ps=ps: e.matmul(
                                ps[:, s_ * 512:s_ * 512 + 512],
                                qpair[64 * he:64 * he + 64, q0:q0 + 128],
                                kpair[64 * he:64 * he + 64, sb_ + s_ * 512:sb_ + s_ * 512 + 512],
                                start=True, stop=True), inc=(s_ == sw // 512 - 1))
                        W("s", "sT", tmm)
                        W("s", "sV", a0_guard[0])
                        tse = kb.s(lambda e, he=he, sb_=sb_, sw=sw, ps=ps: e.activation(
                            a0[:, he * NFULL + sb_:he * NFULL + sb_ + sw], ps[:, 0:sw], AF.Exp), inc=True)
                        psE[pk] = tse
                tCe = kb.cnt["sS"]
                W("v", "sS", tCe)
                W("v", "sg", tsh)
                W("v", "sT", ab_guard[0])
                col = qt
                tvh = None
                for he in range(2):
                    kb.vW(lambda e, he=he, col=col: e.scalar_tensor_tensor(
                        ab[:, he * NFULL:he * NFULL + NFULL], a0[:, he * NFULL:he * NFULL + NFULL],
                        1.0, esh[:, he * NFULL:he * NFULL + NFULL], ALU.mult, ALU.mult,
                        accum_out=dnm[:, he * 16 + col:he * 16 + col + 1]))
                    tvh = kb.vW(lambda e, he=he, col=col: e.reciprocal(
                        dnr[:, he * 16 + col:he * 16 + col + 1], dnm[:, he * 16 + col:he * 16 + col + 1]))
                a0_guard[0] = tvh
                for he in range(2):
                    tprev = None
                    for grp in range(4):
                        pt = PT0 if grp % 2 == 0 else PT1
                        W("t", "sV", ptE[grp % 2])
                        if grp == 0 and he == 0:
                            W("t", "sV", tvh)
                        for cb in range(8):
                            kt = grp * 8 + cb
                            tprev = kb.t(lambda e, he=he, kt=kt, cb=cb, pt=pt: e.transpose(
                                pt[:, cb * 128:cb * 128 + 128],
                                ab[:, he * NFULL + kt * 128:he * NFULL + kt * 128 + 128], idn[:]), inc=True)
                        W("v", "sT", tprev)
                        W("v", "sT", atb_guard[0])
                        tev = kb.v(lambda e, grp=grp, pt=pt: e.tensor_copy(
                            atb[:, grp * 1024:grp * 1024 + 1024], pt[:]), inc=True)
                        ptE[grp % 2] = tev
                    W("t", "sV", ptE[0])
                    W("t", "sV", ptE[1])
                    tav = None
                    for kt in range(32):
                        tav = kb.t(lambda e, he=he, kt=kt: e.matmul(
                            PA[:, 1024 + he * 64:1024 + he * 64 + 64],
                            atb[:, kt * 128:kt * 128 + 128],
                            vpair[:, kt * 128 + he * 64:kt * 128 + he * 64 + 64],
                            start=(kt == 0), stop=(kt == 31)), inc=(kt == 31))
                    atb_guard[0] = tav
                    W("v", "sT", tav)
                    kb.vW(lambda e, he=he, col=col, qt=qt, p=p: e.tensor_scalar(
                        attn_tm[:].rearrange("p (a n) -> p a n", a=16)[:, qt, p * 128 + he * 64:p * 128 + he * 64 + 64],
                        PA[:, 1024 + he * 64:1024 + he * 64 + 64],
                        dnr[:, he * 16 + col:he * 16 + col + 1], None, ALU.mult))
                tvd = kb.v(lambda e, qt=qt, p=p: e.tensor_tensor(
                    attn_tm[:].rearrange("p (a n) -> p a n", a=16)[:, qt, p * 128:p * 128 + 128],
                    attn_tm[:].rearrange("p (a n) -> p a n", a=16)[:, qt, p * 128:p * 128 + 128],
                    bv_bc[:, p * 128:p * 128 + 128], ALU.add), inc=True)
                av_guard[0] = tvd
                ab_guard[0] = kb.cnt["sT"]
            pair_guard[0] = kb.cnt["sT"]
        kb.barrier()
        for nm, src in [("d_attn", attn_tm), ("d_dnm", dnm[:]), ("d_epad", epad), ("d_ab", ab)]:
            if nm in dbg:
                kb.gdma(dbg[nm], src)
        kb.barrier()

        kb.gdma(fmb[0:1, :], I["bo"])
        kb.barrier()
        kb.t(lambda e: e.matmul(PA[:, 0:512], onesb[:], fmb[0:1, 0:512], start=True, stop=True))
        _tbc = kb.t(lambda e: e.matmul(PA[:, 512:1024], onesb[:], fmb[0:1, 512:1024], start=True, stop=True), inc=True)
        W("v", "sT", _tbc)
        kb.v(lambda e: e.tensor_copy(bo_bc[:], PA[:, 0:1024]), inc=True)
        kb.barrier()
        tw = kb.gdma(wsb[:].rearrange("p (c f) -> p c f", c=8),
                     I["woT"].rearrange("(c p) f -> p c f", p=128))
        W("t", "sg", tw)
        ptF = [kb.cnt["sV"], kb.cnt["sV"]]
        evF = {"A": 0, "B": 0}
        s_endF = [kb.cnt["sS"]]
        fmb_guard = [0]
        zst_guard = [0]
        xld_guard = [0]
        x1st_guard = [0]
        for t in range(16):
            src_tm = attn_tm[:].rearrange("p (a n) -> p a n", a=16)[:, t, :]
            pt = PT0 if t % 2 == 0 else PT1
            W("t", "sV", ptF[t % 2])
            tprev = None
            for cb in range(8):
                tprev = kb.t(lambda e, cb=cb, pt=pt, src_tm=src_tm: e.transpose(
                    pt[:, cb * 128:cb * 128 + 128], src_tm[:, cb * 128:cb * 128 + 128], idn[:]), inc=True)
            W("v", "sT", tprev)
            W("v", "sT", fmb_guard[0])
            tev = kb.v(lambda e, pt=pt: e.tensor_copy(fmb[:], pt[:]), inc=True)
            ptF[t % 2] = tev
            ps, pk = (PA, "A") if t % 2 == 0 else (PB, "B")
            W("t", "sV", tev)
            W("t", "sV", evF[pk])
            tmm = None
            for ec in range(8):
                for sb in range(2):
                    tmm = kb.t(lambda e, ec=ec, sb=sb, ps=ps: e.matmul(
                        ps[:, sb * 512:sb * 512 + 512],
                        fmb[:, ec * 128:ec * 128 + 128],
                        wsb[:, ec * D + sb * 512:ec * D + sb * 512 + 512],
                        start=(ec == 0), stop=(ec == 7)), inc=(ec == 7 and sb == 1))
            fmb_guard[0] = tmm
            W("y", "sV", xld_guard[0])
            tldx = kb.ydma(xtile[:, 0:D], xown_t[t])
            W("v", "sT", tmm)
            W("v", "sq", tldx)
            W("v", "sq", x1st_guard[0])
            kb.v(lambda e, ps=ps: e.tensor_add(srow[:], ps[:, 0:1024], bo_bc[:]))
            kb.v(lambda e, t=t: e.tensor_mul(srow[:], srow[:], g_tm[:, t * D:t * D + D]))
            tx1 = kb.v(lambda e: e.tensor_add(xtile[:, D:2 * D], xtile[:, 0:D], srow[:]), inc=True)
            evF[pk] = tx1
            W("y", "sV", tx1)
            tx1s = kb.ydma(x1_out.rearrange("(a p) d -> a p d", p=128)[t], xtile[:, D:2 * D])
            x1st_guard[0] = tx1s
            x1t = xtile[:, D:2 * D]
            W("v", "sS", s_endF[0])
            kb.vW(lambda e, x1t=x1t: e.reduce_sum(stat[:, 0:1], x1t, AX))
            kb.vW(lambda e: e.tensor_scalar_mul(stat[:, 0:1], stat[:, 0:1], 1.0 / D))
            tsub = kb.v(lambda e, x1t=x1t: e.tensor_scalar(xcF, x1t, stat[:, 0:1], None, ALU.subtract), inc=True)
            W("s", "sV", tsub)
            tsq = kb.s(lambda e: e.activation(sqF, xcF, AF.Square), inc=True)
            W("v", "sS", tsq)
            kb.vW(lambda e: e.reduce_sum(stat[:, 1:2], sqF, AX))
            tvar = kb.v(lambda e: e.tensor_scalar(stat[:, 1:2], stat[:, 1:2], 1.0 / D, EPS, ALU.mult, ALU.add), inc=True)
            W("s", "sV", tvar)
            kb.sW(lambda e: e.activation(stat[:, 2:3], stat[:, 1:2], AF.Ln))
            trs = kb.s(lambda e: e.activation(stat[:, 3:4], stat[:, 2:3], AF.Exp, scale=-0.5), inc=True)
            s_endF[0] = trs
            W("v", "sS", trs)
            kb.v(lambda e: e.tensor_scalar(xcF, xcF, stat[:, 3:4], None, ALU.mult))
            W("v", "sT", fmb_guard[0])
            tbf = kb.v(lambda e: e.tensor_copy(fmb[:], xcF), inc=True)
            xld_guard[0] = tbf
            pt2 = PT0 if (t + 1) % 2 == 0 else PT1
            W("t", "sV", tbf)
            W("t", "sV", ptF[(t + 1) % 2])
            tprev = None
            for cb in range(8):
                tprev = kb.t(lambda e, cb=cb, pt2=pt2: e.transpose(
                    pt2[:, cb * 128:cb * 128 + 128], fmb[:, cb * 128:cb * 128 + 128], idn[:]), inc=True)
            W("v", "sT", tprev)
            W("v", "sT", fmb_guard[0])
            tev2 = kb.v(lambda e, pt2=pt2: e.tensor_copy(fmb[:], pt2[:]), inc=True)
            ptF[(t + 1) % 2] = tev2
            W("t", "sV", tev2)
            tzm = None
            for ec in range(8):
                tzm = kb.t(lambda e, ec=ec: e.matmul(
                    PB[0:8, 1024:1152],
                    pw_sb[:, ec * 8:ec * 8 + 8],
                    fmb[:, ec * 128:ec * 128 + 128],
                    start=(ec == 0), stop=(ec == 7)), inc=(ec == 7))
            fmb_guard[0] = tzm
            W("v", "sT", tzm)
            W("v", "sq", zst_guard[0])
            tzs = kb.v(lambda e, t=t: e.tensor_scalar(
                zst[:], PB[0:8, 1024:1152], pb_sb[:], None, ALU.add), inc=True)
            W("t", "sV", tzs)
            W("y", "sV", tzs)
            zst_guard[0] = kb.ydma(zT_out[:, t * 128:t * 128 + 128], zst[:])
        kb.barrier()
        kb.run(block)
    return nc


def host_inputs(x_full, OFF, wq, bq, wk, bk, wv, bv, wo, bo, rel_table,
                w_gate, w_up, w_down, proj_w, proj_b):
    f32 = np.float32
    g = np.arange(8192)
    lag = np.clip(g - 4095 - OFF, -1023, 1023)
    ridx = 1023 - lag
    rbig = np.ascontiguousarray(
        np.concatenate([rel_table[ridx, :].T, rel_table[ridx, :].T], axis=0)).astype(f32)
    return {
        "x": np.ascontiguousarray(x_full, f32),
        "xown": np.ascontiguousarray(x_full[OFF:OFF + NOWN], f32),
        "rbig": rbig,
        "wqT": np.ascontiguousarray(wq.T, f32),
        "wkT": np.ascontiguousarray(wk.T, f32),
        "wvT": np.ascontiguousarray(wv.T, f32),
        "woT": np.ascontiguousarray(wo.T, f32),
        "bq": np.ascontiguousarray(bq.reshape(8, 128).T, f32),
        "bk": np.ascontiguousarray(bk.reshape(8, 128).T, f32),
        "bv": np.ascontiguousarray(bv.reshape(1, 1024), f32),
        "bo": np.ascontiguousarray(bo.reshape(1, 1024), f32),
        "wgT": np.ascontiguousarray(w_gate.T, f32),
        "wuT": np.ascontiguousarray(w_up.T, f32),
        "wdT": np.ascontiguousarray(w_down.T, f32),
        "pwT": np.ascontiguousarray(proj_w.T, f32),
        "pb": np.ascontiguousarray(proj_b.reshape(8, 1), f32),
        "ident": np.eye(128, dtype=f32),
    }


"""Launch 2 builder: layer 1 of mk_recurrent_d2 (Performer attn + relu-FFN + MoE).

Host per-core inputs (f32):
  x1 (4096,1024) full batch; x1own (2048,1024);
  wqT,wkT,wvT,woT (1024,1024)=W.T; bqn,bkn (128,8)=(b*norm).. NO: bq,bk raw (128,8);
  bv (128,8) f-major col chunks; bo (1,1024); projP (128,256) = proj.T stacked x2;
  w1T (1024,4096); b1 (128,32); w2T (4096,1024); b2 (1,1024);
  g2,be2,g3,be3 (1,1024); zt (16384,8) router logits; eb (1,8); cap (scalar via const);
  fc1T (8,1024,2048); f1b (8,128,16); fc2T (8,2048,1024); f2b (8,1,1024);
  sh1T (1024,4096); s1b (128,32); sh2T (4096,1024); s2b (1,1024); ident (128,128).
Output: out (2048,1024) f32.
"""

F32 = mybir.dt.float32
BF16 = mybir.dt.bfloat16
AF = mybir.ActivationFunctionType
ALU = mybir.AluOpType
AX = mybir.AxisListType.X

NFULL, NOWN, D = 4096, 2048, 1024
EPS = 1e-5
NORM = 64.0 ** -0.25
RATIO = 256.0 ** -0.5
EPS_K = 1e-4
CAP = 5120.0
NITER = 44


def build_l1(debug=(), stop_after=None):
    nc = bass.Bass("TRN2", target_bir_lowering=False)
    I = {}
    for nm, shp in [("x1", [NFULL, D]), ("x1own", [NOWN, D]),
                    ("wqT", [D, D]), ("wkT", [D, D]), ("wvT", [D, D]), ("woT", [D, D]),
                    ("bq", [128, 8]), ("bk", [128, 8]), ("bvc", [128, 8]), ("bo", [1, D]),
                    ("projP", [128, 256]),
                    ("w1T", [D, 4096]), ("b1", [128, 32]), ("w2T", [4096, D]), ("b2", [1, D]),
                    ("g2", [1, D]), ("be2", [1, D]), ("g3", [1, D]), ("be3", [1, D]),
                    ("zt", [16384, 8]), ("eb", [1, 8]),
                    ("fc1T", [8, D, 2048]), ("f1b", [8, 128, 16]),
                    ("fc2T", [8, 2048, D]), ("f2b", [8, 1, D]),
                    ("sh1T", [D, 4096]), ("s1b", [128, 32]),
                    ("sh2T", [4096, D]), ("s2b", [1, D]), ("ident", [128, 128])]:
        I[nm] = nc.dram_tensor(nm, shp, F32, kind="ExternalInput").ap()
    out_t = nc.dram_tensor("out", [NOWN, D], F32, kind="ExternalOutput").ap()
    dbg = {}
    for nm, shp in [("d_xn1", [128, 8 * NFULL]), ("d_attn", [128, 8 * NOWN]),
                    ("d_s1", [128, 16 * D]), ("d_y", [128, 16 * D]),
                    ("d_wm", [128, 1024]), ("d_tau", [8, 8]), ("d_kp", [128, 256]),
                    ("d_ctx", [128, 2 * 65]), ("d_g1", [128, 16 * D]),
                    ("d_kfm", [128, 8 * NFULL]), ("d_vtm", [128, 32 * D]),
                    ("d_qfm", [128, 8 * NOWN]), ("d_x1a", [128, 16 * D]),
                    ("d_ff1", [128, 16 * 4096]), ("d_xaf", [128, 16 * 1024]),
                    ("d_h1", [128, 2048]), ("d_oe", [128, 1024])]:
        if nm in debug:
            dbg[nm] = nc.dram_tensor(nm, shp, F32, kind="ExternalOutput").ap()

    k_sp = nc.dram_tensor("k_sp", [128, 8 * NFULL], BF16, kind="Internal").ap()
    v_sp = nc.dram_tensor("v_sp", [128, 32 * D], BF16, kind="Internal").ap()
    q_sp = nc.dram_tensor("q_sp", [128, 8 * NOWN], BF16, kind="Internal").ap()
    xn1tm_sp = nc.dram_tensor("xn1tm_sp", [128, 16 * D], F32, kind="Internal").ap()
    ff1_sp = nc.dram_tensor("ff1_sp", [128, 16 * 4096], BF16, kind="Internal").ap()
    sh1_sp = nc.dram_tensor("sh1_sp", [128, 16 * 4096], BF16, kind="Internal").ap()
    y_sp = nc.dram_tensor("y_sp", [128, 16 * D], F32, kind="Internal").ap()

    Al = lambda *a: nc.alloc_sbuf_tensor(*a)
    XB = Al("XB", [128, 49152], BF16)
    S32a = Al("S32a", [128, 16384], BF16)
    S32b = Al("S32b", [128, 16384], BF16)
    MISC = Al("MISC", [128, 5120], F32)
    idn = Al("idn", [128, 128], BF16)
    bqn = Al("bqn", [128, 8], F32)
    bkn = Al("bkn", [128, 8], F32)
    bo2 = Al("bo2", [128, D], BF16)
    g2b = Al("g2b", [128, D], BF16)
    e2b = Al("e2b", [128, D], BF16)
    g3b = Al("g3b", [128, D], BF16)
    e3b = Al("e3b", [128, D], BF16)
    b2b = Al("b2b", [128, D], BF16)
    s2bb = Al("s2bb", [128, D], BF16)
    ebb = Al("ebb", [128, 8], F32)
    onesb = Al("onesb", [1, 128], BF16)
    onesf = Al("onesf", [1, 128], F32)
    onef = Al("onef", [128, 1], F32)
    onef64 = Al("onef64", [128, 1], BF16)
    stat = Al("stat", [128, 8], F32)
    tau = Al("tau", [8, 8], F32)
    diagk = Al("diagk", [128, 32], F32)
    diagq = Al("diagq", [128, 16], F32)
    bias32 = Al("bias32", [128, 64], F32)
    mkcol = Al("mkcol", [128, 2], F32)
    stabb = Al("stabb", [128, 2], F32)
    fmb = Al("fmb", [128, 1024], BF16)
    kpq = Al("kpq", [128, 512], BF16)
    ctxs = Al("ctxs", [128, 2 * 2 * 65], BF16)
    ff1T = Al("ff1T", [128, 16 * 128], BF16)

    xn1_fm = XB[:, 0:32768]
    xn1_own = XB[:, 32768:49152]
    wsb = XB[:, 0:8192]
    wsb = S32a[:, 0:8192]
    kpair = S32a[:, 8192:12288]
    vpair = S32a[:, 12288:16384]
    attn1_fm = XB[:, 0:16384]
    qpT = S32b[:, 0:8192]
    s1_tm = S32b[:, 0:16384]
    xpsq = XB[:, 16384:20480]
    w1stg = XB[:, 0:32768]
    w2stg = XB[:, 0:32768]
    fc1stg = XB[:, 0:16384]
    fc2stg = XB[:, 16384:32768]
    sh1stg = XB[:, 0:32768]
    sh2stg = XB[:, 0:32768]
    xtile = MISC[:, 0:2048]
    tmpf = MISC[:, 2048:4096]
    tmpi = MISC[:, 4096:5120].bitcast(mybir.dt.int32)
    qpair = MISC[:, 4096:5120].bitcast(BF16)
    xaf = MISC[:, 4096:5120].bitcast(BF16)
    ztile = MISC[:, 4096:4224]
    midt = MISC[0:1, 4096:5120]
    yact = MISC[:, 0:2048]
    drow = MISC[0:1, 0:512]
    wmat = S32a[:, 0:2048].bitcast(F32)
    wkept = S32a[:, 2048:4096].bitcast(F32)
    h1T = S32a[:, 4096:8192]

    with (
        nc.psum_tensor("PA", [128, 1536], F32) as PA,
        nc.psum_tensor("PB", [128, 1536], F32) as PB,
        nc.psum_tensor("PT0", [128, 1024], BF16) as PT0,
        nc.psum_tensor("PT1", [128, 1024], BF16) as PT1,
        nc.Block() as block,
    ):
        kb = KB(nc)
        W = kb.wait

        def bcast_row(dst_bf16, src_row_f32, width=D):
            tg = kb.gdma(fmb[0:1, 0:width], src_row_f32)
            kb.barrier()
            tmm = None
            for s_ in range(0, width, 512):
                w_ = min(512, width - s_)
                tmm = kb.t(lambda e, s_=s_, w_=w_: e.matmul(
                    PA[:, s_ % 1024:s_ % 1024 + w_], onesb[:], fmb[0:1, s_:s_ + w_],
                    start=True, stop=True), inc=(s_ + 512 >= width))
            W("v", "sT", tmm)
            kb.v(lambda e, width=width: e.tensor_copy(dst_bf16[:, 0:width], PA[:, 0:width] if width <= 1024 else None), inc=True)
            kb.barrier()

        kb.v(lambda e: e.memset(onesb[:], 1.0))
        kb.v(lambda e: e.memset(onesf[:], 1.0))
        kb.v(lambda e: e.memset(onef[:], 1.0))
        kb.v(lambda e: e.memset(onef64[:], 1.0))
        kb.ydma(bqn[:], I["bq"])
        kb.ydma(bkn[:], I["bk"])
        kb.gdma(idn[:], I["ident"])
        kb.barrier()
        kb.v(lambda e: e.tensor_scalar_mul(bqn[:], bqn[:], NORM))
        kb.v(lambda e: e.tensor_scalar_mul(bkn[:], bkn[:], NORM), inc=True)
        for dst, src in [(g2b, "g2"), (e2b, "be2"), (g3b, "g3"), (e3b, "be3"),
                         (b2b, "b2"), (s2bb, "s2b")]:
            bcast_row(dst, I[src])
        tg = kb.gdma(stat[0:1, 0:8], I["eb"])
        kb.barrier()
        kb.t(lambda e: e.matmul(PA[:, 0:8], onesf[:], stat[0:1, 0:8], start=True, stop=True), inc=True)
        kb.barrier()
        kb.v(lambda e: e.tensor_copy(ebb[:], PA[:, 0:8]), inc=True)
        twv = kb.gdma(wsb[:].rearrange("p (c f) -> p c f", c=8),
                      I["woT"].rearrange("(c p) f -> p c f", p=128))
        tbv = kb.ydma(stat[:, 0:8] if False else bqn[:] if False else MISC[:, 0:8], I["bvc"])
        bvc = MISC[:, 0:8]
        kb.barrier()
        kb.v(lambda e: e.tensor_copy(fmb[:, 0:8], bvc[:, 0:8]), inc=True)
        kb.barrier()
        tmm = None
        for sb in range(2):
            for ec in range(8):
                tmm = kb.t(lambda e, ec=ec, sb=sb: e.matmul(
                    PB[0:1, sb * 512:sb * 512 + 512], fmb[:, ec:ec + 1],
                    wsb[:, ec * D + sb * 512:ec * D + sb * 512 + 512],
                    start=(ec == 0), stop=(ec == 7)), inc=(ec == 7 and sb == 1))
        kb.barrier()
        tbo = kb.gdma(fmb[0:1, 0:1024], I["bo"])
        kb.barrier()
        kb.v(lambda e: e.tensor_copy(kpq[0:1, 0:128] if False else fmb[1:2, 0:1024] if False else tmpf[0:1, 0:1024], PB[0:1, 0:1024]), inc=True)
        kb.barrier()
        kb.v(lambda e: e.tensor_tensor(tmpf[0:1, 0:1024], tmpf[0:1, 0:1024], fmb[0:1, 0:1024], ALU.add), inc=True)
        kb.barrier()
        kb.v(lambda e: e.tensor_copy(fmb[0:1, 0:1024], tmpf[0:1, 0:1024]), inc=True)
        kb.barrier()
        tmm = None
        for s_ in (0, 512):
            tmm = kb.t(lambda e, s_=s_: e.matmul(PA[:, s_:s_ + 512], onesb[:], fmb[0:1, s_:s_ + 512],
                                                 start=True, stop=True), inc=(s_ == 512))
        W("v", "sT", tmm)
        kb.v(lambda e: e.tensor_copy(bo2[:], PA[:, 0:1024]), inc=True)
        kb.barrier()

        if stop_after == "G":
            kb.run(block)
            return nc
        xf_t = I["x1"].rearrange("(a p) d -> a p d", p=128)
        xo_t = I["x1own"].rearrange("(a p) d -> a p d", p=128)
        pt_guard = [kb.cnt["sV"], kb.cnt["sV"]]
        xload_guard = [0, 0]
        s_end = [kb.cnt["sS"]]
        pti_c = [0]

        def transp8(src_tm_bf16, dst_cview, pti):
            pt = PT0 if pti % 2 == 0 else PT1
            W("t", "sV", pt_guard[pti % 2])
            ttr = None
            for cb in range(8):
                ttr = kb.t(lambda e, cb=cb, pt=pt, s=src_tm_bf16: e.transpose(
                    pt[:, cb * 128:cb * 128 + 128], s[:, cb * 128:cb * 128 + 128], idn[:]), inc=True)
            W("v", "sT", ttr)
            tev = kb.v(lambda e, pt=pt, d=dst_cview: e.tensor_copy(
                d, pt[:].rearrange("p (c n) -> p c n", c=8)), inc=True)
            pt_guard[pti % 2] = tev
            return tev

        def ln_tile(ti, src, fm_dst_col, own_t):
            buf = ti % 2
            xt = xtile[:, buf * D:(buf + 1) * D]
            W("y", "sV", xload_guard[buf])
            tld = kb.ydma(xt, src)
            W("v", "sq", tld)
            W("v", "sS", s_end[0])
            kb.vW(lambda e, xt=xt: e.reduce_sum(stat[:, 0:1], xt, AX))
            kb.vW(lambda e: e.tensor_scalar_mul(stat[:, 0:1], stat[:, 0:1], 1.0 / D))
            tsub = kb.v(lambda e, xt=xt: e.tensor_scalar(tmpf[:, 0:D], xt, stat[:, 0:1], None, ALU.subtract), inc=True)
            xload_guard[buf] = tsub
            W("s", "sV", tsub)
            tsq = kb.s(lambda e: e.activation(tmpf[:, D:2 * D], tmpf[:, 0:D], AF.Square), inc=True)
            W("v", "sS", tsq)
            kb.vW(lambda e: e.reduce_sum(stat[:, 1:2], tmpf[:, D:2 * D], AX))
            tvar = kb.v(lambda e: e.tensor_scalar(stat[:, 1:2], stat[:, 1:2], 1.0 / D, EPS, ALU.mult, ALU.add), inc=True)
            W("s", "sV", tvar)
            kb.sW(lambda e: e.activation(stat[:, 2:3], stat[:, 1:2], AF.Ln))
            trs = kb.s(lambda e: e.activation(stat[:, 3:4], stat[:, 2:3], AF.Exp, scale=-0.5), inc=True)
            s_end[0] = trs
            W("v", "sS", trs)
            kb.v(lambda e: e.tensor_scalar(tmpf[:, 0:D], tmpf[:, 0:D], stat[:, 3:4], None, ALU.mult))
            tbf = kb.v(lambda e: e.tensor_copy(fmb[:], tmpf[:, 0:D]), inc=True)
            if own_t is not None:
                W("y", "sV", tbf)
                kb.ydma(xn1tm_sp[:, own_t * D:own_t * D + D], tmpf[:, 0:D])
            W("t", "sV", tbf)
            if fm_dst_col is not None:
                dst = xn1_fm.rearrange("p (c n) -> p c n", c=8)[:, :, fm_dst_col:fm_dst_col + 128]
            else:
                dst = xn1_own.rearrange("p (c n) -> p c n", c=8)[:, :, own_t * 128:own_t * 128 + 128]
            transp8(fmb, dst, pti_c[0]); pti_c[0] += 1

        for ti in range(32):
            ln_tile(ti, xf_t[ti], ti * 128, None)
        for t in range(16):
            ln_tile(32 + t, xo_t[t], None, t)
        kb.barrier()
        if "d_xn1" in dbg:
            kb.gdma(dbg["d_xn1"], xn1_fm)
        kb.barrier()

        if stop_after == "H":
            kb.run(block)
            return nc
        ev_guard = {"A": None, "B": None}
        stg_guard = [0, 0]
        gidx = [0]
        wsb_guard = [0]
        ksg = MISC[:, 0:1536].bitcast(BF16)

        def mm_group(ps, emit_mms, evac_fn, spill_dst, eng="v"):
            pkey = "A" if ps is PA else "B"
            sbf = gidx[0] % 2
            if ev_guard[pkey]:
                W("t", ev_guard[pkey][0], ev_guard[pkey][1])
            tmm = emit_mms()
            W(eng, "sT", tmm)
            W(eng, "sq", stg_guard[sbf])
            emit = kb.v if eng == "v" else kb.s
            sem = "sV" if eng == "v" else "sS"
            tse = emit(lambda e, sbf=sbf: evac_fn(e, ksg[:, sbf * 1536:sbf * 1536 + 1536]), inc=True)
            ev_guard[pkey] = (sem, tse)
            W("y", sem, tse)
            tst = kb.ydma(spill_dst, ksg[:, sbf * 1536:sbf * 1536 + spill_dst.shape[-1]])
            stg_guard[sbf] = tst
            gidx[0] += 1

        def proj_fm(wname, src_fm, ncols, spill, bias_sb, scale):
            W("g", "sT", wsb_guard[0])
            tw = kb.gdma(wsb[:].rearrange("p (c f) -> p c f", c=8),
                         I[wname].rearrange("(c p) f -> p c f", p=128))
            W("t", "sg", tw)
            groups = [(0, 1536), (1536, 1536), (3072, 1024)] if ncols == 4096 else [(0, 1536), (1536, 512)]
            for fc in range(8):
                for (nb, nw) in groups:
                    ps = PA if gidx[0] % 2 == 0 else PB
                    def mms(fc=fc, nb=nb, nw=nw, ps=ps, src=src_fm, ncols=ncols):
                        tmm = None
                        for ec in range(8):
                            for sb in range(nw // 512):
                                tmm = kb.t(lambda e, fc=fc, ec=ec, nb=nb, sb=sb, ps=ps, src=src, ncols=ncols: e.matmul(
                                    ps[:, sb * 512:sb * 512 + 512],
                                    wsb[:, ec * D + fc * 128:ec * D + fc * 128 + 128],
                                    src[:, ec * ncols + nb + sb * 512:ec * ncols + nb + sb * 512 + 512],
                                    start=(ec == 0), stop=(ec == 7)), inc=(ec == 7 and sb == nw // 512 - 1))
                        return tmm
                    def ev(e, dst, fc=fc, nw=nw, ps=ps, bias_sb=bias_sb, scale=scale):
                        return e.tensor_scalar(dst[:, 0:nw], ps[:, 0:nw], scale,
                                               bias_sb[:, fc:fc + 1], ALU.mult, ALU.add)
                    mm_group(ps, mms, ev, spill.rearrange("p (c n) -> p c n", c=8)[:, fc, nb:nb + nw])
            wsb_guard[0] = kb.cnt["sT"]

        proj_fm("wkT", xn1_fm, NFULL, k_sp, bkn, NORM)
        kb.barrier()
        proj_fm("wqT", xn1_own, NOWN, q_sp, bqn, NORM)
        kb.barrier()
        W("g", "sT", wsb_guard[0])
        tw = kb.gdma(wsb[:].rearrange("p (c f) -> p c f", c=8),
                     I["wvT"].rearrange("(c p) f -> p c f", p=128))
        W("t", "sg", tw)
        for nt in [16, 17] + list(range(32)):
            ps = PA if gidx[0] % 2 == 0 else PB
            def mms(nt=nt, ps=ps):
                tmm = None
                for ec in range(8):
                    for sb in range(2):
                        tmm = kb.t(lambda e, nt=nt, ec=ec, sb=sb, ps=ps: e.matmul(
                            ps[:, sb * 512:sb * 512 + 512],
                            xn1_fm[:, ec * NFULL + nt * 128:ec * NFULL + nt * 128 + 128],
                            wsb[:, ec * D + sb * 512:ec * D + sb * 512 + 512],
                            start=(ec == 0), stop=(ec == 7)), inc=(ec == 7 and sb == 1))
                return tmm
            def ev(e, dst, ps=ps):
                return e.tensor_copy(dst[:, 0:1024], ps[:, 0:1024])
            mm_group(ps, mms, ev, v_sp[:, nt * D:nt * D + D])
        kb.barrier()

        for nm, src in [("d_kfm", k_sp), ("d_vtm", v_sp), ("d_qfm", q_sp)]:
            if nm in dbg:
                kb.gdma(dbg[nm], src)
        kb.barrier()

        projs = Al("projs", [128, 256], BF16)
        idf32 = Al("idf32", [128, 128], F32)
        vrhs = Al("vrhs", [128, 2 * 130], BF16)
        kb.gdma(projs[:], I["projP"])
        kb.ydma(idf32[:], I["ident"])
        kb.barrier()
        kb.v(lambda e: e.memset(vrhs[:, 64:65], 1.0))
        kb.v(lambda e: e.memset(vrhs[:, 129:130], 1.0))
        kb.v(lambda e: e.memset(vrhs[:, 194:195], 1.0))
        kb.v(lambda e: e.memset(vrhs[:, 259:260], 1.0), inc=True)
        kb.barrier()

        if stop_after == "I":
            kb.run(block)
            return nc
        pair_g = [0]
        for p in range(8):
            W("y", "sT", pair_g[0])
            W("y", "sV", kb.cnt["sV"])
            tk = kb.ydma(kpair[:], k_sp.rearrange("p (c n) -> p c n", c=8)[:, p, :])
            tv_ = kb.ydma(vpair[:].rearrange("p (k c) -> p k c", k=32),
                          v_sp.rearrange("p (k f) -> p k f", k=32)[:, :, p * 128:p * 128 + 128])
            tq_ = kb.ydma(qpair[:], q_sp.rearrange("p (c n) -> p c n", c=8)[:, p, :])
            W("s", "sq", tk)
            tks = kb.s(lambda e: e.activation(xpsq[:], kpair[:], AF.Square), inc=True)
            W("t", "sS", tks)
            for he in range(2):
                tdg = None
                for nt in range(32):
                    tdg = kb.t(lambda e, he=he, nt=nt: e.matmul(
                        (PA if he == 0 else PB)[:, 1024 + nt:1024 + nt + 1],
                        xpsq[64 * he:64 * he + 64, nt * 128:nt * 128 + 128],
                        onef64[64 * he:64 * he + 64, :],
                        start=True, stop=True), inc=(nt == 31))
                W("v", "sT", tdg)
                kb.v(lambda e, he=he: e.tensor_scalar_mul(
                    diagk[:], (PA if he == 0 else PB)[:, 1024:1056], 0.5), inc=True)
                kb.vW(lambda e: e.memset(mkcol[:, 0:1], -1e30))
                mm_prev = [0]
                for nt in range(32):
                    ps = PA if nt % 2 == 0 else PB
                    W("t", "sV", mm_prev[0])
                    tx = kb.t(lambda e, he=he, nt=nt, ps=ps: e.matmul(
                        ps[:, 0:256],
                        kpair[64 * he:64 * he + 64, nt * 128:nt * 128 + 128],
                        projs[64 * he:64 * he + 64, :],
                        start=True, stop=True), inc=True)
                    W("v", "sT", tx)
                    kb.vW(lambda e, ps=ps: e.reduce_max(stat[:, 6:7], ps[:, 0:256], AX))
                    mm_prev[0] = kb.vW(lambda e: e.tensor_max(mkcol[:, 0:1], mkcol[:, 0:1], stat[:, 6:7]))
                W("t", "sV", mm_prev[0])
                tt1 = kb.t(lambda e: e.matmul(PB[0:1, 512:640], mkcol[:, 0:1], idf32[:], start=True, stop=True), inc=True)
                W("v", "sT", tt1)
                kb.vW(lambda e: e.reduce_max(stat[0:1, 7:8], PB[0:1, 512:640], AX))
                tt2 = kb.v(lambda e: e.tensor_copy(mkcol[0:1, 1:2], stat[0:1, 7:8]), inc=True)
                W("t", "sV", tt2)
                tt3 = kb.t(lambda e: e.matmul(PB[:, 640:641], onesf[:], mkcol[0:1, 1:2], start=True, stop=True), inc=True)
                W("v", "sT", tt3)
                kb.vW(lambda e, he=he: e.tensor_copy(stabb[:, he:he + 1], PB[:, 640:641]))
                tb32 = kb.vW(lambda e, he=he: e.tensor_scalar(
                    bias32[:, he * 32:he * 32 + 32], diagk[:], stabb[:, he:he + 1], -1.0,
                    ALU.add, ALU.mult))
                kp_g = [tb32]
                for nt in range(32):
                    ps = PA if nt % 2 == 0 else PB
                    W("t", "sV", kp_g[0])
                    tx = kb.t(lambda e, he=he, nt=nt, ps=ps: e.matmul(
                        ps[:, 0:256],
                        kpair[64 * he:64 * he + 64, nt * 128:nt * 128 + 128],
                        projs[64 * he:64 * he + 64, :],
                        start=True, stop=True), inc=True)
                    W("s", "sT", tx)
                    buf = nt % 2
                    tex = kb.s(lambda e, he=he, nt=nt, ps=ps, buf=buf: e.activation(
                        kpq[:, buf * 256:buf * 256 + 256], ps[:, 0:256], AF.Exp,
                        bias=bias32[:, he * 32 + nt:he * 32 + nt + 1]), inc=True)
                    W("v", "sS", tex)
                    W("v", "sq", tv_)
                    tkp = kb.v(lambda e, buf=buf: e.tensor_scalar(
                        kpq[:, buf * 256:buf * 256 + 256], kpq[:, buf * 256:buf * 256 + 256],
                        EPS_K, RATIO, ALU.add, ALU.mult), inc=True)
                    tvr = kb.v(lambda e, he=he, nt=nt, buf=buf: e.tensor_copy(
                        vrhs[:, buf * 130 + 0:buf * 130 + 64],
                        vpair[:, nt * 128 + he * 64:nt * 128 + he * 64 + 64]), inc=True)
                    W("t", "sV", tvr)
                    tcx = None
                    for mh in range(2):
                        col = 512 if mh == 0 else 1024
                        pc = PA if he == 0 else PB
                        tcx = kb.t(lambda e, he=he, nt=nt, mh=mh, buf=buf, col=col, pc=pc: e.matmul(
                            pc[:, col:col + 65],
                            kpq[:, buf * 256 + mh * 128:buf * 256 + mh * 128 + 128],
                            vrhs[:, buf * 130:buf * 130 + 65],
                            start=(nt == 0), stop=(nt == 31)), inc=(nt == 31 and mh == 1))
                    kp_g[0] = tcx if nt >= 30 else kp_g[0]
                W("v", "sT", kp_g[0])
                for mh in range(2):
                    col = 512 if mh == 0 else 1024
                    pc = PA if he == 0 else PB
                    kb.v(lambda e, he=he, mh=mh, col=col, pc=pc: e.tensor_copy(
                        ctxs[:, (he * 2 + mh) * 65:(he * 2 + mh) * 65 + 65], pc[:, col:col + 65]), inc=(mh == 1))
            tctx = kb.cnt["sV"]
            if "d_ctx" in dbg and p == 0:
                W("g", "sV", tctx)
                kb.gdma(dbg["d_ctx"], ctxs[:, 0:130])
            W("s", "sq", tq_)
            W("s", "sV", tctx)
            tqs = kb.s(lambda e: e.activation(xpsq[:, 0:2048], qpair[:], AF.Square), inc=True)
            W("t", "sS", tqs)
            for he in range(2):
                tdg = None
                for nt in range(16):
                    tdg = kb.t(lambda e, he=he, nt=nt: e.matmul(
                        (PA if he == 0 else PB)[:, 1056 + nt:1056 + nt + 1],
                        xpsq[64 * he:64 * he + 64, nt * 128:nt * 128 + 128],
                        onef64[64 * he:64 * he + 64, :],
                        start=True, stop=True), inc=(nt == 15))
                W("v", "sT", tdg)
                kb.v(lambda e, he=he: e.tensor_scalar_mul(
                    diagq[:], (PA if he == 0 else PB)[:, 1056:1072], 0.5), inc=True)
                qp_g = [kb.cnt["sV"]]
                ptj = [kb.cnt["sV"], kb.cnt["sV"]]
                for nt in range(16):
                    ps = PA if nt % 2 == 0 else PB
                    W("t", "sV", qp_g[0])
                    tx = kb.t(lambda e, he=he, nt=nt, ps=ps: e.matmul(
                        ps[:, 0:256],
                        qpair[64 * he:64 * he + 64, nt * 128:nt * 128 + 128],
                        projs[64 * he:64 * he + 64, :],
                        start=True, stop=True), inc=True)
                    W("v", "sT", tx)
                    kb.vW(lambda e, ps=ps: e.reduce_max(stat[:, 6:7], ps[:, 0:256], AX))
                    trm = kb.vW(lambda e, he=he, nt=nt: e.tensor_scalar(
                        stat[:, 5:6], diagq[:, nt:nt + 1], stat[:, 6:7], -1.0, ALU.add, ALU.mult))
                    W("s", "sV", trm)
                    tex = kb.s(lambda e, ps=ps: e.activation(
                        kpq[:, 0:256], ps[:, 0:256], AF.Exp, bias=stat[:, 5:6]), inc=True)
                    W("v", "sS", tex)
                    tqp = kb.v(lambda e: e.tensor_scalar(
                        kpq[:, 0:256], kpq[:, 0:256], EPS_K, RATIO, ALU.add, ALU.mult), inc=True)
                    qp_g[0] = tqp
                    pt = PT0 if nt % 2 == 0 else PT1
                    W("t", "sV", ptj[nt % 2])
                    W("t", "sV", tqp)
                    ttr = None
                    for mh in range(2):
                        ttr = kb.t(lambda e, mh=mh, pt=pt: e.transpose(
                            pt[:, mh * 128:mh * 128 + 128], kpq[:, mh * 128:mh * 128 + 128], idn[:]), inc=True)
                    W("v", "sT", ttr)
                    tev = kb.v(lambda e, he=he, nt=nt, pt=pt: e.tensor_copy(
                        qpT[:].rearrange("p (h m n) -> p h m n", h=2, m=2)[:, he, :, nt * 128:nt * 128 + 128],
                        pt[:, 0:256].rearrange("p (m n) -> p m n", m=2)), inc=True)
                    ptj[nt % 2] = tev
                W("t", "sV", ptj[0])
                W("t", "sV", ptj[1])
                for qc in range(4):
                    tom = None
                    for mh in range(2):
                        tom = kb.t(lambda e, he=he, mh=mh, qc=qc: e.matmul(
                            PA[0:65, 0:512],
                            ctxs[:, (he * 2 + mh) * 65:(he * 2 + mh) * 65 + 65],
                            qpT[:].rearrange("p (h m n) -> p h m n", h=2, m=2)[:, he, mh, qc * 512:qc * 512 + 512],
                            start=(mh == 0), stop=(mh == 1)), inc=(mh == 1))
                    W("v", "sT", tom)
                    trc = kb.vW(lambda e: e.reciprocal(drow[0:1, 0:512], PA[64:65, 0:512]))
                    W("t", "sV", trc)
                    tbc = kb.t(lambda e: e.matmul(PB[0:64, 0:512], onesf[0:1, 0:64], drow[0:1, 0:512],
                                                  start=True, stop=True), inc=True)
                    W("v", "sT", tbc)
                    tbe = kb.vW(lambda e: e.tensor_copy(fmb[0:64, 0:512], PB[0:64, 0:512]))
                    tdv = kb.v(lambda e, p=p, he=he, qc=qc: e.tensor_tensor(
                        attn1_fm[64 * he:64 * he + 64, p * 2048 + qc * 512:p * 2048 + qc * 512 + 512],
                        PA[0:64, 0:512], fmb[0:64, 0:512], ALU.mult), inc=True)
                    W("t", "sV", tdv)
            pair_g[0] = kb.cnt["sT"]
        kb.barrier()
        if "d_attn" in dbg:
            kb.gdma(dbg["d_attn"], attn1_fm)
        kb.barrier()

        if stop_after == "J":
            kb.run(block)
            return nc
        tw = kb.gdma(wsb[:].rearrange("p (c f) -> p c f", c=8),
                     I["woT"].rearrange("(c p) f -> p c f", p=128))
        x1a_sp = nc.dram_tensor("x1a_sp", [128, 16 * D], F32, kind="Internal").ap()
        x1afm_sp = nc.dram_tensor("x1afm_sp", [128, 16 * 1024], BF16, kind="Internal").ap()
        W("t", "sg", tw)
        evK = {"A": None, "B": None}
        s_end2 = [kb.cnt["sS"]]
        xg2 = [0, 0]
        ptk = [kb.cnt["sV"], kb.cnt["sV"]]
        for t in range(16):
            ps, pk = (PA, "A") if t % 2 == 0 else (PB, "B")
            if evK[pk]:
                W("t", evK[pk][0], evK[pk][1])
            tmm = None
            for ec in range(8):
                for sb in range(2):
                    tmm = kb.t(lambda e, t=t, ec=ec, sb=sb, ps=ps: e.matmul(
                        ps[:, sb * 512:sb * 512 + 512],
                        attn1_fm[:, ec * 2048 + t * 128:ec * 2048 + t * 128 + 128],
                        wsb[:, ec * D + sb * 512:ec * D + sb * 512 + 512],
                        start=(ec == 0), stop=(ec == 7)), inc=(ec == 7 and sb == 1))
            W("y", "sV", xg2[t % 2])
            tldv = kb.ydma(xtile[:, (t % 2) * D:(t % 2) * D + D], xn1tm_sp[:, t * D:t * D + D])
            W("v", "sT", tmm)
            W("v", "sq", tldv)
            vt = xtile[:, (t % 2) * D:(t % 2) * D + D]
            kb.v(lambda e, ps=ps: e.tensor_add(tmpf[:, 0:D], ps[:, 0:1024], bo2[:]))
            tx1a = kb.v(lambda e, vt=vt: e.tensor_add(tmpf[:, 0:D], tmpf[:, 0:D], vt), inc=True)
            evK[pk] = ("sV", tx1a)
            xg2[t % 2] = tx1a
            W("v", "sS", s_end2[0])
            kb.vW(lambda e: e.reduce_sum(stat[:, 0:1], tmpf[:, 0:D], AX))
            kb.vW(lambda e: e.tensor_scalar_mul(stat[:, 0:1], stat[:, 0:1], 1.0 / D))
            tsub = kb.v(lambda e: e.tensor_scalar(tmpf[:, 0:D], tmpf[:, 0:D], stat[:, 0:1], None, ALU.subtract), inc=True)
            W("s", "sV", tsub)
            tsq = kb.s(lambda e: e.activation(tmpf[:, D:2 * D], tmpf[:, 0:D], AF.Square), inc=True)
            W("v", "sS", tsq)
            kb.vW(lambda e: e.reduce_sum(stat[:, 1:2], tmpf[:, D:2 * D], AX))
            tvar = kb.v(lambda e: e.tensor_scalar(stat[:, 1:2], stat[:, 1:2], 1.0 / D, EPS, ALU.mult, ALU.add), inc=True)
            W("s", "sV", tvar)
            kb.sW(lambda e: e.activation(stat[:, 2:3], stat[:, 1:2], AF.Ln))
            trs = kb.s(lambda e: e.activation(stat[:, 3:4], stat[:, 2:3], AF.Exp, scale=-0.5), inc=True)
            s_end2[0] = trs
            W("v", "sS", trs)
            kb.v(lambda e: e.tensor_scalar(tmpf[:, 0:D], tmpf[:, 0:D], stat[:, 3:4], None, ALU.mult))
            kb.v(lambda e: e.tensor_mul(tmpf[:, 0:D], tmpf[:, 0:D], g2b[:]))
            tx1b = kb.v(lambda e: e.tensor_add(tmpf[:, 0:D], tmpf[:, 0:D], e2b[:]), inc=True)
            W("y", "sV", tx1b)
            tsp = kb.ydma(x1a_sp[:, t * D:t * D + D], tmpf[:, 0:D])
            tbf = kb.v(lambda e: e.tensor_copy(fmb[:], tmpf[:, 0:D]), inc=True)
            pt = PT0 if t % 2 == 0 else PT1
            W("t", "sV", tbf)
            W("t", "sV", ptk[t % 2])
            ttr = None
            for cb in range(8):
                ttr = kb.t(lambda e, cb=cb, pt=pt: e.transpose(
                    pt[:, cb * 128:cb * 128 + 128], fmb[:, cb * 128:cb * 128 + 128], idn[:]), inc=True)
            W("v", "sT", ttr)
            tev = kb.v(lambda e, pt=pt: e.tensor_copy(fmb[:], pt[:]), inc=True)
            ptk[t % 2] = tev
            W("y", "sV", tev)
            kb.ydma(x1afm_sp[:, t * 1024:t * 1024 + 1024], fmb[:])
        kb.barrier()
        if "d_x1a" in dbg:
            kb.gdma(dbg["d_x1a"], x1a_sp)
        kb.barrier()
        if stop_after == "K1":
            kb.run(block)
            return nc
        b1sb = Al("b1sb", [128, 32], F32)
        s1bsb = Al("s1bsb", [128, 32], F32)
        kb.ydma(b1sb[:], I["b1"])
        kb.ydma(s1bsb[:], I["s1b"])
        tw1 = kb.gdma(w1stg[:].rearrange("p (c f) -> p c f", c=8),
                      I["w1T"].rearrange("(c p) f -> p c f", p=128))
        kb.barrier()

        HSLOTS = [PA[:, 0:128], PA[:, 512:640], PA[:, 1024:1152], PB[:, 0:128]]

        def mlp_h_pass(wstg_v, bias_sb, act, src_fm_loader, spill, nht, slots=HSLOTS):
            slot_g = [0, 0, 0, 0]
            sp_g = [0]
            for t in range(16):
                tldx = src_fm_loader(t)
                W("t", "sg", tldx)
                for ht in range(nht):
                    sl = ht % 4
                    W("t", "sS", slot_g[sl])
                    tmm = None
                    for ec in range(8):
                        tmm = kb.t(lambda e, ht=ht, ec=ec, sl=sl, wstg_v=wstg_v, nht=nht, slots=slots: e.matmul(
                            slots[sl],
                            wstg_v[:, ec * (nht * 128) + ht * 128:ec * (nht * 128) + ht * 128 + 128],
                            fmb[:, ec * 128:ec * 128 + 128],
                            start=(ec == 0), stop=(ec == 7)), inc=(ec == 7))
                    W("s", "sT", tmm)
                    if ht % 16 == 0:
                        W("s", "sg", sp_g[0])
                    tse = kb.s(lambda e, ht=ht, sl=sl, bias_sb=bias_sb, act=act, slots=slots: e.activation(
                        ff1T[:, (ht % 16) * 128:(ht % 16) * 128 + 128],
                        slots[sl], act,
                        bias=bias_sb[:, ht:ht + 1]), inc=True)
                    slot_g[sl] = tse
                    if ht % 16 == 15:
                        W("g", "sS", tse)
                        sp_g[0] = kb.gdma(spill[:, t * (nht * 128) + (ht // 16) * 2048:
                                                t * (nht * 128) + (ht // 16) * 2048 + 2048], ff1T[:])

        def mk_loader(sp):
            def load(t, sp=sp):
                W("g", "sT", kb.cnt["sT"])
                return kb.gdma(fmb[:], sp[:, t * 1024:t * 1024 + 1024])
            return load

        mlp_h_pass(w1stg, b1sb, AF.Relu, mk_loader(x1afm_sp), ff1_sp, 32)
        kb.barrier()
        if "d_ff1" in dbg:
            kb.gdma(dbg["d_ff1"], ff1_sp)
        if "d_xaf" in dbg:
            kb.gdma(dbg["d_xaf"], x1afm_sp)
        kb.barrier()
        if stop_after == "K2":
            kb.run(block)
            return nc
        tw2 = kb.gdma(w2stg[:].rearrange("p (c f) -> p c f", c=32),
                      I["w2T"].rearrange("(c p) f -> p c f", p=128))
        W("t", "sg", tw2)
        evK2 = {"A": None, "B": None}
        s_end3 = [kb.cnt["sS"]]
        ld_g = [0]
        xr3 = [0]
        for t in range(16):
            W("g", "sV", ld_g[0])
            tldf = kb.gdma(h1T[:], ff1_sp[:, t * 4096:t * 4096 + 4096])
            W("t", "sg", tldf)
            ps, pk = (PA, "A") if t % 2 == 0 else (PB, "B")
            if evK2[pk]:
                W("t", evK2[pk][0], evK2[pk][1])
            tmm = None
            for hc in range(32):
                for sb in range(2):
                    tmm = kb.t(lambda e, hc=hc, sb=sb, ps=ps: e.matmul(
                        ps[:, sb * 512:sb * 512 + 512],
                        h1T[:, hc * 128:hc * 128 + 128],
                        w2stg[:, hc * 1024 + sb * 512:hc * 1024 + sb * 512 + 512],
                        start=(hc == 0), stop=(hc == 31)), inc=(hc == 31 and sb == 1))
            ld_g[0] = kb.cnt["sV"] + 1
            W("y", "sV", xr3[0])
            tldx = kb.ydma(xtile[:, 0:D], x1a_sp[:, t * D:t * D + D])
            W("v", "sT", tmm)
            W("v", "sq", tldx)
            kb.v(lambda e, ps=ps: e.tensor_add(tmpf[:, 0:D], ps[:, 0:1024], b2b[:]))
            tadd = kb.v(lambda e: e.tensor_add(tmpf[:, 0:D], tmpf[:, 0:D], xtile[:, 0:D]), inc=True)
            evK2[pk] = ("sV", tadd)
            xr3[0] = tadd
            W("v", "sS", s_end3[0])
            kb.vW(lambda e: e.reduce_sum(stat[:, 0:1], tmpf[:, 0:D], AX))
            kb.vW(lambda e: e.tensor_scalar_mul(stat[:, 0:1], stat[:, 0:1], 1.0 / D))
            tsub = kb.v(lambda e: e.tensor_scalar(tmpf[:, 0:D], tmpf[:, 0:D], stat[:, 0:1], None, ALU.subtract), inc=True)
            W("s", "sV", tsub)
            tsq = kb.s(lambda e: e.activation(tmpf[:, D:2 * D], tmpf[:, 0:D], AF.Square), inc=True)
            W("v", "sS", tsq)
            kb.vW(lambda e: e.reduce_sum(stat[:, 1:2], tmpf[:, D:2 * D], AX))
            tvar = kb.v(lambda e: e.tensor_scalar(stat[:, 1:2], stat[:, 1:2], 1.0 / D, EPS, ALU.mult, ALU.add), inc=True)
            W("s", "sV", tvar)
            kb.sW(lambda e: e.activation(stat[:, 2:3], stat[:, 1:2], AF.Ln))
            trs = kb.s(lambda e: e.activation(stat[:, 3:4], stat[:, 2:3], AF.Exp, scale=-0.5), inc=True)
            s_end3[0] = trs
            W("v", "sS", trs)
            kb.v(lambda e: e.tensor_scalar(tmpf[:, 0:D], tmpf[:, 0:D], stat[:, 3:4], None, ALU.mult))
            kb.v(lambda e: e.tensor_mul(tmpf[:, 0:D], tmpf[:, 0:D], g3b[:]))
            kb.v(lambda e: e.tensor_add(tmpf[:, 0:D], tmpf[:, 0:D], e3b[:]))
            kb.v(lambda e, t=t: e.tensor_copy(s1_tm[:, t * D:t * D + D], tmpf[:, 0:D]), inc=True)
        kb.barrier()
        if "d_s1" in dbg:
            kb.gdma(dbg["d_s1"], s1_tm)
        kb.barrier()

        if stop_after == "K":
            kb.run(block)
            return nc
        zt_t = I["zt"].rearrange("(a p) d -> a p d", p=128)
        zl_g = [0]
        for zt_i in range(128):
            W("y", "sV", zl_g[0])
            tld = kb.ydma(ztile[:, 0:8], zt_t[zt_i])
            W("v", "sq", tld)
            W("s", "sq", tld)
            W("s", "sV", kb.cnt["sV"])
            tsg = kb.s(lambda e: e.activation(stat[:, 0:1] if False else tmpf[:, 0:8], ztile[:, 0:8], AF.Sigmoid), inc=True)
            W("v", "sS", tsg)
            kb.vW(lambda e: e.tensor_add(tmpf[:, 0:8], tmpf[:, 0:8], ebb[:]))
            kb.vW(lambda e: e.tensor_scalar_max(tmpf[:, 0:8], tmpf[:, 0:8], 1e-12))
            kb.vW(lambda e: e.reduce_max(stat[:, 0:1], tmpf[:, 0:8], AX))
            kb.vW(lambda e: e.tensor_scalar(tmpf[:, 8:16], tmpf[:, 0:8], stat[:, 0:1], None, ALU.is_ge))
            kb.vW(lambda e: e.scalar_tensor_tensor(tmpf[:, 16:24], tmpf[:, 8:16], -1e30, tmpf[:, 0:8], ALU.mult, ALU.add))
            kb.vW(lambda e: e.reduce_max(stat[:, 1:2], tmpf[:, 16:24], AX))
            kb.vW(lambda e: e.tensor_scalar(tmpf[:, 24:32], tmpf[:, 16:24], stat[:, 1:2], None, ALU.is_ge))
            kb.vW(lambda e: e.tensor_add(stat[:, 2:3], stat[:, 0:1], stat[:, 1:2]))
            kb.vW(lambda e: e.reciprocal(stat[:, 3:4], stat[:, 2:3]))
            kb.vW(lambda e: e.tensor_scalar(stat[:, 4:5], stat[:, 0:1], stat[:, 3:4], None, ALU.mult))
            kb.vW(lambda e: e.tensor_scalar(stat[:, 5:6], stat[:, 1:2], stat[:, 3:4], None, ALU.mult))
            kb.vW(lambda e: e.tensor_scalar(tmpf[:, 8:16], tmpf[:, 8:16], stat[:, 4:5], None, ALU.mult))
            kb.vW(lambda e: e.scalar_tensor_tensor(tmpf[:, 24:32], tmpf[:, 24:32], 1.0, tmpf[:, 8:16], ALU.bypass, ALU.mult) if False else
                  e.tensor_scalar(tmpf[:, 24:32], tmpf[:, 24:32], stat[:, 5:6], None, ALU.mult))
            zl_g[0] = kb.vW(lambda e, zt_i=zt_i: e.tensor_add(wmat[:, zt_i * 8:zt_i * 8 + 8], tmpf[:, 8:16], tmpf[:, 24:32]))
        kb.barrier()
        if "d_wm" in dbg:
            kb.gdma(dbg["d_wm"], wmat[:])
        kb.vW(lambda e: e.memset(tau[:, 0:1], 0.0))
        kb.vW(lambda e: e.memset(tau[:, 1:2], 1.0))
        for it in range(NITER):
            kb.vW(lambda e: e.tensor_add(tau[:, 2:3], tau[:, 0:1], tau[:, 1:2]))
            tmid = kb.vW(lambda e: e.tensor_scalar_mul(tau[:, 2:3], tau[:, 2:3], 0.5))
            W("t", "sV", tmid)
            tt = kb.t(lambda e: e.matmul(PB[0:1, 512:520], tau[0:8, 2:3], idf32[0:8, 0:8], start=True, stop=True), inc=True)
            W("v", "sT", tt)
            kb.vW(lambda e: e.tensor_copy(midt[0:1, 0:8], PB[0:1, 512:520]))
            for db in range(7):
                w_ = 8 << db
                kb.vW(lambda e, w_=w_: e.tensor_copy(midt[0:1, w_:2 * w_], midt[0:1, 0:w_]))
            tmr = kb.cnt["sV"]
            W("t", "sV", tmr)
            tbc = None
            for s_ in (0, 512):
                tbc = kb.t(lambda e, s_=s_: e.matmul(PB[:, s_:s_ + 512], onesf[:], midt[0:1, s_:s_ + 512], start=True, stop=True), inc=(s_ == 512))
            W("v", "sT", tbc)
            kb.vW(lambda e: e.tensor_copy(tmpf[:, 0:1024], PB[:, 0:1024]))
            kb.vW(lambda e: e.tensor_tensor(wkept[:], wmat[:], tmpf[:, 0:1024], ALU.is_ge))
            kb.vW(lambda e: e.tensor_reduce(
                stat[:, 0:8], wkept.rearrange("p (t x) -> p x t", x=8), AX, ALU.add))
            tcn = kb.cnt["sV"]
            W("t", "sV", tcn)
            tct = kb.t(lambda e: e.matmul(PB[0:8, 520:521], stat[:, 0:8], onef[:], start=True, stop=True), inc=True)
            W("v", "sT", tct)
            kb.vW(lambda e: e.tensor_copy(tau[:, 3:4], PB[0:8, 520:521]))
            kb.vW(lambda e: e.tensor_scalar(tau[:, 4:5], tau[:, 3:4], CAP, None, ALU.is_ge))
            kb.vW(lambda e: e.tensor_sub(tau[:, 5:6], tau[:, 2:3], tau[:, 0:1]))
            kb.vW(lambda e: e.scalar_tensor_tensor(tau[:, 5:6], tau[:, 5:6], 1.0, tau[:, 4:5], ALU.mult, ALU.mult))
            kb.vW(lambda e: e.tensor_add(tau[:, 0:1], tau[:, 0:1], tau[:, 5:6]))
            kb.vW(lambda e: e.tensor_sub(tau[:, 6:7], tau[:, 2:3], tau[:, 1:2]))
            kb.vW(lambda e: e.tensor_scalar(tau[:, 7:8], tau[:, 4:5], -1.0, 1.0, ALU.mult, ALU.add))
            kb.vW(lambda e: e.scalar_tensor_tensor(tau[:, 6:7], tau[:, 6:7], 1.0, tau[:, 7:8], ALU.mult, ALU.mult))
            kb.vW(lambda e: e.tensor_add(tau[:, 1:2], tau[:, 1:2], tau[:, 6:7]))
        ttf = kb.cnt["sV"]
        W("t", "sV", ttf)
        tt = kb.t(lambda e: e.matmul(PB[0:1, 512:520], tau[0:8, 0:1], idf32[0:8, 0:8], start=True, stop=True), inc=True)
        W("v", "sT", tt)
        kb.vW(lambda e: e.tensor_copy(midt[0:1, 0:8], PB[0:1, 512:520]))
        for db in range(7):
            w_ = 8 << db
            kb.vW(lambda e, w_=w_: e.tensor_copy(midt[0:1, w_:2 * w_], midt[0:1, 0:w_]))
        tmr = kb.cnt["sV"]
        W("t", "sV", tmr)
        tbc = None
        for s_ in (0, 512):
            tbc = kb.t(lambda e, s_=s_: e.matmul(PB[:, s_:s_ + 512], onesf[:], midt[0:1, s_:s_ + 512], start=True, stop=True), inc=(s_ == 512))
        W("v", "sT", tbc)
        kb.vW(lambda e: e.tensor_copy(tmpf[:, 0:1024], PB[:, 0:1024]))
        kb.vW(lambda e: e.tensor_tensor(wkept[:], wmat[:], tmpf[:, 0:1024], ALU.is_ge))
        kb.vW(lambda e: e.tensor_mul(wkept[:], wkept[:], wmat[:]))
        kb.barrier()
        if "d_tau" in dbg:
            kb.gdma(dbg["d_tau"], tau[:])
        kb.barrier()

        if stop_after == "Lcap":
            kb.run(block)
            return nc
        f1bs = Al("f1bs", [128, 8 * 16], F32)
        kb.ydma(f1bs[:].rearrange("p (a c) -> p a c", a=8),
                I["f1b"].rearrange("a p c -> p a c"))
        f2b8 = Al("f2b8", [8, 1024], BF16)
        kb.gdma(f2b8[:], I["f2b"].rearrange("a x d -> (a x) d"))
        kb.barrier()
        wk8T = kpq[0:8, 0:128]
        yib = [0, 0]
        for t in range(16):
            W("t", "sV", yib[0])
            tt = kb.t(lambda e, t=t: e.matmul(PA[0:8, 1024:1152], wkept[:, t * 8:t * 8 + 8],
                                              idf32[:], start=True, stop=True), inc=True)
            W("v", "sT", tt)
            tv = kb.v(lambda e: e.tensor_copy(wk8T, PA[0:8, 1024:1152]), inc=True)
            W("t", "sV", tv)
            tmm = None
            for sb in range(2):
                tmm = kb.t(lambda e, sb=sb: e.matmul(PB[:, sb * 512:sb * 512 + 512],
                                                     wk8T, f2b8[:, sb * 512:sb * 512 + 512],
                                                     start=True, stop=True), inc=(sb == 1))
            W("v", "sT", tmm)
            W("v", "sg", yib[1])
            tv2 = kb.v(lambda e: e.tensor_copy(yact[:, 0:D], PB[:, 0:1024]), inc=True)
            yib[0] = tv2
            W("g", "sV", tv2)
            yib[1] = kb.gdma(y_sp[:, t * D:t * D + D], yact[:, 0:D])
        kb.barrier()
        eg = [0]
        for e_ in range(8):
            W("g", "sT", eg[0])
            tw1e = kb.gdma(fc1stg[:].rearrange("p (c f) -> p c f", c=8),
                           I["fc1T"][e_].rearrange("(c p) f -> p c f", p=128))
            tw2e = kb.gdma(fc2stg[:].rearrange("p (c f) -> p c f", c=16),
                           I["fc2T"][e_].rearrange("(c p) f -> p c f", p=128))
            W("t", "sg", tw2e)
            slot_g = [0, 0, 0, 0]
            y_g = [0]
            ES256 = [PA[:, 0:256], PA[:, 512:768], PA[:, 1024:1280], PB[:, 1024:1280]]
            for u in range(8):
                for ht in range(16):
                    sl = ht % 4
                    W("t", "sS", slot_g[sl])
                    tmm = None
                    for ec in range(8):
                        tmm = kb.t(lambda e, ht=ht, ec=ec, sl=sl, u=u, ES256=ES256: e.matmul(
                            ES256[sl],
                            fc1stg[:, ec * 2048 + ht * 128:ec * 2048 + ht * 128 + 128],
                            xn1_own[:, ec * 2048 + u * 256:ec * 2048 + u * 256 + 256],
                            start=(ec == 0), stop=(ec == 7)), inc=(ec == 7))
                    W("s", "sT", tmm)
                    tse = kb.s(lambda e, ht=ht, sl=sl, e_=e_, ES256=ES256: e.activation(
                        h1T[:, ht * 256:ht * 256 + 256],
                        ES256[sl], AF.Gelu,
                        bias=f1bs[:, e_ * 16 + ht:e_ * 16 + ht + 1]), inc=True)
                    slot_g[sl] = tse
                W("t", "sS", slot_g[0]); W("t", "sS", slot_g[1])
                W("t", "sS", slot_g[2]); W("t", "sS", slot_g[3])
                for tt in range(2):
                    t = u * 2 + tt
                    tmm = None
                    for hc in range(16):
                        for sb in range(2):
                            tmm = kb.t(lambda e, hc=hc, sb=sb, tt=tt: e.matmul(
                                PB[:, sb * 512:sb * 512 + 512],
                                h1T[:, hc * 256 + tt * 128:hc * 256 + tt * 128 + 128],
                                fc2stg[:, hc * 1024 + sb * 512:hc * 1024 + sb * 512 + 512],
                                start=(hc == 0), stop=(hc == 15)), inc=(hc == 15 and sb == 1))
                    W("v", "sT", tmm)
                    W("v", "sg", y_g[0])
                    tya = kb.v(lambda e, e_=e_, t=t: e.tensor_scalar(
                        yact[:, 0:D], PB[:, 0:1024], wkept[:, t * 8 + e_:t * 8 + e_ + 1], None, ALU.mult), inc=True)
                    W("g", "sV", tya)
                    ty = kb._emit("g", lambda e, t=t: e.dma_start(
                        out=y_sp[:, t * D:t * D + D], in_=yact[:, 0:D],
                        accum_op=ALU.add), "sg", 16)
                    y_g[0] = ty
                    W("t", "sV", tya)
            eg[0] = kb.cnt["sT"]
        kb.barrier()
        if "d_y" in dbg:
            kb.gdma(dbg["d_y"], y_sp)
        kb.barrier()

        if stop_after == "L":
            kb.run(block)
            return nc
        tw1 = kb.gdma(sh1stg[:].rearrange("p (c f) -> p c f", c=8),
                      I["sh1T"].rearrange("(c p) f -> p c f", p=128))
        kb.barrier()

        def own_loader(t):
            def f(e, t=t):
                return e.tensor_copy(fmb[:], xn1_own[:, 0:1024])
            W("v", "sT", kb.cnt["sT"])
            return kb.v(lambda e, t=t: e.tensor_copy(
                fmb[:].rearrange("p (c n) -> p c n", c=8),
                xn1_own.rearrange("p (c n) -> p c n", c=8)[:, :, t * 128:t * 128 + 128]), inc=True)

        def mk_own_loader():
            def load(t):
                tk = own_loader(t)
                W("g", "sV", tk)
                return kb.cnt["sg"]
            return load

        def mlp_h_pass_own(wstg_v, bias_sb, act, spill, nht):
            slot_g = [0, 0, 0, 0]
            sp_g = [0]
            for t in range(16):
                W("v", "sT", kb.cnt["sT"])
                tk = kb.v(lambda e, t=t: e.tensor_copy(
                    fmb[:].rearrange("p (c n) -> p c n", c=8),
                    xn1_own.rearrange("p (c n) -> p c n", c=8)[:, :, t * 128:t * 128 + 128]), inc=True)
                W("t", "sV", tk)
                for ht in range(nht):
                    sl = ht % 4
                    W("t", "sS", slot_g[sl])
                    tmm = None
                    for ec in range(8):
                        tmm = kb.t(lambda e, ht=ht, ec=ec, sl=sl, wstg_v=wstg_v, nht=nht: e.matmul(
                            HSLOTS[sl],
                            wstg_v[:, ec * (nht * 128) + ht * 128:ec * (nht * 128) + ht * 128 + 128],
                            fmb[:, ec * 128:ec * 128 + 128],
                            start=(ec == 0), stop=(ec == 7)), inc=(ec == 7))
                    W("s", "sT", tmm)
                    if ht % 16 == 0:
                        W("s", "sg", sp_g[0])
                    tse = kb.s(lambda e, ht=ht, sl=sl, bias_sb=bias_sb, act=act: e.activation(
                        ff1T[:, (ht % 16) * 128:(ht % 16) * 128 + 128],
                        HSLOTS[sl], act,
                        bias=bias_sb[:, ht:ht + 1]), inc=True)
                    slot_g[sl] = tse
                    if ht % 16 == 15:
                        W("g", "sS", tse)
                        sp_g[0] = kb.gdma(spill[:, t * (nht * 128) + (ht // 16) * 2048:
                                                t * (nht * 128) + (ht // 16) * 2048 + 2048], ff1T[:])

        mlp_h_pass_own(sh1stg, s1bsb, AF.Gelu, sh1_sp, 32)
        kb.barrier()
        tw2 = kb.gdma(sh2stg[:].rearrange("p (c f) -> p c f", c=32),
                      I["sh2T"].rearrange("(c p) f -> p c f", p=128))
        W("t", "sg", tw2)
        evM = {"A": None, "B": None}
        ld_g = [0]
        og = [0]
        for t in range(16):
            W("g", "sV", ld_g[0])
            tldf = kb.gdma(h1T[:], sh1_sp[:, t * 4096:t * 4096 + 4096])
            W("t", "sg", tldf)
            ps, pk = (PA, "A") if t % 2 == 0 else (PB, "B")
            if evM[pk]:
                W("t", evM[pk][0], evM[pk][1])
            tmm = None
            for hc in range(32):
                for sb in range(2):
                    tmm = kb.t(lambda e, hc=hc, sb=sb, ps=ps: e.matmul(
                        ps[:, sb * 512:sb * 512 + 512],
                        h1T[:, hc * 128:hc * 128 + 128],
                        sh2stg[:, hc * 1024 + sb * 512:hc * 1024 + sb * 512 + 512],
                        start=(hc == 0), stop=(hc == 31)), inc=(hc == 31 and sb == 1))
            tldy = kb.ydma(xtile[:, 0:D], y_sp[:, t * D:t * D + D])
            tldx = kb.ydma(xtile[:, D:2 * D], xo_t[t])
            W("v", "sT", tmm)
            W("v", "sq", tldx)
            kb.v(lambda e, ps=ps: e.tensor_add(tmpf[:, 0:D], ps[:, 0:1024], s2bb[:]))
            tsh = kb.v(lambda e: e.tensor_add(tmpf[:, 0:D], tmpf[:, 0:D], xtile[:, 0:D]), inc=True)
            evM[pk] = ("sV", tsh)
            ld_g[0] = tsh
            W("s", "sV", tsh)
            tg1 = kb.s(lambda e: e.activation(tmpf[:, D:2 * D], tmpf[:, 0:D], AF.Silu), inc=True)
            W("v", "sS", tg1)
            kb.v(lambda e, t=t: e.tensor_mul(tmpf[:, D:2 * D], tmpf[:, D:2 * D], s1_tm[:, t * D:t * D + D]))
            tfin = kb.v(lambda e: e.tensor_add(tmpf[:, 0:D], tmpf[:, D:2 * D], xtile[:, D:2 * D]), inc=True)
            W("y", "sV", tfin)
            W("y", "sq", og[0])
            og[0] = kb.ydma(out_t.rearrange("(a p) d -> a p d", p=128)[t], tmpf[:, 0:D])
            if "d_g1" in dbg:
                pass
        kb.barrier()
        kb.run(block)
    return nc


def host_inputs_l1(x1_full, OFF, GOFF, wq, bq, wk, bk, wv, bv, wo, bo, proj,
                   w1, b1, w2, b2, g2, be2, g3, be3, z_all,
                   eb, fc1, f1b, fc2, f2b, sh1, s1b, sh2, s2b):
    f32 = np.float32
    zrot = np.concatenate([z_all[GOFF:GOFF + NOWN], z_all[:GOFF], z_all[GOFF + NOWN:]], 0)
    return {
        "x1": np.ascontiguousarray(x1_full, f32),
        "x1own": np.ascontiguousarray(x1_full[OFF:OFF + NOWN], f32),
        "wqT": np.ascontiguousarray(wq.T, f32),
        "wkT": np.ascontiguousarray(wk.T, f32),
        "wvT": np.ascontiguousarray(wv.T, f32),
        "woT": np.ascontiguousarray(wo.T, f32),
        "bq": np.ascontiguousarray(bq.reshape(8, 128).T, f32),
        "bk": np.ascontiguousarray(bk.reshape(8, 128).T, f32),
        "bvc": np.ascontiguousarray(bv.reshape(8, 128).T, f32),
        "bo": np.ascontiguousarray(bo.reshape(1, 1024), f32),
        "projP": np.ascontiguousarray(np.concatenate([proj.T, proj.T], 0), f32),
        "w1T": np.ascontiguousarray(w1.T, f32),
        "b1": np.ascontiguousarray(b1.reshape(32, 128).T, f32),
        "w2T": np.ascontiguousarray(w2.T, f32),
        "b2": np.ascontiguousarray(b2.reshape(1, 1024), f32),
        "g2": np.ascontiguousarray(g2.reshape(1, 1024), f32),
        "be2": np.ascontiguousarray(be2.reshape(1, 1024), f32),
        "g3": np.ascontiguousarray(g3.reshape(1, 1024), f32),
        "be3": np.ascontiguousarray(be3.reshape(1, 1024), f32),
        "zt": np.ascontiguousarray(zrot, f32),
        "eb": np.ascontiguousarray(eb.reshape(1, 8), f32),
        "fc1T": np.ascontiguousarray(fc1.transpose(0, 2, 1), f32),
        "f1b": np.ascontiguousarray(f1b.reshape(8, 16, 128).transpose(0, 2, 1), f32),
        "fc2T": np.ascontiguousarray(fc2.transpose(0, 2, 1), f32),
        "f2b": np.ascontiguousarray(f2b.reshape(8, 1, 1024), f32),
        "sh1T": np.ascontiguousarray(sh1.T, f32),
        "s1b": np.ascontiguousarray(s1b.reshape(32, 128).T, f32),
        "sh2T": np.ascontiguousarray(sh2.T, f32),
        "s2b": np.ascontiguousarray(s2b.reshape(1, 1024), f32),
        "ident": np.eye(128, dtype=f32),
    }


from concourse.bass_utils import run_bass_kernel_spmd

_NC_CACHE = {}


def _programs():
    if "l0" not in _NC_CACHE:
        _NC_CACHE["l0"] = build_l0()
        _NC_CACHE["l1"] = build_l1()
    return _NC_CACHE["l0"], _NC_CACHE["l1"]


def kernel(**inputs):
    f32 = np.float32
    g = {k: np.asarray(v, f32) for k, v in inputs.items()}
    X = g["L0_attn_x"]
    nc1, nc2 = _programs()

    in1 = []
    for c in range(8):
        b, off = c // 2, (c % 2) * 2048
        in1.append(host_inputs(
            X[b], off,
            g["L0_attn_Wq"], g["L0_attn_bq"], g["L0_attn_Wk"], g["L0_attn_bk"],
            g["L0_attn_Wv"], g["L0_attn_bv"], g["L0_attn_Wo"], g["L0_attn_bo"],
            g["L0_attn_rel_table"],
            g["L0_ffn_w_gate"], g["L0_ffn_w_up"], g["L0_ffn_w_down"],
            g["L1_ffn_proj_w"], g["L1_ffn_proj_b"]))
    r1 = run_bass_kernel_spmd(nc1, in1, core_ids=list(range(8)))

    x1 = np.empty((4, 4096, 1024), f32)
    z_all = np.empty((16384, 8), f32)
    for c in range(8):
        b, off = c // 2, (c % 2) * 2048
        o = r1.results[c]
        x1[b, off:off + 2048] = o["x1"]
        z_all[b * 4096 + off:b * 4096 + off + 2048] = np.asarray(o["zT"], f32).T

    in2 = []
    for c in range(8):
        b, off = c // 2, (c % 2) * 2048
        goff = b * 4096 + off
        in2.append(host_inputs_l1(
            x1[b], off, goff,
            g["L1_attn_Wq"], g["L1_attn_bq"], g["L1_attn_Wk"], g["L1_attn_bk"],
            g["L1_attn_Wv"], g["L1_attn_bv"], g["L1_attn_Wo"], g["L1_attn_bo"],
            g["L1_attn_proj"], g["L1_attn_W1"], g["L1_attn_b1"],
            g["L1_attn_W2"], g["L1_attn_b2"], g["L1_attn_g2"], g["L1_attn_be2"],
            g["L1_attn_g3"], g["L1_attn_be3"], z_all,
            g["L1_ffn_expert_bias"], g["L1_ffn_fc1_w"], g["L1_ffn_fc1_b"],
            g["L1_ffn_fc2_w"], g["L1_ffn_fc2_b"], g["L1_ffn_sh1_w"], g["L1_ffn_sh1_b"],
            g["L1_ffn_sh2_w"], g["L1_ffn_sh2_b"]))
    r2 = run_bass_kernel_spmd(nc2, in2, core_ids=list(range(8)))

    out = np.empty((4, 4096, 1024), f32)
    for c in range(8):
        b, off = c // 2, (c % 2) * 2048
        out[b, off:off + 2048] = r2.results[c]["out"]
    return out
